# version 08
# baseline (speedup 1.0000x reference)
import contextlib
import numpy as np
import concourse.bass as bass
import concourse.mybir as mybir
from concourse.alu_op_type import AluOpType
from concourse.bass_utils import run_bass_kernel_spmd

F32 = mybir.dt.float32
BF16 = mybir.dt.bfloat16
I16 = mybir.dt.int16
AF = mybir.ActivationFunctionType
AX = mybir.AxisListType
TS = AluOpType

D = 2048
T = 4096
OWN = 2048
HD = 8192
NTILE = 4
NSUB = 32
NEG = -1.0e9
SC1 = float(1.0 / np.sqrt(128.0))
M32 = 1.5 * 2.0 ** 23
M64 = 1.5 * 2.0 ** 52
E32 = float(np.finfo(np.float32).eps)
E64 = float(np.finfo(np.float64).eps)

ENG = {"t": "tensor", "v": "vector", "s": "scalar", "d": "sync", "g": "gpsimd",
       "e": "scalar"}

PHASES = 99
DEBUG_TAPS = False
P2TILES = 99
DRAINREP = False
PUMPL1 = 7
PUMPSET = 255
L0STOP = 'Z'


class Prog:

    def __init__(self, nc, stack):
        self.nc = nc
        self.sem = {e: stack.enter_context(nc.semaphore("sem_" + e)) for e in ENG}
        self.cnt = {e: 0 for e in ENG}
        self.ops = {e: [] for e in ENG}
        self.seq = 0

    def add(self, eng, deps, fn, inc=1, self_inc=False, drain_first=False):
        flat = []

        def _fl(d):
            if d is None:
                return
            if isinstance(d, list):
                for x in d:
                    _fl(x)
            else:
                flat.append(d)
        _fl(deps)
        deps = flat
        self.ops[eng].append((self.seq, eng, deps, fn, inc, self_inc, drain_first))
        self.seq += 1
        self.cnt[eng] += inc
        return (eng, self.cnt[eng])

    def dma(self, eng, deps, fn, n=1):
        start_cnt = self.cnt[eng]

        def body(e):
            if start_cnt > 0:
                e.wait_ge(self.sem[eng], start_cnt)
            insts = fn(e)
            assert len(insts) == n, f"dma group: expected {n}, got {len(insts)}"
            for i in insts:
                i.then_inc(self.sem[eng], 16)
            return "self"
        return self.add(eng, deps, body, inc=16 * n, self_inc=True)

    def barrier(self):
        return [(e, self.cnt[e]) for e in ENG if self.cnt[e] > 0]

    def play(self, block):
        sem = self.sem
        phys_keys = {}
        for e in ENG:
            phys_keys.setdefault(ENG[e], []).append(e)
        for phys, keys in phys_keys.items():
            ops = sorted((op for k in keys for op in self.ops[k]), key=lambda t: t[0])
            if not ops:
                continue

            def make_body(phys, keys, ops):
                def body(eng):
                    water = {}
                    for _seq, e, deps, fn, inc, self_inc, drain_first in ops:
                        drain = drain_first
                        for de, dv in deps:
                            if de == e:
                                drain = True
                            elif water.get(de, 0) < dv:
                                eng.wait_ge(sem[de], dv)
                                water[de] = dv
                        if drain:
                            eng.drain()
                        last = fn(eng)
                        if not self_inc and inc > 0:
                            if last is None:
                                eng.drain()
                                last = eng.nop()
                            last.then_inc(sem[e], inc)
                return body
            getattr(block, phys)(make_body(phys, keys, ops))


def build_program():
    nc = bass.Bass("TRN2", target_bir_lowering=False)
    dr = lambda n, s, dt=F32, kind="ExternalInput": nc.dram_tensor(n, s, dt, kind=kind).ap()

    x_seq = dr("x_seq", [T, D])
    x_own = dr("x_own", [OWN, D])
    Wq = dr("Wq", [D, D]); Wk = dr("Wk", [D, D]); Wv = dr("Wv", [D, D]); Wo = dr("Wo", [D, D])
    bo_row = dr("bo_row", [1, D])
    w1T = dr("w1T", [D, HD]); w2T = dr("w2T", [HD, D])
    b1_row = dr("b1_row", [1, HD]); b2_row = dr("b2_row", [1, D])
    g_rep = dr("g_rep", [128, HD]); b_rep = dr("b_rep", [128, HD])
    lqT = dr("lqT", [128, 16 * 64])
    Wk1 = dr("Wk1", [D, D]); Wv1 = dr("Wv1", [D, D])
    Wg1 = dr("Wg1", [D, 1024]); Wp1 = dr("Wp1", [D, D])
    WfcT = dr("WfcT", [D, HD]); WprojT = dr("WprojT", [HD, D])
    c_ident = dr("c_ident", [128, 128])
    c_gsum = dr("c_gsum", [128, 128])
    c_grep = dr("c_grep", [128, 128])
    c_ones1 = dr("c_ones1", [1, 128])
    c_onesc = dr("c_onesc", [128, 1])
    c_cos = dr("c_cos", [64, 16 * 64]); c_sin = dr("c_sin", [64, 16 * 64])
    c_mask = dr("c_mask", [64, 64])
    out = dr("out", [OWN, D], kind="ExternalOutput")

    x1_scr = dr("x1_scr", [OWN, D], kind="Internal")
    x1d_scr = dr("x1d_scr", [64, D], kind="Internal")
    sg_scr = dr("sg_scr", [OWN, D], kind="Internal")
    taps = [nc.dram_tensor(f"tap{i}", [128, HD], F32, kind="Internal").ap() for i in range(4)] if DEBUG_TAPS else None
    sgd_scr = dr("sgd_scr", [64, D], kind="Internal")
    qw1_scr = dr("qw1_scr", [D, HD], BF16, kind="Internal")
    qw2_scr = dr("qw2_scr", [HD, D], BF16, kind="Internal")
    qfc_scr = dr("qfc_scr", [D, HD], I16, kind="Internal")
    qproj_scr = dr("qproj_scr", [HD, D], I16, kind="Internal")
    sc_scr = dr("sc_scr", [4, 2048], BF16, kind="Internal")
    wcol_scr = dr("wcol_scr", [2, 128], kind="Internal")
    wtmp_scr = dr("wtmp_scr", [2, 2], kind="Internal")

    st = contextlib.ExitStack()
    with st:
        SB = lambda n, s, dt=F32: st.enter_context(nc.sbuf_tensor(n, s, dt))
        ident = SB("ident", [128, 128], BF16)
        identf = SB("identf", [128, 128])
        gsum = SB("gsum", [128, 128], BF16)
        grepl = SB("grepl", [128, 128], BF16)
        ones1f = SB("ones1f", [1, 128])
        ones1b = SB("ones1b", [1, 128], BF16)
        onescf = SB("onescf", [128, 1])
        bo16 = SB("bo16", [1, D], BF16)
        mask_s = SB("mask_s", [64, 64])
        ind = SB("ind", [128, 1]); magic = SB("magic", [128, 1]); epsd = SB("epsd", [128, 1])
        kv_ext = SB("kv_ext", [128, 16 * 65], BF16)
        attn_s = SB("attn_s", [128, 8 * 128], BF16)
        wmrow = SB("wmrow", [128, 4])
        pb = [st.enter_context(nc.psum_tensor(f"pb{i}", [128, 512], F32)) for i in range(8)]

        P = Prog(nc, st)

        hD0 = P.dma("d", [], lambda e: [
            e.dma_start(out=mask_s[:], in_=c_mask),
            e.dma_start(out=ones1f[:], in_=c_ones1),
            e.dma_start(out=onescf[:], in_=c_onesc),
            e.dma_start(out=identf[:], in_=c_ident)], n=4)
        hG0 = P.dma("g", [], lambda e: [
            e.dma_start(out=ident[:], in_=c_ident),
            e.dma_start(out=gsum[:], in_=c_gsum),
            e.dma_start(out=grepl[:], in_=c_grep),
            e.dma_start(out=ones1b[:], in_=c_ones1),
            e.dma_start(out=bo16[:], in_=bo_row)], n=5)

        def p0v(e):
            e.memset(ind[:], 1.0)
            e.drain()
            e.tensor_scalar(out=ind[:], in0=ind[:], scalar1=float(2.0 ** -25), scalar2=-1.0,
                            op0=TS.add, op1=TS.add)
            e.drain()
            e.tensor_scalar(out=ind[:], in0=ind[:], scalar1=float(2.0 ** 25), scalar2=None,
                            op0=TS.mult, op1=TS.bypass)
            e.drain()
            e.tensor_scalar(out=magic[:], in0=ind[:], scalar1=float(M64 - M32),
                            scalar2=float(M32), op0=TS.mult, op1=TS.add)
            e.tensor_scalar(out=epsd[:], in0=ind[:], scalar1=float(E64 - E32),
                            scalar2=float(E32), op0=TS.mult, op1=TS.add)
            return None
        hV0 = P.add("v", [], p0v)

        def emit_ln(src, rows, bn6, bn2, statc, deps, war_deps=()):
            def stats(e):
                for c in range(4):
                    e.bn_stats(out=bn6[0:rows, 6 * c:6 * c + 6], in_=src[0:rows, 512 * c:512 * (c + 1)])
                e.drain()
                e.bn_aggr(out=bn2[0:rows, :], in_=bn6[0:rows, 0:24])
                e.drain()
                e.tensor_scalar(out=statc[0:rows, 0:1], in0=bn2[0:rows, 1:2], scalar1=1e-5,
                                scalar2=None, op0=TS.add, op1=TS.bypass)
                return None
            hSt = P.add("v", list(deps) + list(war_deps), stats, drain_first=True)
            hSq = P.add("s", [hSt], lambda e: e.sqrt(out=statc[0:rows, 2:3], in_=statc[0:rows, 0:1]))

            def recip(e):
                e.reciprocal(out=statc[0:rows, 3:4], in_=statc[0:rows, 2:3])
                e.drain()
                e.tensor_tensor(out=statc[0:rows, 1:2], in0=bn2[0:rows, 0:1], in1=statc[0:rows, 3:4], op=TS.mult)
                e.drain()
                e.tensor_scalar(out=statc[0:rows, 1:2], in0=statc[0:rows, 1:2], scalar1=-1.0, scalar2=None,
                                op0=TS.mult, op1=TS.bypass)
                return None
            return P.add("v", [hSq], recip)

        hP1 = P.barrier()
        hWm1 = hWm2 = None
        if PHASES >= 1:
            qs_ = contextlib.ExitStack()
            SBq = lambda n, s, dt=F32: qs_.enter_context(nc.sbuf_tensor(n, s, dt))
            wbuf = [SBq(f"q_wbuf{i}", [128, 2048]) for i in range(2)]
            wqb2 = [SBq(f"q_wqb{i}", [128, 2048], BF16) for i in range(2)]
            wcol = SBq("q_wcol", [128, 128])
            wtmp = SBq("q_wtmp", [1, 4])
            wrow = SBq("q_wrow", [1, 128])
            negm = SBq("q_negm", [128, 1])
            l1b = [SBq(f"q_l1b{i}", [128, 2048], BF16) for i in range(2)]
            wab = [SBq(f"q_wab{i}", [128, 2048], BF16) for i in range(2)]
            sc16q = [SBq(f"q_sc{i}", [128, 2048], BF16) for i in range(2)]
            screp = [SBq(f"q_srp{i}", [128, 2048], BF16) for i in range(2)]
            cmpa = SBq("q_cmpa", [128, 2048])
            cmpb = SBq("q_cmpb", [128, 2048])
            wsgn = SBq("q_wsgn", [128, 2048], BF16)
            qoutb = SBq("q_qout", [128, 2048], BF16)

            def qzero(e):
                e.memset(wcol[:], 0.0)
                e.drain()
                e.memset(sc16q[0][:], 0.0)
                e.drain()
                e.memset(sc16q[1][:], 0.0)
                return None
            hQz = P.add("v", [hV0], qzero)
            hNm = P.add("v", [hQz], lambda e: e.tensor_scalar(
                out=negm[:], in0=magic[:], scalar1=-1.0, scalar2=None,
                op0=TS.mult, op1=TS.bypass))

            QS = {'wl': [None, None], 'wq': [None, None], 'acc': [hQz, hQz],
                  'b7g': None, 'b7r': None, 'l1l': [None, None], 'wabl': [None, None],
                  'scl': [hQz, hQz], 'srpl': [None, None],
                  'cmpa': None, 'cmpb': None, 'wsg': None, 'qo': None,
                  'si': 0, 'l1i': 0}

            def l0p1_item(wi, wt, rb, cs, q):
                slot = QS['si'] % 2; QS['si'] += 1
                b = wbuf[slot]
                hLd = P.dma(q, [QS['wl'][slot]],
                            (lambda e, b=b, wt=wt, rb=rb, cs=cs:
                             [e.dma_start(out=b[:], in_=wt[rb * 128:(rb + 1) * 128, cs:cs + 2048])]))

                def absred(e, b=b, wi=wi):
                    e.tensor_reduce(out=wcol[:, 2 * wi + 1:2 * wi + 2], in_=b[:],
                                    axis=AX.X, op=TS.add, apply_absolute_value=True)
                    e.drain()
                    e.tensor_tensor(out=wcol[:, 2 * wi:2 * wi + 1], in0=wcol[:, 2 * wi:2 * wi + 1],
                                    in1=wcol[:, 2 * wi + 1:2 * wi + 2], op=TS.add)
                    return None
                hAb = P.add("v", [hLd, QS['acc'][wi]], absred, drain_first=True)
                QS['acc'][wi] = hAb
                QS['wl'][slot] = hAb

            def wmean_item(wi):
                hS1 = P.dma("e", [QS['acc'][wi]],
                            (lambda e, wi=wi:
                             [e.dma_start(out=wcol_scr[wi:wi + 1, 0:128].rearrange("one p -> p one"),
                                          in_=wcol[:, 2 * wi:2 * wi + 1])]))
                hL1 = P.dma("e", [hS1, QS.get('wmrd')],
                            (lambda e, wi=wi:
                             [e.dma_start(out=wrow[:], in_=wcol_scr[wi:wi + 1, 0:128])]))

                def wmcalc(e, wi=wi):
                    e.tensor_reduce(out=wtmp[0:1, 2 * wi:2 * wi + 1], in_=wrow[0:1, :],
                                    axis=AX.X, op=TS.add)
                    e.drain()
                    e.tensor_scalar(out=wtmp[0:1, 2 * wi:2 * wi + 1], in0=wtmp[0:1, 2 * wi:2 * wi + 1],
                                    scalar1=float(1.0 / (D * HD)), scalar2=1e-5,
                                    op0=TS.mult, op1=TS.max)
                    e.drain()
                    e.reciprocal(out=wtmp[0:1, 2 * wi + 1:2 * wi + 2], in_=wtmp[0:1, 2 * wi:2 * wi + 1])
                    return None
                hWmv = P.add("v", [hL1], wmcalc, drain_first=True)
                QS['wmrd'] = hWmv
                hS2 = P.dma("e", [hWmv], (lambda e, wi=wi:
                            [e.dma_start(out=wtmp_scr[wi:wi + 1, :], in_=wtmp[0:1, 2 * wi:2 * wi + 2])]))
                hWmr = P.dma("e", [hS2], (lambda e, wi=wi:
                             [e.dma_start(out=wmrow[:, 2 * wi:2 * wi + 2],
                                          in_=wtmp_scr[wi:wi + 1, :].broadcast_to([128, 2]))]))
                return hWmr

            def l0p2_item(wi, wt, scr, rb, cs, q):
                slot = QS['si'] % 2; QS['si'] += 1
                b = wbuf[slot]
                hLd = P.dma(q, [QS['wl'][slot]],
                            (lambda e, b=b, wt=wt, rb=rb, cs=cs:
                             [e.dma_start(out=b[:], in_=wt[rb * 128:(rb + 1) * 128, cs:cs + 2048])]))

                def mround(e, b=b, wi=wi):
                    e.activation(out=b[:], in_=b[:], func=AF.Identity,
                                 bias=magic[:, 0:1], scale=wmrow[:, 2 * wi + 1:2 * wi + 2])
                    e.drain()
                    e.activation(out=b[:], in_=b[:], func=AF.Identity, bias=negm[:, 0:1])
                    return None
                hMr = P.add("s", [hLd, QS['wm1'] if wi == 0 else QS['wm2'], hNm], mround)

                def clip1(e, b=b, slot=slot):
                    e.tensor_scalar(out=wqb2[slot][:], in0=b[:], scalar1=-1.0, scalar2=1.0,
                                    op0=TS.max, op1=TS.min)
                    return None
                hCl = P.add("v", [hMr, QS['wq'][slot]], clip1)
                QS['wl'][slot] = hCl
                hSt = P.dma(q, [hCl], (lambda e, slot=slot, scr=scr, rb=rb, cs=cs:
                                       [e.dma_start(out=scr[rb * 128:(rb + 1) * 128, cs:cs + 2048],
                                                    in_=wqb2[slot][:])]))
                QS['wq'][slot] = hSt

            def l1_item(kind, u):
                slot = QS['l1i'] % 2; QS['l1i'] += 1
                if kind == 0:
                    wt, scr, rb, cs = WfcT, qfc_scr, u // 4, (u % 4) * 2048
                else:
                    wt, scr, rb, cs = WprojT, qproj_scr, u, 0
                b = l1b[slot]
                hLd = P.dma("g", [QS['l1l'][slot]],
                            (lambda e, b=b, wt=wt, rb=rb, cs=cs:
                             [e.dma_start(out=b[:], in_=wt[rb * 128:(rb + 1) * 128, cs:cs + 2048])]))
                if PUMPL1 < 2:
                    QS['l1l'][slot] = hLd
                    return
                hAbs = P.add("s", [hLd, QS['wabl'][slot], hG0],
                             (lambda e, b=b, slot=slot: e.activation(out=wab[slot][:], in_=b[:],
                                                                     func=AF.Abs)))
                if PUMPL1 < 3:
                    QS['l1l'][slot] = hAbs
                    QS['wabl'][slot] = hAbs
                    return
                hQm = None
                hQms = []
                for sl in range(8):
                    hMm = P.add("t", [hAbs if sl == 0 else None, QS['b7g'] if sl == 0 else hQm],
                                (lambda e, slot=slot, sl=sl:
                                 e.matmul(pb[7][0:128, 0:256], gsum[:],
                                          wab[slot][:, sl * 256:(sl + 1) * 256],
                                          start=True, stop=True, skip_group_check=True)))
                    hQm = P.add("v", [hMm, QS['scl'][slot] if sl == 0 else None],
                                (lambda e, slot=slot, sl=sl:
                                 e.tensor_scalar(out=sc16q[slot][0:2, sl * 256:(sl + 1) * 256],
                                                 in0=pb[7][0:2, 0:256], scalar1=float(1.0 / 64.0),
                                                 scalar2=1e-8, op0=TS.mult, op1=TS.max)))
                    hQms.append(hQm)
                if PUMPL1 < 4:
                    QS['b7g'] = hQm
                    QS['scl'][slot] = hQm
                    QS['l1l'][slot] = hAbs
                    QS['wabl'][slot] = hQm
                    return
                nrep = 1 if PUMPL1 == 41 else 8
                rcol = 0 if PUMPL1 == 43 else 256
                hEv = None
                for sl in range(nrep):
                    hRp = P.add("t", [hQms[sl], hQms[-1] if PUMPL1 in (41, 43) else None, hEv,
                                      [QS['b7r'], QS['srpl'][slot]] if sl == 0 else None],
                                (lambda e, slot=slot, sl=sl, rcol=rcol:
                                 e.matmul(pb[7][0:128, rcol:rcol + 256], grepl[:],
                                          sc16q[slot][0:128, sl * 256:(sl + 1) * 256],
                                          start=True, stop=True, skip_group_check=True)),
                                drain_first=(DRAINREP and sl == 0))

                    if PUMPL1 == 35:
                        hEv = P.add("v", [hRp], (lambda e, slot=slot, sl=sl:
                                     e.tensor_copy(out=screp[slot][0:2, sl * 256:sl * 256 + 2],
                                                   in_=pb[7][0:2, 256:258])))
                    else:
                        hEv = P.add("v", [hRp], (lambda e, slot=slot, sl=sl:
                                     e.tensor_copy(out=screp[slot][:, sl * 256:(sl + 1) * 256],
                                                   in_=pb[7][0:128, 256:512])))
                QS['b7g'] = hQm
                QS['b7r'] = hEv
                QS['scl'][slot] = hEv
                if PUMPL1 < 5 or PUMPL1 == 35:
                    QS['l1l'][slot] = hAbs
                    QS['wabl'][slot] = hQm
                    QS['srpl'][slot] = hEv
                    return

                def sacts(e, slot=slot):
                    e.activation(out=cmpa[:], in_=screp[slot][:], func=AF.Identity, scale=0.5)
                    e.drain()
                    e.activation(out=cmpb[:], in_=screp[slot][:], func=AF.Identity,
                                 scale=float(2.0 ** -9 + 2.0 ** -25))
                    return None
                hSa = P.add("s", [hEv, QS['cmpa'], QS['cmpb']], sacts)

                def dsub(e, slot=slot):
                    e.tensor_tensor(out=cmpa[:], in0=wab[slot][:], in1=cmpa[:], op=TS.subtract)
                    e.drain()
                    e.tensor_tensor(out=cmpa[:], in0=cmpa[:], in1=cmpb[:], op=TS.subtract)
                    return None
                hDs = P.add("v", [hSa, hAbs], dsub, drain_first=True)
                QS['cmpb'] = hDs
                if PUMPL1 < 6:
                    QS['cmpa'] = hDs
                    QS['l1l'][slot] = hAbs
                    QS['wabl'][slot] = hDs
                    QS['srpl'][slot] = hSa
                    return

                def signs(e, b=b):
                    e.activation(out=cmpa[:], in_=cmpa[:], func=AF.Relu)
                    e.drain()
                    e.activation(out=cmpa[:], in_=cmpa[:], func=AF.Sign)
                    e.drain()
                    return e.activation(out=wsgn[:], in_=b[:], func=AF.Sign)
                hSg = P.add("s", [hDs, QS['wsg'], hLd], signs)
                QS['l1l'][slot] = hSg

                def qmul(e, slot=slot):
                    e.tensor_tensor(out=wab[slot][:], in0=cmpa[:], in1=wsgn[:], op=TS.mult)
                    e.drain()
                    e.tensor_tensor(out=qoutb[:], in0=wab[slot][:], in1=screp[slot][:], op=TS.mult)
                    return None
                hQl = P.add("v", [hSg, QS['qo'], hEv], qmul, drain_first=True)
                QS['cmpa'] = hQl
                QS['wsg'] = hQl
                QS['wabl'][slot] = hQl
                QS['srpl'][slot] = hQl
                if PUMPL1 < 7:
                    return
                hSo = P.dma("e", [hQl], (lambda e, scr=scr, rb=rb, cs=cs:
                                         [e.dma_start(out=scr[rb * 128:(rb + 1) * 128, cs:cs + 2048],
                                                      in_=qoutb.ap().bitcast(I16)[:])]))
                QS['qo'] = hSo

            pump_sched = []
            if PHASES >= 2:
                if PUMPSET & 1:
                    for u in range(64):
                        pump_sched.append((u // 4, lambda u=u: l0p1_item(0, w1T, u // 4, (u % 4) * 2048, "e")))
                if PUMPSET & 2:
                    for u in range(64):
                        pump_sched.append((u // 2, lambda u=u: l1_item(0, u)))
                if PUMPSET & 4:
                    for u in range(64):
                        pump_sched.append((2 + u // 2, lambda u=u: l0p1_item(1, w2T, u, 0, "d")))

                def do_wm1():
                    QS['wm1'] = wmean_item(0)
                if PUMPSET & 8:
                    pump_sched.append((16, do_wm1))
                if PUMPSET & 16:
                    for u in range(64):
                        pump_sched.append((17 + u // 2, lambda u=u:
                                           l0p2_item(0, w1T, qw1_scr, u // 4, (u % 4) * 2048, "e")))
                if PUMPSET & 32:
                    for u in range(64):
                        pump_sched.append((32 + u // 2, lambda u=u: l1_item(1, u)))

                def do_wm2():
                    QS['wm2'] = wmean_item(1)
                if PUMPSET & 64:
                    pump_sched.append((34, do_wm2))
                if PUMPSET & 128:
                    for u in range(64):
                        pump_sched.append((35 + u // 2, lambda u=u:
                                           l0p2_item(1, w2T, qw2_scr, u, 0, "d")))
            pump_sched.sort(key=lambda x: x[0])

            def pump(it_now):
                while pump_sched and pump_sched[0][0] <= it_now:
                    pump_sched.pop(0)[1]()

            with contextlib.ExitStack() as p1s:
                SB1 = lambda n, s, dt=F32: p1s.enter_context(nc.sbuf_tensor(n, s, dt))
                wk_s = SB1("wk_s", [128, 16 * 1024], BF16)
                wv_s = SB1("wv_s", [128, 16 * 1024], BF16)
                xt = [SB1(f"p1_xt{i}", [128, D]) for i in range(2)]
                xn1p = SB1("p1_xn", [128, D])
                xnb1p = SB1("p1_xnb", [128, D], BF16)
                xnT1p = SB1("p1_xnT", [128, 16 * 128], BF16)
                kt1p = SB1("p1_kt", [128, 16 * 64], BF16)
                vt1p = SB1("p1_vt", [128, 16 * 65], BF16)
                kvst = SB1("p1_kvst", [64, 16 * 65], BF16)
                bn6 = SB1("p1_bn6", [128, 24])
                bn2 = SB1("p1_bn2", [128, 2])
                statc = SB1("p1_stat", [128, 4])

                hVones = P.add("v", [hV0], lambda e: e.memset(
                    vt1p.ap().rearrange("p (h c) -> p h c", c=65)[:, :, 64:65], 1.0))

                hWld = None
                hXa = None
                hVxT = None
                hEvK = None
                hEvV = None
                hKVa = None
                hXa_hist = {}
                for pss in range(2):
                    def wload(e, pss=pss):
                        return [e.dma_start(out=wk_s.ap().rearrange("p (a b) -> p a b", b=1024),
                                            in_=Wk[:, pss * 1024:(pss + 1) * 1024]
                                            .rearrange("(a p) b -> p a b", p=128)),
                                e.dma_start(out=wv_s.ap().rearrange("p (a b) -> p a b", b=1024),
                                            in_=Wv[:, pss * 1024:(pss + 1) * 1024]
                                            .rearrange("(a p) b -> p a b", p=128))]
                    hWld = P.dma("g", [hG0 if pss == 0 else hKVa], wload, n=2)
                    for i in range(NSUB):
                        it = pss * NSUB + i
                        buf = xt[i % 2]
                        hDx = P.dma("d", [hD0, hXa_hist.get(it - 2)],
                                    (lambda e, i=i, buf=buf: [e.dma_start(out=buf[:], in_=x_seq[i * 128:(i + 1) * 128, :])]))
                        hRc = emit_ln(buf, 128, bn6, bn2, statc, [hDx], war_deps=[hXa])

                        def xapply(e, buf=buf):
                            e.activation(out=xn1p[:], in_=buf[:], func=AF.Identity,
                                         bias=statc[:, 1:2], scale=statc[:, 3:4])
                            e.drain()
                            return e.activation(out=xnb1p[:], in_=xn1p[:], func=AF.Identity)
                        hXa = P.add("s", [hRc, hDx, hVxT], xapply)
                        hXa_hist[it] = hXa

                        def trans(e):
                            last = None
                            for c in range(16):
                                last = e.matmul(pb[c // 4][:, (c % 4) * 128:(c % 4) * 128 + 128],
                                                xnb1p[:, c * 128:(c + 1) * 128], ident[:],
                                                start=True, stop=True, skip_group_check=True)
                            return last
                        hTt = P.add("t", [hXa, hEvK, hEvV, hG0], trans)

                        def evT(e):
                            for q in range(4):
                                e.tensor_copy(out=xnT1p[:, 512 * q:512 * (q + 1)], in_=pb[q][:, 0:512])
                            return None
                        hVxT = P.add("v", [hTt], evT)

                        def proj(e):
                            last = None
                            for c in range(16):
                                s0, s1 = (c == 0), (c == 15)
                                lhsT = xnT1p[:, c * 128:c * 128 + 128]
                                last = e.matmul(pb[0][:], lhsT, wk_s[:, c * 1024:c * 1024 + 512], start=s0, stop=s1)
                                last = e.matmul(pb[1][:], lhsT, wk_s[:, c * 1024 + 512:c * 1024 + 1024], start=s0, stop=s1)
                                last = e.matmul(pb[2][:], lhsT, wv_s[:, c * 1024:c * 1024 + 512], start=s0, stop=s1)
                                last = e.matmul(pb[3][:], lhsT, wv_s[:, c * 1024 + 512:c * 1024 + 1024], start=s0, stop=s1)
                            return last
                        hPr = P.add("t", [hVxT, hWld], proj)

                        def evK(e):
                            e.activation(out=kt1p[:, 0:512], in_=pb[0][:], func=AF.Relu)
                            e.drain()
                            return e.activation(out=kt1p[:, 512:1024], in_=pb[1][:], func=AF.Relu)
                        hEvK = P.add("s", [hPr, hKVa], evK)

                        def evV(e):
                            for q in range(2):
                                e.tensor_copy(
                                    out=vt1p.ap().rearrange("p (h c) -> p h c", c=65)[:, 8 * q:8 * (q + 1), 0:64],
                                    in_=pb[2 + q].ap().rearrange("p (h c) -> p h c", c=64))
                            return None
                        hEvV = P.add("v", [hPr, hVones, hKVa], evV)

                        def kvacc(e, i=i):
                            last = None
                            for h in range(16):
                                bank = pb[4 + h // 6]
                                o = (h % 6) * 85
                                last = e.matmul(bank[0:64, o:o + 65],
                                                kt1p[:, h * 64:(h + 1) * 64],
                                                vt1p[:, h * 65:(h + 1) * 65],
                                                start=(i == 0 and h % 6 == 0),
                                                stop=(i == NSUB - 1 and h in (5, 11, 15)))
                            return last
                        hKVa = P.add("t", [hEvK, hEvV], kvacc)
                        pump(pss * NSUB + i)

                    def evKV(e, pss=pss):
                        for h in range(16):
                            bank = pb[4 + h // 6]
                            o = (h % 6) * 85
                            e.tensor_copy(out=kvst[0:64, h * 65:(h + 1) * 65],
                                          in_=bank[0:64, o:o + 65])
                        return None
                    hKVa = P.add("v", [hKVa], evKV)

                    def kvmove(e, pss=pss):
                        return [e.dma_start(
                                    out=kv_ext.ap().rearrange("p (H c) -> p H c", c=65)[0:64, pss * 8:pss * 8 + 8, :],
                                    in_=kvst.ap().rearrange("p (h c) -> p h c", c=65)[0:64, 0:16:2, :]),
                                e.dma_start(
                                    out=kv_ext.ap().rearrange("p (H c) -> p H c", c=65)[64:128, pss * 8:pss * 8 + 8, :],
                                    in_=kvst.ap().rearrange("p (h c) -> p h c", c=65)[0:64, 1:16:2, :])]
                    hKVa = P.dma("g", [hKVa], kvmove, n=2)
                hP1 = P.barrier()

        if PHASES >= 1:
            pump(10 ** 9)
            qs_.close()
            hWm1 = QS.get('wm1')
            hWm2 = QS.get('wm2')
        hP1b = P.barrier()

        hP2 = hP1b

        def emit_l0_tile(tag, subs, xsrc_fn, dst_fn, sg_fn, deps_in):
            TW = sum(r for _, r in subs)
            ns = len(subs)
            sq = (subs[0][1] == 128)
            tstack = contextlib.ExitStack()
            with tstack:
                SBt = lambda n, s, dt=F32: tstack.enter_context(nc.sbuf_tensor(f"{tag}_{n}", s, dt))

                def RT(n, s, dt=F32):
                    cm = nc.sbuf_tensor(f"{tag}_{n}", s, dt, side="right")
                    return cm, cm.__enter__()

                xb = SBt("xb", [128, D])
                xn2 = SBt("xn2", [128, D])
                xnb2 = SBt("xnb2", [128, D], BF16)
                bn6t = SBt("bn6t", [128, 96])
                bn2t = SBt("bn2t", [128, 2])
                stq = SBt("stq", [128, 8])
                rc2 = SBt("rc2", [128, 2 * ns])
                fc2 = SBt("fc2", [128, 2 * ns])
                frow = SBt("frow", [1, 2 * TW], BF16)
                gbuf = SBt("gbuf", [128, 512])
                bbuf = SBt("bbuf", [128, 512])
                b1b = SBt("b1b", [1, 512], BF16)
                mx4 = SBt("mx4", [128, 16])
                hcm = [None] * ns
                h_s = [None] * ns
                for j in reversed(range(ns)):
                    hcm[j], h_s[j] = RT(f"h{j}", [subs[j][1], HD], F32)
                qaTcm, qaT = RT("qaT", [128, 16 * TW], BF16)
                qwb = [SBt(f"qwb{i}", [128, 4 * 512], BF16) for i in range(2)]

                hLN = {}
                hQs = {}
                hIr = None
                hQtr = None
                hAdone = {}
                for j, (off, rows) in enumerate(subs):
                    hDx = P.dma("d", [deps_in, hAdone.get(j - 1)],
                                (lambda e, j=j, rows=rows:
                                 [e.dma_start(out=xb[0:rows, :], in_=xsrc_fn(j))]))
                    hRc = emit_ln(xb, rows, bn6t, bn2t, stq, [hDx], war_deps=[hLN.get(j - 1)])

                    def savestats(e, j=j, rows=rows):
                        e.tensor_copy(out=rc2[0:rows, 2 * j:2 * j + 1], in_=stq[0:rows, 3:4])
                        e.tensor_copy(out=rc2[0:rows, 2 * j + 1:2 * j + 2], in_=stq[0:rows, 1:2])
                        return None
                    hSv = P.add("v", [hRc], savestats, drain_first=True)

                    def xapply(e, j=j, rows=rows):
                        return e.activation(out=xn2[0:rows, :], in_=xb[0:rows, :], func=AF.Identity,
                                            bias=stq[0:rows, 1:2], scale=stq[0:rows, 3:4])
                    hXa = P.add("s", [hRc, hDx, hQs.get(j - 1)], xapply)
                    hLN[j] = hXa

                    def stats2(e, rows=rows):
                        for c in range(4):
                            e.bn_stats(out=bn6t[0:rows, 6 * c:6 * c + 6], in_=xn2[0:rows, 512 * c:512 * (c + 1)])
                        e.drain()
                        e.bn_aggr(out=bn2t[0:rows, :], in_=bn6t[0:rows, 0:24])
                        e.drain()
                        e.tensor_tensor(out=stq[0:rows, 4:5], in0=bn2t[0:rows, 0:1], in1=bn2t[0:rows, 0:1], op=TS.mult)
                        e.drain()
                        e.tensor_tensor(out=stq[0:rows, 4:5], in0=stq[0:rows, 4:5], in1=bn2t[0:rows, 1:2], op=TS.add)
                        e.drain()
                        e.tensor_scalar(out=stq[0:rows, 4:5], in0=stq[0:rows, 4:5], scalar1=1e-5, scalar2=None,
                                        op0=TS.add, op1=TS.bypass)
                        e.drain()
                        e.tensor_reduce(out=stq[0:rows, 5:6], in_=xn2[0:rows, :], axis=AX.X,
                                        op=TS.max, apply_absolute_value=True)
                        return None
                    hS2 = P.add("v", [hXa, hSv], stats2)
                    hSq2 = P.add("s", [hS2, hXa], (lambda e, rows=rows:
                                                   e.sqrt(out=stq[0:rows, 6:7], in_=stq[0:rows, 4:5])))

                    def qscale(e, j=j, rows=rows):
                        e.reciprocal(out=stq[0:rows, 7:8], in_=stq[0:rows, 6:7])
                        e.drain()
                        e.tensor_tensor(out=stq[0:rows, 5:6], in0=stq[0:rows, 5:6], in1=stq[0:rows, 7:8], op=TS.mult)
                        e.drain()
                        e.tensor_scalar(out=stq[0:rows, 5:6], in0=stq[0:rows, 5:6], scalar1=1e-5, scalar2=None,
                                        op0=TS.max, op1=TS.bypass)
                        e.drain()
                        e.reciprocal(out=stq[0:rows, 4:5], in_=stq[0:rows, 5:6])
                        e.drain()
                        e.tensor_scalar(out=stq[0:rows, 4:5], in0=stq[0:rows, 4:5], scalar1=127.0, scalar2=None,
                                        op0=TS.mult, op1=TS.bypass)
                        e.drain()
                        e.tensor_tensor(out=stq[0:rows, 4:5], in0=stq[0:rows, 4:5], in1=stq[0:rows, 7:8], op=TS.mult)
                        e.drain()
                        e.tensor_tensor(out=fc2[0:rows, 2 * j:2 * j + 1], in0=stq[0:rows, 5:6],
                                        in1=wmrow[0:rows, 0:1], op=TS.mult)
                        e.drain()
                        e.tensor_scalar(out=fc2[0:rows, 2 * j:2 * j + 1], in0=fc2[0:rows, 2 * j:2 * j + 1],
                                        scalar1=float(1.0 / 127.0), scalar2=None, op0=TS.mult, op1=TS.bypass)
                        e.drain()
                        e.reciprocal(out=stq[0:rows, 7:8], in_=fc2[0:rows, 2 * j:2 * j + 1])
                        e.drain()
                        e.tensor_scalar(out=xn2[0:rows, :], in0=xn2[0:rows, :],
                                        scalar1=stq[0:rows, 4:5], scalar2=magic[0:rows, 0:1],
                                        op0=TS.mult, op1=TS.add)
                        e.drain()
                        e.tensor_scalar(out=xn2[0:rows, :], in0=xn2[0:rows, :],
                                        scalar1=magic[0:rows, 0:1], scalar2=-128.0,
                                        op0=TS.subtract, op1=TS.max)
                        e.drain()
                        e.tensor_scalar(out=xnb2[0:rows, :], in0=xn2[0:rows, :],
                                        scalar1=127.0, scalar2=None, op0=TS.min, op1=TS.bypass)
                        return None
                    hQ1 = P.add("v", [hSq2, hWm1, hV0], qscale)
                    hQs[j] = hQ1
                    hIr = P.add("t", [hQ1, hIr, hD0], (lambda e, rows=rows:
                                e.matmul(pb[7][0:1, 0:rows], stq[0:rows, 7:8], identf[0:rows, 0:rows],
                                         start=True, stop=True, skip_group_check=True)))
                    hIr = P.add("v", [hIr], (lambda e, off=off, rows=rows:
                                e.tensor_copy(out=frow[0:1, off:off + rows], in_=pb[7][0:1, 0:rows])))
                    hEvq = None
                    for cc in range(0, 16, 2):
                        def qt2(e, cc=cc, rows=rows):
                            last = None
                            for c in (cc, cc + 1):
                                last = e.matmul(pb[5 + c % 2][0:128, 0:rows], xnb2[0:rows, c * 128:(c + 1) * 128],
                                                ident[0:rows, 0:rows], start=True, stop=True, skip_group_check=True)
                            return last
                        hT2 = P.add("t", [hQ1 if cc == 0 else None, hQtr, hEvq], qt2)
                        hQtr = hT2

                        def qev(e, cc=cc, j=j, off=off, rows=rows):
                            for c in (cc, cc + 1):
                                e.tensor_copy(out=qaT[:, c * TW + off:c * TW + off + rows],
                                              in_=pb[5 + c % 2][0:128, 0:rows])
                            return None
                        hEvq = P.add("v", [hT2], qev)
                    hAdone[j] = hEvq
                    hQs[j] = hEvq
                hAq = P.barrier()
                if L0STOP == 'A':
                    return hAq

                hQld = None
                hEvH = {}
                hMM = {}
                hB1d = None
                lastu = {0: None, 1: None}
                for s in range(16):
                    for kh in range(4):
                        hQld = P.dma("g", [hQld, hAq if s == 0 and kh == 0 else None,
                                           lastu[kh % 2]],
                                     (lambda e, s=s, kh=kh, b=qwb[kh % 2]:
                                      [e.dma_start(out=b.ap().rearrange("p (c n) -> p c n", n=512),
                                                   in_=qw1_scr.rearrange("(c p) n -> p c n", p=128)
                                                   [:, kh * 4:(kh + 1) * 4, s * 512:(s + 1) * 512])]))

                        def mm1(e, s=s, kh=kh):
                            last = None
                            for j, (off, rows) in enumerate(subs):
                                for cc in range(4):
                                    c = kh * 4 + cc
                                    last = e.matmul(pb[j][0:rows, 0:512],
                                                    qaT[:, c * TW + off:c * TW + off + rows],
                                                    qwb[kh % 2][:, cc * 512:(cc + 1) * 512],
                                                    start=(c == 0), stop=False)
                            return last
                        hMMg = P.add("t", [hQld] + [hEvH.get((s - 1, j)) for j in range(ns)], mm1)
                        hMM[(s, kh)] = hMMg
                        lastu[kh % 2] = hMMg
                    hB1d = P.dma("g", [hB1d, hMM[(s, 2)]],
                                 (lambda e, s=s: [e.dma_start(out=b1b[:], in_=b1_row[0:1, s * 512:(s + 1) * 512])]))

                    def fold1(e, s=s):
                        last = None
                        for j, (off, rows) in enumerate(subs):
                            last = e.matmul(pb[j][0:rows, 0:512], frow[0:1, off:off + rows],
                                            b1b[0:1, 0:512], start=False, stop=True)
                        return last
                    hF = P.add("t", [hMM[(s, 3)], hB1d], fold1)
                    for j, (off, rows) in enumerate(subs):
                        def evh(e, s=s, j=j, rows=rows):
                            return e.activation(out=h_s[j][0:rows, s * 512:(s + 1) * 512],
                                                in_=pb[j][0:rows, 0:512], func=AF.Silu,
                                                scale=fc2[0:rows, 2 * j:2 * j + 1])
                        hEvH[(s, j)] = P.add("s", [hF, hEvH.get((s - 1, j))], evh)
                hBq = P.barrier()
                if L0STOP == 'B':
                    qaTcm.__exit__(None, None, None)
                    return hBq
                qaTcm.__exit__(None, None, None)
                sgst = SBt("sgst", [128, 512])
                hSgD_last = None

                pairs = [list(range(ns))[i:i + 2] for i in range(0, ns, 2)]
                hgbd = None
                hQ2 = {}
                hapv = None
                hQld2 = None
                hMM2 = {}
                hEvG = {}
                lastu2 = {0: lastu[0], 1: lastu[1]}
                for pi, pair in enumerate(pairs):
                    xqcm = {}
                    xq2 = {}
                    xqpre = {}
                    xqprecm = {}
                    for j in pair:
                        off, rows = subs[j]
                        if sq:
                            xqcm[j] = nc.sbuf_tensor(f"{tag}_xq2_{j}", [128, HD], BF16)
                            xq2[j] = xqcm[j].__enter__()
                            xqpre[j] = xq2[j]
                        else:
                            xqprecm[j] = nc.sbuf_tensor(f"{tag}_xq2p_{j}", [rows, HD], BF16)
                            xqpre[j] = xqprecm[j].__enter__()
                            xqcm[j] = nc.sbuf_tensor(f"{tag}_xq2_{j}", [128, 64 * rows], BF16)
                            xq2[j] = xqcm[j].__enter__()
                        hGel = P.add("s", [hEvH[(15, j)], hQ2.get(j - 1)], (lambda e, j=j, rows=rows:
                                     e.activation(out=h_s[j][0:rows, :], in_=h_s[j][0:rows, :], func=AF.Gelu)))

                        def hstats(e, j=j, rows=rows):
                            for c in range(16):
                                e.bn_stats(out=bn6t[0:rows, 6 * c:6 * c + 6], in_=h_s[j][0:rows, 512 * c:512 * (c + 1)])
                            e.drain()
                            e.bn_aggr(out=bn2t[0:rows, :], in_=bn6t[0:rows, 0:96])
                            e.drain()
                            e.tensor_scalar(out=stq[0:rows, 0:1], in0=bn2t[0:rows, 1:2], scalar1=1e-5,
                                            scalar2=None, op0=TS.add, op1=TS.bypass)
                            return None
                        hHs = P.add("v", [hGel, hapv], hstats, drain_first=True)
                        hSq3 = P.add("s", [hHs, hGel], (lambda e, rows=rows:
                                                        e.sqrt(out=stq[0:rows, 2:3], in_=stq[0:rows, 0:1])))

                        def recip3(e, rows=rows):
                            e.reciprocal(out=stq[0:rows, 3:4], in_=stq[0:rows, 2:3])
                            e.drain()
                            e.tensor_tensor(out=stq[0:rows, 1:2], in0=bn2t[0:rows, 0:1], in1=stq[0:rows, 3:4], op=TS.mult)
                            e.drain()
                            e.tensor_scalar(out=stq[0:rows, 1:2], in0=stq[0:rows, 1:2], scalar1=-1.0, scalar2=None,
                                            op0=TS.mult, op1=TS.bypass)
                            return None
                        hR3 = P.add("v", [hSq3], recip3)
                        hap = None
                        for cq in range(16):
                            hgbd = P.dma("d", [hgbd, hap, hR3 if cq == 0 else None],
                                         (lambda e, cq=cq:
                                          [e.dma_start(out=gbuf[:], in_=g_rep[:, cq * 512:(cq + 1) * 512]),
                                           e.dma_start(out=bbuf[:], in_=b_rep[:, cq * 512:(cq + 1) * 512])]), n=2)
                            sl = slice(cq * 512, (cq + 1) * 512)

                            def lnap_s(e, j=j, rows=rows, sl=sl):
                                return e.activation(out=h_s[j][0:rows, sl], in_=h_s[j][0:rows, sl],
                                                    func=AF.Identity, bias=stq[0:rows, 1:2], scale=stq[0:rows, 3:4])
                            hAs = P.add("s", [hR3, hap], lnap_s)

                            def lnap_v(e, j=j, rows=rows, sl=sl, cq=cq):
                                e.tensor_tensor(out=h_s[j][0:rows, sl], in0=h_s[j][0:rows, sl],
                                                in1=gbuf[0:rows, :], op=TS.mult)
                                e.drain()
                                e.tensor_tensor(out=h_s[j][0:rows, sl], in0=h_s[j][0:rows, sl],
                                                in1=bbuf[0:rows, :], op=TS.add)
                                e.drain()
                                e.tensor_reduce(out=mx4[0:rows, cq:cq + 1], in_=h_s[j][0:rows, sl],
                                                axis=AX.X, op=TS.max, apply_absolute_value=True)
                                return None
                            hap = P.add("v", [hAs, hgbd], lnap_v)
                        hapv = hap
                        if DEBUG_TAPS and tag == "t0" and j == 0:
                            hap = P.dma("d", [hap], (lambda e, rows=rows:
                                        [e.dma_start(out=taps[2][0:rows, :], in_=h_s[0][0:rows, :])]))

                        def q2stats(e, j=j, rows=rows):
                            for cch in range(16):
                                e.bn_stats(out=bn6t[0:rows, 6 * cch:6 * cch + 6],
                                           in_=h_s[j][0:rows, 512 * cch:512 * (cch + 1)])
                            e.drain()
                            e.bn_aggr(out=bn2t[0:rows, :], in_=bn6t[0:rows, 0:96])
                            e.drain()
                            e.tensor_tensor(out=stq[0:rows, 4:5], in0=bn2t[0:rows, 0:1], in1=bn2t[0:rows, 0:1], op=TS.mult)
                            e.drain()
                            e.tensor_tensor(out=stq[0:rows, 4:5], in0=stq[0:rows, 4:5], in1=bn2t[0:rows, 1:2], op=TS.add)
                            e.drain()
                            e.tensor_scalar(out=stq[0:rows, 4:5], in0=stq[0:rows, 4:5], scalar1=1e-5, scalar2=None,
                                            op0=TS.add, op1=TS.bypass)
                            return None
                        hap = P.add("v", [hap], q2stats, drain_first=True)
                        hap = P.add("s", [hap], (lambda e, rows=rows:
                                                 e.sqrt(out=stq[0:rows, 6:7], in_=stq[0:rows, 4:5])))

                        def q2scale(e, j=j, rows=rows, xqpre=xqpre):
                            e.reciprocal(out=stq[0:rows, 2:3], in_=stq[0:rows, 6:7])
                            e.drain()
                            e.tensor_reduce(out=stq[0:rows, 5:6], in_=mx4[0:rows, 0:16], axis=AX.X, op=TS.max)
                            e.drain()
                            e.tensor_tensor(out=stq[0:rows, 5:6], in0=stq[0:rows, 5:6], in1=stq[0:rows, 2:3], op=TS.mult)
                            e.drain()
                            e.tensor_scalar(out=stq[0:rows, 5:6], in0=stq[0:rows, 5:6], scalar1=1e-5, scalar2=None,
                                            op0=TS.max, op1=TS.bypass)
                            e.drain()
                            e.reciprocal(out=stq[0:rows, 4:5], in_=stq[0:rows, 5:6])
                            e.drain()
                            e.tensor_scalar(out=stq[0:rows, 4:5], in0=stq[0:rows, 4:5], scalar1=127.0, scalar2=None,
                                            op0=TS.mult, op1=TS.bypass)
                            e.drain()
                            e.tensor_tensor(out=stq[0:rows, 4:5], in0=stq[0:rows, 4:5], in1=stq[0:rows, 2:3], op=TS.mult)
                            e.drain()
                            e.tensor_tensor(out=fc2[0:rows, 2 * j + 1:2 * j + 2], in0=stq[0:rows, 5:6],
                                            in1=wmrow[0:rows, 2:3], op=TS.mult)
                            e.drain()
                            e.tensor_scalar(out=fc2[0:rows, 2 * j + 1:2 * j + 2], in0=fc2[0:rows, 2 * j + 1:2 * j + 2],
                                            scalar1=float(1.0 / 127.0), scalar2=None, op0=TS.mult, op1=TS.bypass)
                            e.drain()
                            e.reciprocal(out=stq[0:rows, 7:8], in_=fc2[0:rows, 2 * j + 1:2 * j + 2])
                            e.drain()
                            e.tensor_scalar(out=h_s[j][0:rows, :], in0=h_s[j][0:rows, :],
                                            scalar1=stq[0:rows, 4:5], scalar2=magic[0:rows, 0:1],
                                            op0=TS.mult, op1=TS.add)
                            e.drain()
                            e.tensor_scalar(out=h_s[j][0:rows, :], in0=h_s[j][0:rows, :],
                                            scalar1=magic[0:rows, 0:1], scalar2=-128.0,
                                            op0=TS.subtract, op1=TS.max)
                            e.drain()
                            e.tensor_scalar(out=xqpre[j][0:rows, :], in0=h_s[j][0:rows, :],
                                            scalar1=127.0, scalar2=None, op0=TS.min, op1=TS.bypass)
                            return None
                        hQ2v = P.add("v", [hap, hWm2], q2scale)
                        if DEBUG_TAPS and tag == "t0" and j == 0:
                            hQ2v = P.dma("g", [hQ2v], (lambda e, rows=rows, xqpre=xqpre:
                                         [e.dma_start(out=taps[3][0:rows, :], in_=xqpre[0][0:rows, :])]))
                        hIr = P.add("t", [hQ2v, hIr], (lambda e, rows=rows:
                                    e.matmul(pb[7][0:1, 0:rows], stq[0:rows, 7:8], identf[0:rows, 0:rows],
                                             start=True, stop=True, skip_group_check=True)))
                        hIr = P.add("v", [hIr], (lambda e, off=off, rows=rows:
                                    e.tensor_copy(out=frow[0:1, TW + off:TW + off + rows], in_=pb[7][0:1, 0:rows])))
                        hTq2 = None
                        hEvq2 = None
                        for cc in range(0, 64, 2):
                            def xt2(e, cc=cc, j=j, rows=rows, xqpre=xqpre):
                                last = None
                                for c in (cc, cc + 1):
                                    last = e.matmul(pb[5 + c % 2][0:128, 0:rows],
                                                    xqpre[j][0:rows, c * 128:(c + 1) * 128],
                                                    ident[0:rows, 0:rows], start=True, stop=True, skip_group_check=True)
                                return last
                            hT2 = P.add("t", [hQ2v if cc == 0 else None, hTq2, hEvq2, hIr if cc == 0 else None], xt2)
                            hTq2 = hT2

                            def xev(e, cc=cc, j=j, rows=rows, xq2=xq2):
                                for c in (cc, cc + 1):
                                    if sq:
                                        e.tensor_copy(out=xq2[j][0:128, c * 128:(c + 1) * 128],
                                                      in_=pb[5 + c % 2][0:128, 0:128])
                                    else:
                                        e.tensor_copy(out=xq2[j][0:128, c * rows:(c + 1) * rows],
                                                      in_=pb[5 + c % 2][0:128, 0:rows])
                                return None
                            hEvq2 = P.add("v", [hT2], xev)
                        hQ2[j] = hEvq2
                        hcm[j].__exit__(None, None, None)
                    for s4 in range(4):
                        for kh in range(16):
                            hQld2 = P.dma("g", [hQld2, [hQ2[j] for j in pair] if s4 == 0 and kh == 0 else None,
                                                lastu2[kh % 2]],
                                          (lambda e, s4=s4, kh=kh, b=qwb[kh % 2]:
                                           [e.dma_start(out=b.ap().rearrange("p (c n) -> p c n", n=512),
                                                        in_=qw2_scr.rearrange("(c p) n -> p c n", p=128)
                                                        [:, kh * 4:(kh + 1) * 4, s4 * 512:(s4 + 1) * 512])]))

                            def mm2(e, s4=s4, kh=kh, pair=pair, xq2=xq2):
                                last = None
                                for j in pair:
                                    off, rows = subs[j]
                                    for cc in range(4):
                                        c = kh * 4 + cc
                                        last = e.matmul(pb[j % 2][0:rows, 0:512],
                                                        xq2[j][:, c * rows:(c + 1) * rows],
                                                        qwb[kh % 2][:, cc * 512:(cc + 1) * 512],
                                                        start=(c == 0), stop=False)
                                return last
                            hMMg = P.add("t", [hQld2] + [hEvG.get((pi, s4 - 1, j), (None,))[0] for j in pair]
                                         + ([hQ2[j] for j in pair] if s4 == 0 and kh == 0 else []), mm2)
                            hMM2[(pi, s4, kh)] = hMMg
                            lastu2[kh % 2] = hMMg
                        hB1d = P.dma("g", [hB1d, hMM2[(pi, s4, 14)]],
                                     (lambda e, s4=s4: [e.dma_start(out=b1b[:], in_=b2_row[0:1, s4 * 512:(s4 + 1) * 512])]))

                        def fold2(e, s4=s4, pair=pair):
                            last = None
                            for j in pair:
                                off, rows = subs[j]
                                last = e.matmul(pb[j % 2][0:rows, 0:512], frow[0:1, TW + off:TW + off + rows],
                                                b1b[0:1, 0:512], start=False, stop=True)
                            return last
                        hF2 = P.add("t", [hMM2[(pi, s4, 15)], hB1d], fold2)
                        for j in pair:
                            off, rows = subs[j]

                            def evg(e, s4=s4, j=j, rows=rows):
                                return e.activation(out=sgst[0:rows, 0:512],
                                                    in_=pb[j % 2][0:rows, 0:512], func=AF.Silu,
                                                    scale=fc2[0:rows, 2 * j + 1:2 * j + 2])
                            hEvGs = P.add("s", [hF2, hSgD_last], evg)
                            hSgD = P.dma("d", [hEvGs],
                                         (lambda e, s4=s4, j=j, rows=rows:
                                          [e.dma_start(out=sg_fn(j)[0:rows, s4 * 512:(s4 + 1) * 512],
                                                       in_=sgst[0:rows, 0:512])]))
                            hSgD_last = hSgD
                            hEvG[(pi, s4, j)] = (hSgD, hEvGs)
                    hPairDone = P.barrier()
                    for j in reversed(pair):
                        xqcm[j].__exit__(None, None, None)
                        if not sq:
                            xqprecm[j].__exit__(None, None, None)
                hDq = P.barrier()
                if L0STOP == 'D':
                    return hDq

                xnT = SBt("xnT", [128, 16 * TW], BF16)
                qT = SBt("qT", [128, 16 * TW], BF16)
                inner = SBt("inner", [128, D], BF16)
                innerT = SBt("innerT", [128, 16 * TW], BF16)
                normc = SBt("normc", [128, 32])
                recn = SBt("recn", [128, 32])
                s_s = [SBt(f"ss{j}", [r, D]) for j, (_, r) in enumerate(subs)]
                wobcm = []
                wob = []
                for i in range(2):
                    cm, tt = RT(f"wob{i}", [128, 8 * 512], BF16)
                    wobcm.append(cm)
                    wob.append(tt)

                hXn = {}
                hTx = None
                hEvX = None
                for j, (off, rows) in enumerate(subs):
                    hDx = P.dma("d", [hDq, hXn.get(j - 1)],
                                (lambda e, j=j, rows=rows:
                                 [e.dma_start(out=xb[0:rows, :], in_=xsrc_fn(j))]))

                    def xapp2(e, j=j, rows=rows):
                        e.activation(out=xn2[0:rows, :], in_=xb[0:rows, :], func=AF.Identity,
                                     bias=rc2[0:rows, 2 * j + 1:2 * j + 2], scale=rc2[0:rows, 2 * j:2 * j + 1])
                        e.drain()
                        return e.activation(out=xnb2[0:rows, :], in_=xn2[0:rows, :], func=AF.Identity)
                    hXa2 = P.add("s", [hDx, hDq, hXn.get(j - 1)], xapp2)
                    for cc in range(0, 16, 2):
                        def xt3(e, cc=cc, rows=rows):
                            last = None
                            for c in (cc, cc + 1):
                                last = e.matmul(pb[5 + c % 2][0:128, 0:rows], xnb2[0:rows, c * 128:(c + 1) * 128],
                                                ident[0:rows, 0:rows], start=True, stop=True, skip_group_check=True)
                            return last
                        hT3 = P.add("t", [hXa2 if cc == 0 else None, hTx, hEvX], xt3)
                        hTx = hT3

                        def xev3(e, cc=cc, off=off, rows=rows):
                            for c in (cc, cc + 1):
                                e.tensor_copy(out=xnT[:, c * TW + off:c * TW + off + rows],
                                              in_=pb[5 + c % 2][0:128, 0:rows])
                            return None
                        hEvX = P.add("v", [hT3], xev3)
                    hXn[j] = hEvX
                hE1 = P.barrier()
                if L0STOP == 'E1':
                    return hE1

                hWqd = None
                hQe = {}
                lastu3 = {0: None, 1: None}
                for qc4 in range(4):
                    for khq in range(2):
                        hWqd = P.dma("g", [hWqd, hE1 if qc4 == 0 and khq == 0 else None, lastu3[khq]],
                                     (lambda e, qc4=qc4, khq=khq, b=wob[khq]:
                                      [e.dma_start(out=b.ap().rearrange("p (c n) -> p c n", n=512),
                                                   in_=Wq.rearrange("(c p) n -> p c n", p=128)
                                                   [:, khq * 8:(khq + 1) * 8, qc4 * 512:(qc4 + 1) * 512])]))

                        def qproj(e, qc4=qc4, khq=khq):
                            last = None
                            for qq in range(4):
                                for cc in range(8):
                                    c = khq * 8 + cc
                                    last = e.matmul(pb[qq][0:128, 0:TW],
                                                    wob[khq][:, cc * 512 + qq * 128:cc * 512 + (qq + 1) * 128],
                                                    xnT[:, c * TW:(c + 1) * TW],
                                                    start=(c == 0), stop=(c == 15))
                            return last
                        hQp = P.add("t", [hWqd, hE1, hQe.get(qc4 - 1)], qproj)
                        lastu3[khq] = hQp

                    def qev4(e, qc4=qc4):
                        for qq in range(4):
                            qc = qc4 * 4 + qq
                            e.activation(out=qT[:, qc * TW:(qc + 1) * TW], in_=pb[qq][0:128, 0:TW], func=AF.Relu)
                        return None
                    hQe[qc4] = P.add("s", [hQp, hQe.get(qc4 - 1)], qev4)
                hE2 = P.barrier()
                if L0STOP == 'E2':
                    return hE2

                hIt = None
                hIev = None
                hQkv = {}
                for j, (off, rows) in enumerate(subs):
                    def qkv(e, j=j, off=off, rows=rows):
                        last = None
                        order = [H for H in range(32) if H % 2 == 0] + [H for H in range(32) if H % 2 == 1]
                        for i, H in enumerate(order):
                            if i == 16:
                                e.drain()
                            last = e.matmul(pb[H // 4][0:rows, (H % 4) * 128:(H % 4) * 128 + 65],
                                            qT[(H % 2) * 64:(H % 2) * 64 + 64,
                                               (H // 2) * TW + off:(H // 2) * TW + off + rows],
                                            kv_ext[(H % 2) * 64:(H % 2) * 64 + 64,
                                                   (H // 2) * 65:(H // 2) * 65 + 65],
                                            start=True, stop=True, skip_group_check=True)
                        return last
                    hK = P.add("t", [hE2 if j == 0 else None, hQkv.get(j - 1), hIt], qkv)

                    def ngather(e, j=j, rows=rows):
                        for b in range(8):
                            e.tensor_copy(out=normc[0:rows, b * 4:(b + 1) * 4],
                                          in_=bass.AP(pb[b], 64, [[512, rows], [128, 4]]))
                        e.drain()
                        e.tensor_scalar(out=normc[0:rows, :], in0=normc[0:rows, :], scalar1=1e-6,
                                        scalar2=None, op0=TS.max, op1=TS.bypass)
                        e.drain()
                        e.reciprocal(out=recn[0:rows, :], in_=normc[0:rows, :])
                        e.drain()
                        for H in range(32):
                            e.tensor_scalar(out=inner[0:rows, H * 64:(H + 1) * 64],
                                            in0=pb[H // 4][0:rows, (H % 4) * 128:(H % 4) * 128 + 64],
                                            scalar1=recn[0:rows, H:H + 1], scalar2=None,
                                            op0=TS.mult, op1=TS.bypass)
                        return None
                    hN = P.add("v", [hK, hIev], ngather, drain_first=True)
                    for cc in range(0, 16, 2):
                        def it2(e, cc=cc, rows=rows):
                            last = None
                            for c in (cc, cc + 1):
                                last = e.matmul(pb[5 + c % 2][0:128, 0:rows], inner[0:rows, c * 128:(c + 1) * 128],
                                                ident[0:rows, 0:rows], start=True, stop=True, skip_group_check=True)
                            return last
                        hT4 = P.add("t", [hN if cc == 0 else None, hIt, hIev], it2)
                        hIt = hT4

                        def iev(e, cc=cc, off=off, rows=rows):
                            for c in (cc, cc + 1):
                                e.tensor_copy(out=innerT[:, c * TW + off:c * TW + off + rows],
                                              in_=pb[5 + c % 2][0:128, 0:rows])
                            return None
                        hIev = P.add("v", [hT4], iev)
                    hQkv[j] = hIev
                hE3 = P.barrier()
                if L0STOP == 'E3':
                    return hE3

                hWod = None
                hWoe = {}
                lastu4 = {0: lastu3[0], 1: lastu3[1]}
                for s4 in range(4):
                    for kho in range(2):
                        hWod = P.dma("g", [hWod, hE3 if s4 == 0 and kho == 0 else None, lastu4[kho]],
                                     (lambda e, s4=s4, kho=kho, b=wob[kho]:
                                      [e.dma_start(out=b.ap().rearrange("p (c n) -> p c n", n=512),
                                                   in_=Wo.rearrange("(c p) n -> p c n", p=128)
                                                   [:, kho * 8:(kho + 1) * 8, s4 * 512:(s4 + 1) * 512])]))

                        def womm(e, s4=s4, kho=kho):
                            last = None
                            for j, (off, rows) in enumerate(subs):
                                for cc in range(8):
                                    c = kho * 8 + cc
                                    last = e.matmul(pb[j][0:rows, 0:512],
                                                    innerT[:, c * TW + off:c * TW + off + rows],
                                                    wob[kho][:, cc * 512:(cc + 1) * 512],
                                                    start=(c == 0), stop=False)
                            return last
                        hWp2 = P.add("t", [hWod, hE3 if s4 == 0 and kho == 0 else None,
                                           hWoe.get(s4 - 1)], womm)
                        lastu4[kho] = hWp2

                    def bofold(e, s4=s4):
                        last = None
                        for j, (off, rows) in enumerate(subs):
                            last = e.matmul(pb[j][0:rows, 0:512], ones1b[0:1, 0:rows],
                                            bo16[0:1, s4 * 512:(s4 + 1) * 512], start=False, stop=True)
                        return last
                    hWp2 = P.add("t", [hWp2, hG0], bofold)

                    def woev(e, s4=s4):
                        for j, (off, rows) in enumerate(subs):
                            e.tensor_copy(out=s_s[j][0:rows, s4 * 512:(s4 + 1) * 512],
                                          in_=pb[j][0:rows, 0:512])
                        return None
                    hWoe[s4] = P.add("v", [hWp2], woev)
                hE4 = P.barrier()
                if L0STOP == 'E':
                    wobcm[1].__exit__(None, None, None)
                    wobcm[0].__exit__(None, None, None)
                    return hE4
                wobcm[1].__exit__(None, None, None)
                wobcm[0].__exit__(None, None, None)

                hCd = None
                hCb = None
                for j, (off, rows) in enumerate(subs):
                    hDx = P.dma("d", [hE4, hCd, hCb],
                                (lambda e, j=j, rows=rows:
                                 [e.dma_start(out=xb[0:rows, :], in_=xsrc_fn(j)),
                                  e.dma_start(out=xn2[0:rows, :], in_=sg_fn(j))]), n=2)

                    def comb(e, j=j, rows=rows):
                        e.tensor_tensor(out=xn2[0:rows, :], in0=xn2[0:rows, :],
                                        in1=s_s[j][0:rows, :], op=TS.mult)
                        e.drain()
                        e.tensor_tensor(out=xn2[0:rows, :], in0=xn2[0:rows, :],
                                        in1=xb[0:rows, :], op=TS.add)
                        return None
                    hCb = P.add("v", [hDx, hE4], comb)
                    hCd = P.dma("d", [hCb], (lambda e, j=j, rows=rows:
                                             [e.dma_start(out=dst_fn(j), in_=xn2[0:rows, :])]))
            return [hCd]


        if PHASES >= 3:
            hP2 = emit_l0_tile("td", [(0, 64)],
                               lambda j: x_seq[0:64, :],
                               lambda j: x1d_scr[0:64, :],
                               lambda j: sgd_scr[0:64, :], hP1b)
            for t in range(min(NTILE, P2TILES)):
                hP2 = emit_l0_tile(f"t{t}", [(j * 128, 128) for j in range(4)],
                                   lambda j, t=t: x_own[t * 512 + j * 128:t * 512 + (j + 1) * 128, :],
                                   lambda j, t=t: x1_scr[t * 512 + j * 128:t * 512 + (j + 1) * 128, :],
                                   lambda j, t=t: sg_scr[t * 512 + j * 128:t * 512 + (j + 1) * 128, :],
                                   hP2)

        hP3 = hP2 if PHASES >= 3 else hP1b
        if PHASES >= 4:
            with contextlib.ExitStack() as p3s:
                SB3 = lambda n, s, dt=F32: p3s.enter_context(nc.sbuf_tensor(n, s, dt))
                x1d = SB3("p3_x1d", [64, D])
                lq_s = SB3("p3_lq", [128, 16 * 64], BF16)
                bn63 = SB3("p3_bn6", [64, 24])
                bn23 = SB3("p3_bn2", [64, 2])
                st3 = SB3("p3_st", [64, 4])
                xn3 = SB3("p3_xn", [64, D])
                xnb3 = SB3("p3_xnb", [64, D], BF16)
                xnT3 = SB3("p3_xnT", [128, 16 * 64], BF16)
                wb3 = [SB3(f"p3_wb{i}", [128, 16 * 512], BF16) for i in range(2)]
                k64 = SB3("p3_k64", [64, D])
                v64 = SB3("p3_v64", [64, D], BF16)
                kr = SB3("p3_kr", [64, D])
                krb = SB3("p3_krb", [64, D], BF16)
                krT = SB3("p3_krT", [128, 16 * 64], BF16)
                cosb = SB3("p3_cos", [64, 16 * 64])
                sinb = SB3("p3_sin", [64, 16 * 64])
                scf = SB3("p3_scf", [64, 64])
                nmcol = SB3("p3_nm", [64, 2])
                wsm = SB3("p3_wsm", [64, 64])
                wsb = SB3("p3_wsb", [64, 64], BF16)
                wTb = SB3("p3_wTb", [64, 64], BF16)
                ast = SB3("p3_ast", [64, 16 * 128], BF16)

                hXd = P.dma("d", hP3, lambda e: [e.dma_start(out=x1d[:], in_=x1d_scr),
                                                 e.dma_start(out=cosb[:], in_=c_cos),
                                                 e.dma_start(out=sinb[:], in_=c_sin)], n=3)
                hLq = P.dma("g", hP3, lambda e: [e.dma_start(out=lq_s[:], in_=lqT)], n=1)
                hRc3 = emit_ln(x1d, 64, bn63, bn23, st3, [hXd])

                def xap3(e):
                    e.activation(out=xn3[:], in_=x1d[:], func=AF.Identity,
                                 bias=st3[:, 1:2], scale=st3[:, 3:4])
                    e.drain()
                    return e.activation(out=xnb3[:], in_=xn3[:], func=AF.Identity)
                hXa3 = P.add("s", [hRc3, hXd], xap3)
                hT5 = None
                hE5 = None
                for c in range(16):
                    hT5 = P.add("t", [hXa3 if c == 0 else None, hT5, hE5],
                                (lambda e, c=c: e.matmul(pb[c % 2][0:128, 0:64],
                                                         xnb3[0:64, c * 128:(c + 1) * 128],
                                                         ident[0:64, 0:64],
                                                         start=True, stop=True, skip_group_check=True)))
                    hE5 = P.add("v", [hT5], (lambda e, c=c:
                                e.tensor_copy(out=xnT3[:, c * 64:(c + 1) * 64], in_=pb[c % 2][0:128, 0:64])))
                hB3 = P.barrier()

                hWd3 = None
                hPrj = {}
                for wi, (wt, kind) in enumerate([(Wk1, "k"), (Wv1, "v")]):
                    for s4 in range(4):
                        hWd3 = P.dma("g", [hWd3, hB3 if wi == 0 and s4 == 0 else None,
                                           hPrj.get((wi, s4 - 1)) or hPrj.get((wi - 1, 3))],
                                     (lambda e, wt=wt, s4=s4, b=wb3[s4 % 2]:
                                      [e.dma_start(out=b.ap().rearrange("p (c n) -> p c n", n=512),
                                                   in_=wt.rearrange("(c p) n -> p c n", p=128)
                                                   [:, :, s4 * 512:(s4 + 1) * 512])]))

                        def prj(e, s4=s4, b=wb3[s4 % 2]):
                            last = None
                            for c in range(16):
                                last = e.matmul(pb[4][0:64, 0:512], xnT3[:, c * 64:(c + 1) * 64],
                                                b[:, c * 512:(c + 1) * 512], start=(c == 0), stop=(c == 15))
                            return last
                        hPj = P.add("t", [hWd3, hB3, hPrj.get((wi, s4 - 1)) or hPrj.get((wi - 1, 3))], prj)
                        if kind == "k":
                            hPrj[(wi, s4)] = P.add("v", [hPj], (lambda e, s4=s4:
                                            e.tensor_copy(out=k64[:, s4 * 512:(s4 + 1) * 512], in_=pb[4][0:64, 0:512])))
                        else:
                            hPrj[(wi, s4)] = P.add("v", [hPj], (lambda e, s4=s4:
                                            e.tensor_copy(out=v64[:, s4 * 512:(s4 + 1) * 512], in_=pb[4][0:64, 0:512])))
                hB4 = P.barrier()

                def rope(e):
                    kv_ = k64.ap().rearrange("p (h i two) -> p h i two", two=2, i=64)
                    kr_ = kr.ap().rearrange("p (h i two) -> p h i two", two=2, i=64)
                    cos3 = cosb.ap().rearrange("p (h i) -> p h i", i=64).unsqueeze(-1)
                    sin3 = sinb.ap().rearrange("p (h i) -> p h i", i=64).unsqueeze(-1)
                    e.tensor_tensor(out=kr_[:, :, :, 0:1], in0=kv_[:, :, :, 0:1], in1=cos3, op=TS.mult)
                    e.drain()
                    e.tensor_tensor(out=kr_[:, :, :, 1:2], in0=kv_[:, :, :, 0:1], in1=sin3, op=TS.mult)
                    e.drain()
                    t_ = xn3.ap().rearrange("p (h i two) -> p h i two", two=2, i=64)
                    e.tensor_tensor(out=t_[:, :, :, 0:1], in0=kv_[:, :, :, 1:2], in1=sin3, op=TS.mult)
                    e.drain()
                    e.tensor_tensor(out=t_[:, :, :, 1:2], in0=kv_[:, :, :, 1:2], in1=cos3, op=TS.mult)
                    e.drain()
                    e.tensor_tensor(out=kr_[:, :, :, 0:1], in0=kr_[:, :, :, 0:1], in1=t_[:, :, :, 0:1], op=TS.subtract)
                    e.drain()
                    e.tensor_tensor(out=kr_[:, :, :, 1:2], in0=kr_[:, :, :, 1:2], in1=t_[:, :, :, 1:2], op=TS.add)
                    return None
                hRp3 = P.add("v", [hB4], rope, drain_first=True)
                hKb = P.add("s", [hRp3], lambda e: e.activation(out=krb[:], in_=kr[:], func=AF.Identity))
                hT6 = None
                hE6 = None
                for c in range(16):
                    hT6 = P.add("t", [hKb if c == 0 else None, hT6, hE6],
                                (lambda e, c=c: e.matmul(pb[c % 2][0:128, 0:64],
                                                         krb[0:64, c * 128:(c + 1) * 128],
                                                         ident[0:64, 0:64],
                                                         start=True, stop=True, skip_group_check=True)))
                    hE6 = P.add("v", [hT6], (lambda e, c=c:
                                e.tensor_copy(out=krT[:, c * 64:(c + 1) * 64], in_=pb[c % 2][0:128, 0:64])))
                hB5 = P.barrier()

                hSc = {}
                hAt = None
                hAe = None
                for h in range(16):
                    hSc[h] = P.add("t", [hB5 if h == 0 else None, hAt,
                                         hSc.get(h - 4), hLq if h == 0 else None],
                                   (lambda e, h=h: e.matmul(pb[h % 4][0:64, (h // 4) * 128:(h // 4) * 128 + 64],
                                                            lq_s[:, h * 64:(h + 1) * 64],
                                                            krT[:, h * 64:(h + 1) * 64],
                                                            start=True, stop=True, skip_group_check=True)))

                    def smax(e, h=h):
                        e.scalar_tensor_tensor(out=scf[:], in0=pb[h % 4][0:64, (h // 4) * 128:(h // 4) * 128 + 64],
                                               scalar=SC1, in1=mask_s[:], op0=TS.mult, op1=TS.add)
                        e.drain()
                        e.tensor_reduce(out=nmcol[:, 0:1], in_=scf[:], axis=AX.X, op=TS.max, negate=True)
                        return None
                    hSm = P.add("v", [hSc[h], hAe], smax)
                    hEx = P.add("s", [hSm], (lambda e: e.activation(out=wsm[:], in_=scf[:], func=AF.Exp,
                                                                    bias=nmcol[:, 0:1])))

                    def norm(e):
                        e.tensor_reduce(out=nmcol[:, 1:2], in_=wsm[:], axis=AX.X, op=TS.add)
                        e.drain()
                        e.reciprocal(out=nmcol[:, 0:1], in_=nmcol[:, 1:2])
                        e.drain()
                        e.tensor_scalar(out=wsb[:], in0=wsm[:], scalar1=nmcol[:, 0:1], scalar2=None,
                                        op0=TS.mult, op1=TS.bypass)
                        return None
                    hNm = P.add("v", [hEx], norm, drain_first=True)
                    hTw = P.add("t", [hNm, hSc[h]], (lambda e, h=h:
                                e.matmul(pb[h % 4][0:64, (h // 4) * 128:(h // 4) * 128 + 64],
                                         wsb[:], ident[0:64, 0:64], start=True, stop=True,
                                         skip_group_check=True)))
                    hWt = P.add("v", [hTw], (lambda e, h=h:
                                e.tensor_copy(out=wTb[:], in_=pb[h % 4][0:64, (h // 4) * 128:(h // 4) * 128 + 64])))
                    hAt = P.add("t", [hWt], (lambda e, h=h:
                                e.matmul(pb[4 + h % 2][0:64, 0:128], wTb[:],
                                         v64[0:64, h * 128:(h + 1) * 128], start=True, stop=True,
                                         skip_group_check=True)))
                    hAe = P.add("v", [hAt], (lambda e, h=h:
                                e.tensor_copy(out=ast[:, h * 128:(h + 1) * 128], in_=pb[4 + h % 2][0:64, 0:128])))
                hP3 = [P.dma("g", [hAe], lambda e: [
                    e.dma_start(out=attn_s.ap().rearrange("p (H c) -> p H c", c=128)[0:64, :, :],
                                in_=ast.ap().rearrange("p (h c) -> p h c", c=128)[0:64, 0:16:2, :]),
                    e.dma_start(out=attn_s.ap().rearrange("p (H c) -> p H c", c=128)[64:128, :, :],
                                in_=ast.ap().rearrange("p (h c) -> p h c", c=128)[0:64, 1:16:2, :])], n=2)]

        hP4 = hP3

        def emit_l1_tile(tag, t, deps_in):
            subs = [(j * 128, 128) for j in range(4)]
            TW = 512
            tstack = contextlib.ExitStack()
            with tstack:
                SBt = lambda n, s, dt=F32: tstack.enter_context(nc.sbuf_tensor(f"{tag}_{n}", s, dt))

                def RT(n, s, dt=F32):
                    cm = nc.sbuf_tensor(f"{tag}_{n}", s, dt, side="right")
                    return cm, cm.__enter__()

                xb = SBt("xb", [128, D])
                xn2 = SBt("xn2", [128, D])
                xnb2 = SBt("xnb2", [128, D], BF16)
                bn6t = SBt("bn6t", [128, 96])
                bn2t = SBt("bn2t", [128, 2])
                stq = SBt("stq", [128, 8])
                xnT = SBt("xnT", [128, 16 * TW], BF16)
                hf = SBt("hf", [128, 512])
                sg1 = [SBt(f"sg1{j}", [128, D]) for j in range(4)]
                h1cm = [None] * 4
                h1_s = [None] * 4
                for j in reversed(range(4)):
                    h1cm[j], h1_s[j] = RT(f"h1{j}", [128, HD], BF16)
                ibufcm = []
                ibuf = []
                for i in range(2):
                    cm, tt = RT(f"ibuf{i}", [128, 8 * 512], I16)
                    ibufcm.append(cm)
                    ibuf.append(tt)

                hXn = {}
                hTx = None
                hEvX = None
                for j, (off, rows) in enumerate(subs):
                    hDx = P.dma("d", [deps_in, hXn.get(j - 1)],
                                (lambda e, j=j:
                                 [e.dma_start(out=xb[:], in_=x1_scr[t * 512 + j * 128:t * 512 + (j + 1) * 128, :])]))
                    hRc = emit_ln(xb, rows, bn6t, bn2t, stq, [hDx], war_deps=[hXn.get(j - 1)])

                    def xapply(e):
                        e.activation(out=xn2[:], in_=xb[:], func=AF.Identity,
                                     bias=stq[:, 1:2], scale=stq[:, 3:4])
                        e.drain()
                        return e.activation(out=xnb2[:], in_=xn2[:], func=AF.Identity)
                    hXa = P.add("s", [hRc, hDx, hXn.get(j - 1)], xapply)
                    for cc in range(0, 16, 2):
                        def xt3(e, cc=cc):
                            last = None
                            for c in (cc, cc + 1):
                                last = e.matmul(pb[5 + c % 2][:, 0:128], xnb2[:, c * 128:(c + 1) * 128],
                                                ident[:], start=True, stop=True, skip_group_check=True)
                            return last
                        hT3 = P.add("t", [hXa if cc == 0 else None, hTx, hEvX], xt3)
                        hTx = hT3

                        def xev3(e, cc=cc, off=off):
                            for c in (cc, cc + 1):
                                e.tensor_copy(out=xnT[:, c * TW + off:c * TW + off + 128],
                                              in_=pb[5 + c % 2][:, 0:128])
                            return None
                        hEvX = P.add("v", [hT3], xev3)
                    hXn[j] = hEvX
                hL1 = P.barrier()

                hQld = None
                hEvH = {}
                hMM = {}
                lastf = {0: None, 1: None}
                for s in range(16):
                    for kh in range(2):
                        hQld = P.dma("g", [hQld, hL1 if s == 0 and kh == 0 else None, lastf[kh]],
                                     (lambda e, s=s, kh=kh, b=ibuf[kh]:
                                      [e.dma_start(out=b.ap().rearrange("p (c n) -> p c n", n=512),
                                                   in_=qfc_scr.rearrange("(c p) n -> p c n", p=128)
                                                   [:, kh * 8:(kh + 1) * 8, s * 512:(s + 1) * 512])]))

                        def mmfc(e, s=s, kh=kh):
                            last = None
                            for j in range(4):
                                for cc in range(8):
                                    c = kh * 8 + cc
                                    last = e.matmul(pb[j][:, 0:512],
                                                    xnT[:, c * TW + j * 128:c * TW + (j + 1) * 128],
                                                    ibuf[kh].ap().bitcast(BF16)[:, cc * 512:(cc + 1) * 512],
                                                    start=(c == 0), stop=(c == 15))
                            return last
                        hMMg = P.add("t", [hQld] + [hEvH.get((s - 1, j)) for j in range(4)]
                                     + ([hL1] if s == 0 and kh == 0 else []), mmfc)
                        hMM[(s, kh)] = hMMg
                        lastf[kh] = hMMg
                    for j in range(4):
                        def evh(e, s=s, j=j):
                            e.activation(out=hf[:, 0:512], in_=pb[j][:, 0:512], func=AF.Relu)
                            e.drain()
                            return e.activation(out=h1_s[j][:, s * 512:(s + 1) * 512], in_=hf[:, 0:512],
                                                func=AF.Square)
                        hEvH[(s, j)] = P.add("s", [hMM[(s, 1)], hEvH.get((s - 1, j)),
                                                   hEvH.get((s, j - 1)) or hEvH.get((s - 1, 3))], evh)
                hL2 = P.barrier()

                hRms = {}
                hTq2 = None
                for j in range(4):
                    def rstats(e, j=j):
                        for c in range(16):
                            e.bn_stats(out=bn6t[:, 6 * c:6 * c + 6], in_=h1_s[j][:, 512 * c:512 * (c + 1)])
                        e.drain()
                        e.bn_aggr(out=bn2t[:], in_=bn6t[:, 0:96])
                        e.drain()
                        e.tensor_tensor(out=stq[:, 0:1], in0=bn2t[:, 0:1], in1=bn2t[:, 0:1], op=TS.mult)
                        e.drain()
                        e.tensor_tensor(out=stq[:, 0:1], in0=stq[:, 0:1], in1=bn2t[:, 1:2], op=TS.add)
                        e.drain()
                        e.tensor_tensor(out=stq[:, 0:1], in0=stq[:, 0:1], in1=epsd[:, 0:1], op=TS.add)
                        return None
                    hRs = P.add("v", [hL2 if j == 0 else None, hRms.get(j - 1)], rstats, drain_first=True)
                    hSq = P.add("s", [hRs, hEvH.get((15, j))], lambda e: e.sqrt(out=stq[:, 2:3], in_=stq[:, 0:1]))
                    hRc4 = P.add("v", [hSq], lambda e: e.reciprocal(out=stq[:, 3:4], in_=stq[:, 2:3]))
                    hAp = P.add("v", [hRc4], (lambda e, j=j:
                                e.tensor_scalar(out=h1_s[j][:], in0=h1_s[j][:], scalar1=stq[:, 3:4],
                                                scalar2=None, op0=TS.mult, op1=TS.bypass)), drain_first=True)
                    hEvq2 = None
                    for cc in range(0, 64, 2):
                        def xt2(e, cc=cc, j=j):
                            last = None
                            for c in (cc, cc + 1):
                                last = e.matmul(pb[5 + c % 2][:, 0:128], h1_s[j][:, c * 128:(c + 1) * 128],
                                                ident[:], start=True, stop=True, skip_group_check=True)
                            return last
                        hT2 = P.add("t", [hAp if cc == 0 else None, hTq2, hEvq2], xt2)
                        hTq2 = hT2

                        def xev(e, cc=cc, j=j):
                            for c in (cc, cc + 1):
                                e.tensor_copy(out=h1_s[j][:, c * 128:(c + 1) * 128], in_=pb[5 + c % 2][:, 0:128])
                            return None
                        hEvq2 = P.add("v", [hT2], xev)
                    hRms[j] = hEvq2
                hL3 = P.barrier()

                hQld2 = None
                hMM2 = {}
                hEvG = {}
                lastp = {0: lastf[0], 1: lastf[1]}
                for s4 in range(4):
                    for kh in range(8):
                        hQld2 = P.dma("g", [hQld2, hL3 if s4 == 0 and kh == 0 else None, lastp[kh % 2]],
                                      (lambda e, s4=s4, kh=kh, b=ibuf[kh % 2]:
                                       [e.dma_start(out=b.ap().rearrange("p (c n) -> p c n", n=512),
                                                    in_=qproj_scr.rearrange("(c p) n -> p c n", p=128)
                                                    [:, kh * 8:(kh + 1) * 8, s4 * 512:(s4 + 1) * 512])]))

                        def mmpj(e, s4=s4, kh=kh):
                            last = None
                            for j in range(4):
                                for cc in range(8):
                                    c = kh * 8 + cc
                                    last = e.matmul(pb[j][:, 0:512],
                                                    h1_s[j][:, c * 128:(c + 1) * 128],
                                                    ibuf[kh % 2].ap().bitcast(BF16)[:, cc * 512:(cc + 1) * 512],
                                                    start=(c == 0), stop=(c == 63))
                            return last
                        hMMg = P.add("t", [hQld2] + [hEvG.get((s4 - 1, j)) for j in range(4)]
                                     + ([hL3] if s4 == 0 and kh == 0 else []), mmpj)
                        hMM2[(s4, kh)] = hMMg
                        lastp[kh % 2] = hMMg
                    for j in range(4):
                        def evg(e, s4=s4, j=j):
                            return e.activation(out=sg1[j][:, s4 * 512:(s4 + 1) * 512],
                                                in_=pb[j][:, 0:512], func=AF.Silu)
                        hEvG[(s4, j)] = P.add("s", [hMM2[(s4, 7)], hEvG.get((s4 - 1, j))], evg)
                hL4 = P.barrier()
                ibufcm[1].__exit__(None, None, None)
                ibufcm[0].__exit__(None, None, None)
                for j in range(4):
                    h1cm[j].__exit__(None, None, None)

                gt = [SBt(f"gt{j}", [128, 1024]) for j in range(4)]
                gT = SBt("gT", [128, 8 * TW], BF16)
                gmx = SBt("gmx", [128, 16])
                wgbcm = []
                wgb = []
                for i in range(2):
                    cm, tt = RT(f"wgb{i}", [128, 8 * 512], BF16)
                    wgbcm.append(cm)
                    wgb.append(tt)
                hWgd = None
                hGev = {}
                lastg = {0: None, 1: None}
                for sg4 in range(2):
                    for khg in range(2):
                        hWgd = P.dma("g", [hWgd, hL4 if sg4 == 0 and khg == 0 else None, lastg[khg]],
                                     (lambda e, sg4=sg4, khg=khg, b=wgb[khg]:
                                      [e.dma_start(out=b.ap().rearrange("p (c n) -> p c n", n=512),
                                                   in_=Wg1.rearrange("(c p) n -> p c n", p=128)
                                                   [:, khg * 8:(khg + 1) * 8, sg4 * 512:(sg4 + 1) * 512])]))

                        def gmm(e, sg4=sg4, khg=khg):
                            last = None
                            for j in range(4):
                                for cc in range(8):
                                    c = khg * 8 + cc
                                    last = e.matmul(pb[j][:, 0:512],
                                                    xnT[:, c * TW + j * 128:c * TW + (j + 1) * 128],
                                                    wgb[khg][:, cc * 512:(cc + 1) * 512],
                                                    start=(c == 0), stop=(c == 15))
                            return last
                        hGm = P.add("t", [hWgd, hL4 if sg4 == 0 and khg == 0 else None, hGev.get(sg4 - 1)], gmm)
                        lastg[khg] = hGm

                    def gev(e, sg4=sg4):
                        for j in range(4):
                            e.tensor_scalar(out=gt[j][:, sg4 * 512:(sg4 + 1) * 512], in0=pb[j][:, 0:512],
                                            scalar1=SC1, scalar2=None, op0=TS.mult, op1=TS.bypass)
                        return None
                    hGev[sg4] = P.add("v", [hGm], gev)
                hL5 = P.barrier()
                wgbcm[1].__exit__(None, None, None)
                wgbcm[0].__exit__(None, None, None)
                hSm2 = {}
                hTg = None
                hEvTg = None
                for j in range(4):
                    def gsmax(e, j=j):
                        e.tensor_reduce(out=gmx[:, 0:16], in_=gt[j].ap().rearrange("p (g k) -> p g k", k=64),
                                        axis=AX.X, op=TS.max, negate=True)
                        e.drain()
                        e.tensor_tensor(out=gt[j].ap().rearrange("p (g k) -> p g k", k=64),
                                        in0=gt[j].ap().rearrange("p (g k) -> p g k", k=64),
                                        in1=gmx.ap()[:, 0:16].unsqueeze(-1).broadcast_to((128, 16, 64)),
                                        op=TS.add)
                        return None
                    hG1 = P.add("v", [hL5 if j == 0 else None, hSm2.get(j - 1)], gsmax, drain_first=True)
                    hG2 = P.add("s", [hG1], (lambda e, j=j:
                                e.activation(out=gt[j][:], in_=gt[j][:], func=AF.Exp)))

                    def gnorm(e, j=j):
                        e.tensor_reduce(out=gmx[:, 0:16], in_=gt[j].ap().rearrange("p (g k) -> p g k", k=64),
                                        axis=AX.X, op=TS.add)
                        e.drain()
                        e.reciprocal(out=gmx[:, 0:16], in_=gmx[:, 0:16])
                        e.drain()
                        e.tensor_tensor(out=gt[j].ap().rearrange("p (g k) -> p g k", k=64),
                                        in0=gt[j].ap().rearrange("p (g k) -> p g k", k=64),
                                        in1=gmx.ap()[:, 0:16].unsqueeze(-1).broadcast_to((128, 16, 64)),
                                        op=TS.mult)
                        return None
                    hG3 = P.add("v", [hG2], gnorm, drain_first=True)
                    hG4 = P.add("s", [hG3], (lambda e, j=j:
                                e.activation(out=xnb2[:, 0:1024], in_=gt[j][:], func=AF.Identity)))
                    for cc in range(0, 8, 2):
                        def gtr(e, cc=cc):
                            last = None
                            for c in (cc, cc + 1):
                                last = e.matmul(pb[5 + c % 2][:, 0:128], xnb2[:, c * 128:(c + 1) * 128],
                                                ident[:], start=True, stop=True, skip_group_check=True)
                            return last
                        hT7 = P.add("t", [hG4 if cc == 0 else None, hTg, hEvTg], gtr)
                        hTg = hT7

                        def gtev(e, cc=cc, j=j):
                            for c in (cc, cc + 1):
                                e.tensor_copy(out=gT[:, c * TW + j * 128:c * TW + (j + 1) * 128],
                                              in_=pb[5 + c % 2][:, 0:128])
                            return None
                        hEvTg = P.add("v", [hT7], gtev)
                    hSm2[j] = hEvTg
                hL6 = P.barrier()

                wtd = SBt("wtd", [128, 16 * TW], BF16)
                hWt2 = {}
                hWe = {}
                worder = [h for h in range(16) if h % 2 == 0] + [h for h in range(16) if h % 2 == 1]
                last_bank_user = {}
                for i, h in enumerate(worder):
                    bank_dep = hWe.get(last_bank_user.get(h % 8))

                    def wmm(e, h=h, i=i):
                        if i == 8:
                            e.drain()
                        return e.matmul(
                            pb[h % 8][:, 0:TW],
                            attn_s[(h % 2) * 64:(h % 2) * 64 + 64, (h // 2) * 128:(h // 2) * 128 + 128],
                            gT[(h % 2) * 64:(h % 2) * 64 + 64, (h // 2) * TW:(h // 2) * TW + TW],
                            start=True, stop=True, skip_group_check=True)
                    hWt2[h] = P.add("t", [hL6 if i == 0 else None, bank_dep], wmm)
                    last_bank_user[h % 8] = h
                    hWe[h] = P.add("v", [hWt2[h]], (lambda e, h=h:
                                   e.tensor_copy(out=wtd[:, h * TW:(h + 1) * TW], in_=pb[h % 8][:, 0:TW])))
                hL7 = P.barrier()

                s1b = SBt("s1b", [128, D])
                wpbcm = []
                wpb = []
                for i in range(2):
                    cm, tt = RT(f"wpb{i}", [128, 8 * 512], BF16)
                    wpbcm.append(cm)
                    wpb.append(tt)
                s1 = [SBt(f"s1{j}", [128, D]) for j in range(4)]
                hWpd = None
                hWpe = {}
                lastw = {0: None, 1: None}
                for s4 in range(4):
                    for khp in range(2):
                        hWpd = P.dma("g", [hWpd, hL7 if s4 == 0 and khp == 0 else None, lastw[khp]],
                                     (lambda e, s4=s4, khp=khp, b=wpb[khp]:
                                      [e.dma_start(out=b.ap().rearrange("p (c n) -> p c n", n=512),
                                                   in_=Wp1.rearrange("(c p) n -> p c n", p=128)
                                                   [:, khp * 8:(khp + 1) * 8, s4 * 512:(s4 + 1) * 512])]))

                        def wpmm(e, s4=s4, khp=khp):
                            last = None
                            for j in range(4):
                                for cc in range(8):
                                    c = khp * 8 + cc
                                    last = e.matmul(pb[j][:, 0:512],
                                                    wtd[:, c * TW + j * 128:c * TW + (j + 1) * 128],
                                                    wpb[khp][:, cc * 512:(cc + 1) * 512],
                                                    start=(c == 0), stop=(c == 15))
                            return last
                        hWp3 = P.add("t", [hWpd, hL7 if s4 == 0 and khp == 0 else None,
                                           hWpe.get(s4 - 1)], wpmm)
                        lastw[khp] = hWp3

                    def wpev(e, s4=s4):
                        for j in range(4):
                            e.tensor_copy(out=s1[j][:, s4 * 512:(s4 + 1) * 512], in_=pb[j][:, 0:512])
                        return None
                    hWpe[s4] = P.add("v", [hWp3], wpev)
                hL8 = P.barrier()
                wpbcm[1].__exit__(None, None, None)
                wpbcm[0].__exit__(None, None, None)

                hCd = None
                hCb = None
                for j in range(4):
                    hDx = P.dma("d", [hL8, hCd, hCb],
                                (lambda e, j=j:
                                 [e.dma_start(out=xb[:], in_=x1_scr[t * 512 + j * 128:t * 512 + (j + 1) * 128, :])]))

                    def comb(e, j=j):
                        e.tensor_tensor(out=sg1[j][:], in0=sg1[j][:], in1=s1[j][:], op=TS.mult)
                        e.drain()
                        e.tensor_tensor(out=sg1[j][:], in0=sg1[j][:], in1=xb[:], op=TS.add)
                        return None
                    hCb = P.add("v", [hDx, hL8], comb)
                    hCd = P.dma("d", [hCb], (lambda e, j=j:
                                [e.dma_start(out=out[t * 512 + j * 128:t * 512 + (j + 1) * 128, :],
                                             in_=sg1[j][:])]))
            return [hCd]


        if PHASES >= 5:
            for t in range(NTILE):
                hP4 = emit_l1_tile(f"u{t}", t, hP4)

        P.add("g", hP4 if isinstance(hP4, list) else [hP4], lambda e: e.nop(), inc=0)

        block = st.enter_context(nc.Block())
        P.play(block)
    return nc


def _f32c(a):
    return np.ascontiguousarray(a, dtype=np.float32)


def make_constants():
    half = 64
    inv_freq = 1.0 / (10000.0 ** (np.arange(0, half, dtype=np.float64) * 2.0 / 128.0))
    ang = np.arange(64, dtype=np.float64)[:, None] * inv_freq[None, :]
    cos = np.cos(ang).astype(np.float32)
    sin = np.sin(ang).astype(np.float32)
    cos16 = np.tile(cos, (1, 16))
    sin16 = np.tile(sin, (1, 16))
    maskadd = np.where(np.arange(64)[None, :] <= np.arange(64)[:, None], 0.0, NEG).astype(np.float32)
    ident = np.eye(128, dtype=np.float32)
    gsum = np.zeros((128, 128), np.float32)
    gsum[0:64, 0] = 1.0
    gsum[64:128, 1] = 1.0
    grep = np.zeros((128, 128), np.float32)
    grep[0, 0:64] = 1.0
    grep[1, 64:128] = 1.0
    ones1 = np.ones((1, 128), np.float32)
    onesc = np.ones((128, 1), np.float32)
    return dict(c_cos=cos16, c_sin=sin16, c_mask=maskadd, c_ident=ident,
                c_gsum=gsum, c_grep=grep, c_ones1=ones1, c_onesc=onesc)


def make_in_map(inputs, core):
    b, half = core // 2, core % 2
    x = inputs["L0_attn_hidden_states"]
    lq = inputs["L1_attn_latent_queries"]
    lqT = np.ascontiguousarray(np.transpose(lq[0], (1, 2, 0)).reshape(16, 128, 64)
                               .transpose(1, 0, 2).reshape(128, 16 * 64))
    m = dict(
        x_seq=_f32c(x[b]),
        x_own=_f32c(x[b, half * OWN:(half + 1) * OWN]),
        Wq=_f32c(inputs["L0_attn_Wq"]), Wk=_f32c(inputs["L0_attn_Wk"]),
        Wv=_f32c(inputs["L0_attn_Wv"]), Wo=_f32c(inputs["L0_attn_Wo"]),
        bo_row=_f32c(inputs["L0_attn_bo"][None, :]),
        w1T=_f32c(np.transpose(inputs["L0_ffn_w1"])),
        w2T=_f32c(np.transpose(inputs["L0_ffn_w2"])),
        b1_row=_f32c(inputs["L0_ffn_b1"][None, :]),
        b2_row=_f32c(inputs["L0_ffn_b2"][None, :]),
        g_rep=_f32c(np.broadcast_to(inputs["L0_ffn_ln_g"][None, :], (128, HD))),
        b_rep=_f32c(np.broadcast_to(inputs["L0_ffn_ln_b"][None, :], (128, HD))),
        lqT=_f32c(lqT),
        Wk1=_f32c(inputs["L1_attn_Wk"]), Wv1=_f32c(inputs["L1_attn_Wv"]),
        Wg1=_f32c(inputs["L1_attn_Wg"]), Wp1=_f32c(inputs["L1_attn_Wp"]),
        WfcT=_f32c(np.transpose(inputs["L1_ffn_W_fc"])),
        WprojT=_f32c(np.transpose(inputs["L1_ffn_W_proj"])),
    )
    m.update(make_constants())
    return m


def kernel(**inputs):
    nc = build_program()
    in_maps = [make_in_map(inputs, c) for c in range(8)]
    res = run_bass_kernel_spmd(nc, in_maps, core_ids=list(range(8)))
    halves = [r["out"] for r in res.results]
    seqs = [np.concatenate([halves[2 * b], halves[2 * b + 1]], axis=0) for b in range(4)]
    return np.stack(seqs, axis=0)


# version 09 (final)
# speedup vs baseline: 1.0172x; 1.0172x over previous
import contextlib
import numpy as np
import concourse.bass as bass
import concourse.mybir as mybir
from concourse.alu_op_type import AluOpType
from concourse.bass_utils import run_bass_kernel_spmd

F32 = mybir.dt.float32
BF16 = mybir.dt.bfloat16
I16 = mybir.dt.int16
AF = mybir.ActivationFunctionType
AX = mybir.AxisListType
TS = AluOpType

D = 2048
T = 4096
OWN = 2048
HD = 8192
NTILE = 4
NSUB = 32
NEG = -1.0e9
SC1 = float(1.0 / np.sqrt(128.0))
M32 = 1.5 * 2.0 ** 23
M64 = 1.5 * 2.0 ** 52
E32 = float(np.finfo(np.float32).eps)
E64 = float(np.finfo(np.float64).eps)

ENG = {"t": "tensor", "v": "vector", "s": "scalar", "d": "sync", "g": "gpsimd",
       "e": "scalar"}

PHASES = 99
DEBUG_TAPS = False
P2TILES = 99
DRAINREP = False
PUMPL1 = 7
PUMPSET = 255
L0STOP = 'Z'


class Prog:

    def __init__(self, nc, stack):
        self.nc = nc
        self.sem = {e: stack.enter_context(nc.semaphore("sem_" + e)) for e in ENG}
        self.cnt = {e: 0 for e in ENG}
        self.ops = {e: [] for e in ENG}
        self.seq = 0

    def add(self, eng, deps, fn, inc=1, self_inc=False, drain_first=False):
        flat = []

        def _fl(d):
            if d is None:
                return
            if isinstance(d, list):
                for x in d:
                    _fl(x)
            else:
                flat.append(d)
        _fl(deps)
        deps = flat
        self.ops[eng].append((self.seq, eng, deps, fn, inc, self_inc, drain_first))
        self.seq += 1
        self.cnt[eng] += inc
        return (eng, self.cnt[eng])

    def dma(self, eng, deps, fn, n=1):
        start_cnt = self.cnt[eng]

        def body(e):
            if start_cnt > 0:
                e.wait_ge(self.sem[eng], start_cnt)
            insts = fn(e)
            assert len(insts) == n, f"dma group: expected {n}, got {len(insts)}"
            for i in insts:
                i.then_inc(self.sem[eng], 16)
            return "self"
        return self.add(eng, deps, body, inc=16 * n, self_inc=True)

    def barrier(self):
        return [(e, self.cnt[e]) for e in ENG if self.cnt[e] > 0]

    def play(self, block):
        sem = self.sem
        phys_keys = {}
        for e in ENG:
            phys_keys.setdefault(ENG[e], []).append(e)
        for phys, keys in phys_keys.items():
            ops = sorted((op for k in keys for op in self.ops[k]), key=lambda t: t[0])
            if not ops:
                continue

            def make_body(phys, keys, ops):
                def body(eng):
                    water = {}
                    for _seq, e, deps, fn, inc, self_inc, drain_first in ops:
                        drain = drain_first
                        for de, dv in deps:
                            if de == e:
                                drain = True
                            elif water.get(de, 0) < dv:
                                eng.wait_ge(sem[de], dv)
                                water[de] = dv
                        if drain:
                            eng.drain()
                        last = fn(eng)
                        if not self_inc and inc > 0:
                            if last is None:
                                eng.drain()
                                last = eng.nop()
                            last.then_inc(sem[e], inc)
                return body
            getattr(block, phys)(make_body(phys, keys, ops))


def build_program():
    nc = bass.Bass("TRN2", target_bir_lowering=False)
    dr = lambda n, s, dt=F32, kind="ExternalInput": nc.dram_tensor(n, s, dt, kind=kind).ap()

    x_seq = dr("x_seq", [T, D])
    x_own = dr("x_own", [OWN, D])
    Wq = dr("Wq", [D, D]); Wk = dr("Wk", [D, D]); Wv = dr("Wv", [D, D]); Wo = dr("Wo", [D, D])
    bo_row = dr("bo_row", [1, D])
    w1T = dr("w1T", [D, HD]); w2T = dr("w2T", [HD, D])
    b1_row = dr("b1_row", [1, HD]); b2_row = dr("b2_row", [1, D])
    g_rep = dr("g_rep", [128, HD]); b_rep = dr("b_rep", [128, HD])
    lqT = dr("lqT", [128, 16 * 64])
    Wk1 = dr("Wk1", [D, D]); Wv1 = dr("Wv1", [D, D])
    Wg1 = dr("Wg1", [D, 1024]); Wp1 = dr("Wp1", [D, D])
    WfcT = dr("WfcT", [D, HD]); WprojT = dr("WprojT", [HD, D])
    c_ident = dr("c_ident", [128, 128])
    c_gsum = dr("c_gsum", [128, 128])
    c_grep = dr("c_grep", [128, 128])
    c_ones1 = dr("c_ones1", [1, 128])
    c_onesc = dr("c_onesc", [128, 1])
    c_cos = dr("c_cos", [64, 16 * 64]); c_sin = dr("c_sin", [64, 16 * 64])
    c_mask = dr("c_mask", [64, 64])
    out = dr("out", [OWN, D], kind="ExternalOutput")

    x1_scr = dr("x1_scr", [OWN, D], kind="Internal")
    x1d_scr = dr("x1d_scr", [64, D], kind="Internal")
    sg_scr = dr("sg_scr", [OWN, D], kind="Internal")
    taps = [nc.dram_tensor(f"tap{i}", [128, HD], F32, kind="Internal").ap() for i in range(4)] if DEBUG_TAPS else None
    sgd_scr = dr("sgd_scr", [64, D], kind="Internal")
    qw1_scr = dr("qw1_scr", [D, HD], BF16, kind="Internal")
    qw2_scr = dr("qw2_scr", [HD, D], BF16, kind="Internal")
    qfc_scr = dr("qfc_scr", [D, HD], I16, kind="Internal")
    qproj_scr = dr("qproj_scr", [HD, D], I16, kind="Internal")
    sc_scr = dr("sc_scr", [4, 2048], BF16, kind="Internal")
    wcol_scr = dr("wcol_scr", [2, 128], kind="Internal")
    wtmp_scr = dr("wtmp_scr", [2, 2], kind="Internal")

    st = contextlib.ExitStack()
    with st:
        SB = lambda n, s, dt=F32: st.enter_context(nc.sbuf_tensor(n, s, dt))
        ident = SB("ident", [128, 128], BF16)
        identf = SB("identf", [128, 128])
        gsum = SB("gsum", [128, 128], BF16)
        grepl = SB("grepl", [128, 128], BF16)
        ones1f = SB("ones1f", [1, 128])
        ones1b = SB("ones1b", [1, 128], BF16)
        onescf = SB("onescf", [128, 1])
        bo16 = SB("bo16", [1, D], BF16)
        mask_s = SB("mask_s", [64, 64])
        ind = SB("ind", [128, 1]); magic = SB("magic", [128, 1]); epsd = SB("epsd", [128, 1])
        kv_ext = SB("kv_ext", [128, 16 * 65], BF16)
        attn_s = SB("attn_s", [128, 8 * 128], BF16)
        wmrow = SB("wmrow", [128, 4])
        pb = [st.enter_context(nc.psum_tensor(f"pb{i}", [128, 512], F32)) for i in range(8)]

        P = Prog(nc, st)

        hD0 = P.dma("d", [], lambda e: [
            e.dma_start(out=mask_s[:], in_=c_mask),
            e.dma_start(out=ones1f[:], in_=c_ones1),
            e.dma_start(out=onescf[:], in_=c_onesc),
            e.dma_start(out=identf[:], in_=c_ident)], n=4)
        hG0 = P.dma("g", [], lambda e: [
            e.dma_start(out=ident[:], in_=c_ident),
            e.dma_start(out=gsum[:], in_=c_gsum),
            e.dma_start(out=grepl[:], in_=c_grep),
            e.dma_start(out=ones1b[:], in_=c_ones1),
            e.dma_start(out=bo16[:], in_=bo_row)], n=5)

        def p0v(e):
            e.memset(ind[:], 1.0)
            e.drain()
            e.tensor_scalar(out=ind[:], in0=ind[:], scalar1=float(2.0 ** -25), scalar2=-1.0,
                            op0=TS.add, op1=TS.add)
            e.drain()
            e.tensor_scalar(out=ind[:], in0=ind[:], scalar1=float(2.0 ** 25), scalar2=None,
                            op0=TS.mult, op1=TS.bypass)
            e.drain()
            e.tensor_scalar(out=magic[:], in0=ind[:], scalar1=float(M64 - M32),
                            scalar2=float(M32), op0=TS.mult, op1=TS.add)
            e.tensor_scalar(out=epsd[:], in0=ind[:], scalar1=float(E64 - E32),
                            scalar2=float(E32), op0=TS.mult, op1=TS.add)
            return None
        hV0 = P.add("v", [], p0v)

        def emit_ln(src, rows, bn6, bn2, statc, deps, war_deps=()):
            def stats(e):
                for c in range(4):
                    e.bn_stats(out=bn6[0:rows, 6 * c:6 * c + 6], in_=src[0:rows, 512 * c:512 * (c + 1)])
                e.drain()
                e.bn_aggr(out=bn2[0:rows, :], in_=bn6[0:rows, 0:24])
                e.drain()
                e.tensor_scalar(out=statc[0:rows, 0:1], in0=bn2[0:rows, 1:2], scalar1=1e-5,
                                scalar2=None, op0=TS.add, op1=TS.bypass)
                return None
            hSt = P.add("v", list(deps) + list(war_deps), stats, drain_first=True)
            hSq = P.add("s", [hSt], lambda e: e.sqrt(out=statc[0:rows, 2:3], in_=statc[0:rows, 0:1]))

            def recip(e):
                e.reciprocal(out=statc[0:rows, 3:4], in_=statc[0:rows, 2:3])
                e.drain()
                e.tensor_tensor(out=statc[0:rows, 1:2], in0=bn2[0:rows, 0:1], in1=statc[0:rows, 3:4], op=TS.mult)
                e.drain()
                e.tensor_scalar(out=statc[0:rows, 1:2], in0=statc[0:rows, 1:2], scalar1=-1.0, scalar2=None,
                                op0=TS.mult, op1=TS.bypass)
                return None
            return P.add("v", [hSq], recip)

        hP1 = P.barrier()
        hWm1 = hWm2 = None
        if PHASES >= 1:
            qs_ = contextlib.ExitStack()
            SBq = lambda n, s, dt=F32: qs_.enter_context(nc.sbuf_tensor(n, s, dt))
            wbuf = [SBq(f"q_wbuf{i}", [128, 2048]) for i in range(2)]
            wqb2 = [SBq(f"q_wqb{i}", [128, 2048], BF16) for i in range(2)]
            wcol = SBq("q_wcol", [128, 128])
            wtmp = SBq("q_wtmp", [1, 4])
            wrow = SBq("q_wrow", [1, 128])
            negm = SBq("q_negm", [128, 1])
            l1b = [SBq(f"q_l1b{i}", [128, 2048], BF16) for i in range(2)]
            wab = [SBq(f"q_wab{i}", [128, 2048], BF16) for i in range(2)]
            sc16q = [SBq(f"q_sc{i}", [128, 2048], BF16) for i in range(2)]
            screp = [SBq(f"q_srp{i}", [128, 2048], BF16) for i in range(2)]
            cmpa = SBq("q_cmpa", [128, 2048])
            cmpb = SBq("q_cmpb", [128, 2048])
            wsgn = SBq("q_wsgn", [128, 2048], BF16)
            qoutb = SBq("q_qout", [128, 2048], BF16)

            def qzero(e):
                e.memset(wcol[:], 0.0)
                e.drain()
                e.memset(sc16q[0][:], 0.0)
                e.drain()
                e.memset(sc16q[1][:], 0.0)
                return None
            hQz = P.add("v", [hV0], qzero)
            hNm = P.add("v", [hQz], lambda e: e.tensor_scalar(
                out=negm[:], in0=magic[:], scalar1=-1.0, scalar2=None,
                op0=TS.mult, op1=TS.bypass))

            QS = {'wl': [None, None], 'wq': [None, None], 'acc': [hQz, hQz],
                  'b7g': None, 'b7r': None, 'l1l': [None, None], 'wabl': [None, None],
                  'scl': [hQz, hQz], 'srpl': [None, None],
                  'cmpa': None, 'cmpb': None, 'wsg': None, 'qo': None,
                  'si': 0, 'l1i': 0}

            def l0p1_item(wi, wt, rb, cs, q):
                slot = QS['si'] % 2; QS['si'] += 1
                b = wbuf[slot]
                hLd = P.dma(q, [QS['wl'][slot]],
                            (lambda e, b=b, wt=wt, rb=rb, cs=cs:
                             [e.dma_start(out=b[:], in_=wt[rb * 128:(rb + 1) * 128, cs:cs + 2048])]))

                def absred(e, b=b, wi=wi):
                    e.tensor_reduce(out=wcol[:, 2 * wi + 1:2 * wi + 2], in_=b[:],
                                    axis=AX.X, op=TS.add, apply_absolute_value=True)
                    e.drain()
                    e.tensor_tensor(out=wcol[:, 2 * wi:2 * wi + 1], in0=wcol[:, 2 * wi:2 * wi + 1],
                                    in1=wcol[:, 2 * wi + 1:2 * wi + 2], op=TS.add)
                    return None
                hAb = P.add("v", [hLd, QS['acc'][wi]], absred, drain_first=True)
                QS['acc'][wi] = hAb
                QS['wl'][slot] = hAb

            def wmean_item(wi):
                hS1 = P.dma("e", [QS['acc'][wi]],
                            (lambda e, wi=wi:
                             [e.dma_start(out=wcol_scr[wi:wi + 1, 0:128].rearrange("one p -> p one"),
                                          in_=wcol[:, 2 * wi:2 * wi + 1])]))
                hL1 = P.dma("e", [hS1, QS.get('wmrd')],
                            (lambda e, wi=wi:
                             [e.dma_start(out=wrow[:], in_=wcol_scr[wi:wi + 1, 0:128])]))

                def wmcalc(e, wi=wi):
                    e.tensor_reduce(out=wtmp[0:1, 2 * wi:2 * wi + 1], in_=wrow[0:1, :],
                                    axis=AX.X, op=TS.add)
                    e.drain()
                    e.tensor_scalar(out=wtmp[0:1, 2 * wi:2 * wi + 1], in0=wtmp[0:1, 2 * wi:2 * wi + 1],
                                    scalar1=float(1.0 / (D * HD)), scalar2=1e-5,
                                    op0=TS.mult, op1=TS.max)
                    e.drain()
                    e.reciprocal(out=wtmp[0:1, 2 * wi + 1:2 * wi + 2], in_=wtmp[0:1, 2 * wi:2 * wi + 1])
                    return None
                hWmv = P.add("v", [hL1], wmcalc, drain_first=True)
                QS['wmrd'] = hWmv
                hS2 = P.dma("e", [hWmv], (lambda e, wi=wi:
                            [e.dma_start(out=wtmp_scr[wi:wi + 1, :], in_=wtmp[0:1, 2 * wi:2 * wi + 2])]))
                hWmr = P.dma("e", [hS2], (lambda e, wi=wi:
                             [e.dma_start(out=wmrow[:, 2 * wi:2 * wi + 2],
                                          in_=wtmp_scr[wi:wi + 1, :].broadcast_to([128, 2]))]))
                return hWmr

            def l0p2_item(wi, wt, scr, rb, cs, q):
                slot = QS['si'] % 2; QS['si'] += 1
                b = wbuf[slot]
                hLd = P.dma(q, [QS['wl'][slot]],
                            (lambda e, b=b, wt=wt, rb=rb, cs=cs:
                             [e.dma_start(out=b[:], in_=wt[rb * 128:(rb + 1) * 128, cs:cs + 2048])]))

                def mround(e, b=b, wi=wi):
                    e.activation(out=b[:], in_=b[:], func=AF.Identity,
                                 bias=magic[:, 0:1], scale=wmrow[:, 2 * wi + 1:2 * wi + 2])
                    e.drain()
                    e.activation(out=b[:], in_=b[:], func=AF.Identity, bias=negm[:, 0:1])
                    return None
                hMr = P.add("s", [hLd, QS['wm1'] if wi == 0 else QS['wm2'], hNm], mround)

                def clip1(e, b=b, slot=slot):
                    e.tensor_scalar(out=wqb2[slot][:], in0=b[:], scalar1=-1.0, scalar2=1.0,
                                    op0=TS.max, op1=TS.min)
                    return None
                hCl = P.add("v", [hMr, QS['wq'][slot]], clip1)
                QS['wl'][slot] = hCl
                hSt = P.dma(q, [hCl], (lambda e, slot=slot, scr=scr, rb=rb, cs=cs:
                                       [e.dma_start(out=scr[rb * 128:(rb + 1) * 128, cs:cs + 2048],
                                                    in_=wqb2[slot][:])]))
                QS['wq'][slot] = hSt

            def l1_item(kind, u):
                slot = QS['l1i'] % 2; QS['l1i'] += 1
                if kind == 0:
                    wt, scr, rb, cs = WfcT, qfc_scr, u // 4, (u % 4) * 2048
                else:
                    wt, scr, rb, cs = WprojT, qproj_scr, u, 0
                b = l1b[slot]
                hLd = P.dma("g", [QS['l1l'][slot]],
                            (lambda e, b=b, wt=wt, rb=rb, cs=cs:
                             [e.dma_start(out=b[:], in_=wt[rb * 128:(rb + 1) * 128, cs:cs + 2048])]))
                if PUMPL1 < 2:
                    QS['l1l'][slot] = hLd
                    return
                hAbs = P.add("s", [hLd, QS['wabl'][slot], hG0],
                             (lambda e, b=b, slot=slot: e.activation(out=wab[slot][:], in_=b[:],
                                                                     func=AF.Abs)))
                if PUMPL1 < 3:
                    QS['l1l'][slot] = hAbs
                    QS['wabl'][slot] = hAbs
                    return
                hQm = None
                hQms = []
                for sl in range(8):
                    hMm = P.add("t", [hAbs if sl == 0 else None, QS['b7g'] if sl == 0 else hQm],
                                (lambda e, slot=slot, sl=sl:
                                 e.matmul(pb[7][0:128, 0:256], gsum[:],
                                          wab[slot][:, sl * 256:(sl + 1) * 256],
                                          start=True, stop=True, skip_group_check=True)))
                    hQm = P.add("v", [hMm, QS['scl'][slot] if sl == 0 else None],
                                (lambda e, slot=slot, sl=sl:
                                 e.tensor_scalar(out=sc16q[slot][0:2, sl * 256:(sl + 1) * 256],
                                                 in0=pb[7][0:2, 0:256], scalar1=float(1.0 / 64.0),
                                                 scalar2=1e-8, op0=TS.mult, op1=TS.max)))
                    hQms.append(hQm)
                if PUMPL1 < 4:
                    QS['b7g'] = hQm
                    QS['scl'][slot] = hQm
                    QS['l1l'][slot] = hAbs
                    QS['wabl'][slot] = hQm
                    return
                nrep = 1 if PUMPL1 == 41 else 8
                rcol = 0 if PUMPL1 == 43 else 256
                hEv = None
                for sl in range(nrep):
                    hRp = P.add("t", [hQms[sl], hQms[-1] if PUMPL1 in (41, 43) else None, hEv,
                                      [QS['b7r'], QS['srpl'][slot]] if sl == 0 else None],
                                (lambda e, slot=slot, sl=sl, rcol=rcol:
                                 e.matmul(pb[7][0:128, rcol:rcol + 256], grepl[:],
                                          sc16q[slot][0:128, sl * 256:(sl + 1) * 256],
                                          start=True, stop=True, skip_group_check=True)),
                                drain_first=(DRAINREP and sl == 0))

                    if PUMPL1 == 35:
                        hEv = P.add("v", [hRp], (lambda e, slot=slot, sl=sl:
                                     e.tensor_copy(out=screp[slot][0:2, sl * 256:sl * 256 + 2],
                                                   in_=pb[7][0:2, 256:258])))
                    else:
                        hEv = P.add("v", [hRp], (lambda e, slot=slot, sl=sl:
                                     e.tensor_copy(out=screp[slot][:, sl * 256:(sl + 1) * 256],
                                                   in_=pb[7][0:128, 256:512])))
                QS['b7g'] = hQm
                QS['b7r'] = hEv
                QS['scl'][slot] = hEv
                if PUMPL1 < 5 or PUMPL1 == 35:
                    QS['l1l'][slot] = hAbs
                    QS['wabl'][slot] = hQm
                    QS['srpl'][slot] = hEv
                    return

                def sacts(e, slot=slot):
                    e.tensor_scalar(out=cmpa[:], in0=screp[slot][:], scalar1=0.5,
                                    scalar2=None, op0=TS.mult, op1=TS.bypass)
                    e.drain()
                    e.tensor_scalar(out=cmpb[:], in0=screp[slot][:],
                                    scalar1=float(2.0 ** -9 + 2.0 ** -25),
                                    scalar2=None, op0=TS.mult, op1=TS.bypass)
                    return None
                hSa = P.add("v", [hEv, QS['cmpa'], QS['cmpb']], sacts, drain_first=True)

                def dsub(e, slot=slot):
                    e.tensor_tensor(out=cmpa[:], in0=wab[slot][:], in1=cmpa[:], op=TS.subtract)
                    e.drain()
                    e.tensor_tensor(out=cmpa[:], in0=cmpa[:], in1=cmpb[:], op=TS.subtract)
                    e.drain()
                    e.tensor_scalar(out=cmpa[:], in0=cmpa[:], scalar1=0.0,
                                    scalar2=None, op0=TS.max, op1=TS.bypass)
                    return None
                hDs = P.add("v", [hSa, hAbs], dsub, drain_first=True)
                QS['cmpb'] = hDs
                if PUMPL1 < 6:
                    QS['cmpa'] = hDs
                    QS['l1l'][slot] = hAbs
                    QS['wabl'][slot] = hDs
                    QS['srpl'][slot] = hSa
                    return

                def signs(e, b=b):
                    e.activation(out=cmpa[:], in_=cmpa[:], func=AF.Sign)
                    e.drain()
                    return e.activation(out=wsgn[:], in_=b[:], func=AF.Sign)
                hSg = P.add("s", [hDs, QS['wsg'], hLd], signs)
                QS['l1l'][slot] = hSg

                def qmul(e, slot=slot):
                    e.tensor_tensor(out=wab[slot][:], in0=cmpa[:], in1=wsgn[:], op=TS.mult)
                    e.drain()
                    e.tensor_tensor(out=qoutb[:], in0=wab[slot][:], in1=screp[slot][:], op=TS.mult)
                    return None
                hQl = P.add("v", [hSg, QS['qo'], hEv], qmul, drain_first=True)
                QS['cmpa'] = hQl
                QS['wsg'] = hQl
                QS['wabl'][slot] = hQl
                QS['srpl'][slot] = hQl
                if PUMPL1 < 7:
                    return
                hSo = P.dma("e", [hQl], (lambda e, scr=scr, rb=rb, cs=cs:
                                         [e.dma_start(out=scr[rb * 128:(rb + 1) * 128, cs:cs + 2048],
                                                      in_=qoutb.ap().bitcast(I16)[:])]))
                QS['qo'] = hSo

            pump_sched = []
            if PHASES >= 2:
                if PUMPSET & 1:
                    for u in range(64):
                        pump_sched.append((u // 4, lambda u=u: l0p1_item(0, w1T, u // 4, (u % 4) * 2048, "e")))
                if PUMPSET & 2:
                    for u in range(64):
                        pump_sched.append((u // 2, lambda u=u: l1_item(0, u)))
                if PUMPSET & 4:
                    for u in range(64):
                        pump_sched.append((2 + u // 2, lambda u=u: l0p1_item(1, w2T, u, 0, "d")))

                def do_wm1():
                    QS['wm1'] = wmean_item(0)
                if PUMPSET & 8:
                    pump_sched.append((16, do_wm1))
                if PUMPSET & 16:
                    for u in range(64):
                        pump_sched.append((17 + u // 2, lambda u=u:
                                           l0p2_item(0, w1T, qw1_scr, u // 4, (u % 4) * 2048, "e")))
                if PUMPSET & 32:
                    for u in range(64):
                        pump_sched.append((32 + u // 2, lambda u=u: l1_item(1, u)))

                def do_wm2():
                    QS['wm2'] = wmean_item(1)
                if PUMPSET & 64:
                    pump_sched.append((34, do_wm2))
                if PUMPSET & 128:
                    for u in range(64):
                        pump_sched.append((35 + u // 2, lambda u=u:
                                           l0p2_item(1, w2T, qw2_scr, u, 0, "d")))
            pump_sched.sort(key=lambda x: x[0])

            def pump(it_now):
                while pump_sched and pump_sched[0][0] <= it_now:
                    pump_sched.pop(0)[1]()

            with contextlib.ExitStack() as p1s:
                SB1 = lambda n, s, dt=F32: p1s.enter_context(nc.sbuf_tensor(n, s, dt))
                wk_s = SB1("wk_s", [128, 16 * 1024], BF16)
                wv_s = SB1("wv_s", [128, 16 * 1024], BF16)
                xt = [SB1(f"p1_xt{i}", [128, D]) for i in range(2)]
                xn1p = SB1("p1_xn", [128, D])
                xnb1p = SB1("p1_xnb", [128, D], BF16)
                xnT1p = SB1("p1_xnT", [128, 16 * 128], BF16)
                kt1p = SB1("p1_kt", [128, 16 * 64], BF16)
                vt1p = SB1("p1_vt", [128, 16 * 65], BF16)
                kvst = SB1("p1_kvst", [64, 16 * 65], BF16)
                bn6 = SB1("p1_bn6", [128, 24])
                bn2 = SB1("p1_bn2", [128, 2])
                statc = SB1("p1_stat", [128, 4])

                hVones = P.add("v", [hV0], lambda e: e.memset(
                    vt1p.ap().rearrange("p (h c) -> p h c", c=65)[:, :, 64:65], 1.0))

                hWld = None
                hXa = None
                hVxT = None
                hEvK = None
                hEvV = None
                hKVa = None
                hXa_hist = {}
                for pss in range(2):
                    def wload(e, pss=pss):
                        return [e.dma_start(out=wk_s.ap().rearrange("p (a b) -> p a b", b=1024),
                                            in_=Wk[:, pss * 1024:(pss + 1) * 1024]
                                            .rearrange("(a p) b -> p a b", p=128)),
                                e.dma_start(out=wv_s.ap().rearrange("p (a b) -> p a b", b=1024),
                                            in_=Wv[:, pss * 1024:(pss + 1) * 1024]
                                            .rearrange("(a p) b -> p a b", p=128))]
                    hWld = P.dma("g", [hG0 if pss == 0 else hKVa], wload, n=2)
                    for i in range(NSUB):
                        it = pss * NSUB + i
                        buf = xt[i % 2]
                        hDx = P.dma("d", [hD0, hXa_hist.get(it - 2)],
                                    (lambda e, i=i, buf=buf: [e.dma_start(out=buf[:], in_=x_seq[i * 128:(i + 1) * 128, :])]))
                        hRc = emit_ln(buf, 128, bn6, bn2, statc, [hDx], war_deps=[hXa])

                        def xapply(e, buf=buf):
                            e.activation(out=xn1p[:], in_=buf[:], func=AF.Identity,
                                         bias=statc[:, 1:2], scale=statc[:, 3:4])
                            e.drain()
                            return e.activation(out=xnb1p[:], in_=xn1p[:], func=AF.Identity)
                        hXa = P.add("s", [hRc, hDx, hVxT], xapply)
                        hXa_hist[it] = hXa

                        def trans(e):
                            last = None
                            for c in range(16):
                                last = e.matmul(pb[c // 4][:, (c % 4) * 128:(c % 4) * 128 + 128],
                                                xnb1p[:, c * 128:(c + 1) * 128], ident[:],
                                                start=True, stop=True, skip_group_check=True)
                            return last
                        hTt = P.add("t", [hXa, hEvK, hEvV, hG0], trans)

                        def evT(e):
                            for q in range(4):
                                e.tensor_copy(out=xnT1p[:, 512 * q:512 * (q + 1)], in_=pb[q][:, 0:512])
                            return None
                        hVxT = P.add("v", [hTt], evT)

                        def proj(e):
                            last = None
                            for c in range(16):
                                s0, s1 = (c == 0), (c == 15)
                                lhsT = xnT1p[:, c * 128:c * 128 + 128]
                                last = e.matmul(pb[0][:], lhsT, wk_s[:, c * 1024:c * 1024 + 512], start=s0, stop=s1)
                                last = e.matmul(pb[1][:], lhsT, wk_s[:, c * 1024 + 512:c * 1024 + 1024], start=s0, stop=s1)
                                last = e.matmul(pb[2][:], lhsT, wv_s[:, c * 1024:c * 1024 + 512], start=s0, stop=s1)
                                last = e.matmul(pb[3][:], lhsT, wv_s[:, c * 1024 + 512:c * 1024 + 1024], start=s0, stop=s1)
                            return last
                        hPr = P.add("t", [hVxT, hWld], proj)

                        def evK(e):
                            e.activation(out=kt1p[:, 0:512], in_=pb[0][:], func=AF.Relu)
                            e.drain()
                            return e.activation(out=kt1p[:, 512:1024], in_=pb[1][:], func=AF.Relu)
                        hEvK = P.add("s", [hPr, hKVa], evK)

                        def evV(e):
                            for q in range(2):
                                e.tensor_copy(
                                    out=vt1p.ap().rearrange("p (h c) -> p h c", c=65)[:, 8 * q:8 * (q + 1), 0:64],
                                    in_=pb[2 + q].ap().rearrange("p (h c) -> p h c", c=64))
                            return None
                        hEvV = P.add("v", [hPr, hVones, hKVa], evV)

                        def kvacc(e, i=i):
                            last = None
                            for h in range(16):
                                bank = pb[4 + h // 6]
                                o = (h % 6) * 85
                                last = e.matmul(bank[0:64, o:o + 65],
                                                kt1p[:, h * 64:(h + 1) * 64],
                                                vt1p[:, h * 65:(h + 1) * 65],
                                                start=(i == 0 and h % 6 == 0),
                                                stop=(i == NSUB - 1 and h in (5, 11, 15)))
                            return last
                        hKVa = P.add("t", [hEvK, hEvV], kvacc)
                        pump(pss * NSUB + i)

                    def evKV(e, pss=pss):
                        for h in range(16):
                            bank = pb[4 + h // 6]
                            o = (h % 6) * 85
                            e.tensor_copy(out=kvst[0:64, h * 65:(h + 1) * 65],
                                          in_=bank[0:64, o:o + 65])
                        return None
                    hKVa = P.add("v", [hKVa], evKV)

                    def kvmove(e, pss=pss):
                        return [e.dma_start(
                                    out=kv_ext.ap().rearrange("p (H c) -> p H c", c=65)[0:64, pss * 8:pss * 8 + 8, :],
                                    in_=kvst.ap().rearrange("p (h c) -> p h c", c=65)[0:64, 0:16:2, :]),
                                e.dma_start(
                                    out=kv_ext.ap().rearrange("p (H c) -> p H c", c=65)[64:128, pss * 8:pss * 8 + 8, :],
                                    in_=kvst.ap().rearrange("p (h c) -> p h c", c=65)[0:64, 1:16:2, :])]
                    hKVa = P.dma("g", [hKVa], kvmove, n=2)
                hP1 = P.barrier()

        if PHASES >= 1:
            pump(10 ** 9)
            qs_.close()
            hWm1 = QS.get('wm1')
            hWm2 = QS.get('wm2')
        hP1b = P.barrier()

        hP2 = hP1b

        def emit_l0_tile(tag, subs, xsrc_fn, dst_fn, sg_fn, deps_in):
            TW = sum(r for _, r in subs)
            ns = len(subs)
            sq = (subs[0][1] == 128)
            tstack = contextlib.ExitStack()
            with tstack:
                SBt = lambda n, s, dt=F32: tstack.enter_context(nc.sbuf_tensor(f"{tag}_{n}", s, dt))

                def RT(n, s, dt=F32):
                    cm = nc.sbuf_tensor(f"{tag}_{n}", s, dt, side="right")
                    return cm, cm.__enter__()

                xb = SBt("xb", [128, D])
                xn2 = SBt("xn2", [128, D])
                xnb2 = SBt("xnb2", [128, D], BF16)
                bn6t = SBt("bn6t", [128, 96])
                bn2t = SBt("bn2t", [128, 2])
                stq = SBt("stq", [128, 8])
                rc2 = SBt("rc2", [128, 2 * ns])
                fc2 = SBt("fc2", [128, 2 * ns])
                frow = SBt("frow", [1, 2 * TW], BF16)
                gbuf = SBt("gbuf", [128, 512])
                bbuf = SBt("bbuf", [128, 512])
                b1b = SBt("b1b", [1, 512], BF16)
                mx4 = SBt("mx4", [128, 16])
                hcm = [None] * ns
                h_s = [None] * ns
                for j in reversed(range(ns)):
                    hcm[j], h_s[j] = RT(f"h{j}", [subs[j][1], HD], F32)
                qaTcm, qaT = RT("qaT", [128, 16 * TW], BF16)
                qwb = [SBt(f"qwb{i}", [128, 4 * 512], BF16) for i in range(2)]

                hLN = {}
                hQs = {}
                hIr = None
                hQtr = None
                hAdone = {}
                for j, (off, rows) in enumerate(subs):
                    hDx = P.dma("d", [deps_in, hAdone.get(j - 1)],
                                (lambda e, j=j, rows=rows:
                                 [e.dma_start(out=xb[0:rows, :], in_=xsrc_fn(j))]))
                    hRc = emit_ln(xb, rows, bn6t, bn2t, stq, [hDx], war_deps=[hLN.get(j - 1)])

                    def savestats(e, j=j, rows=rows):
                        e.tensor_copy(out=rc2[0:rows, 2 * j:2 * j + 1], in_=stq[0:rows, 3:4])
                        e.tensor_copy(out=rc2[0:rows, 2 * j + 1:2 * j + 2], in_=stq[0:rows, 1:2])
                        return None
                    hSv = P.add("v", [hRc], savestats, drain_first=True)

                    def xapply(e, j=j, rows=rows):
                        return e.activation(out=xn2[0:rows, :], in_=xb[0:rows, :], func=AF.Identity,
                                            bias=stq[0:rows, 1:2], scale=stq[0:rows, 3:4])
                    hXa = P.add("s", [hRc, hDx, hQs.get(j - 1)], xapply)
                    hLN[j] = hXa

                    def stats2(e, rows=rows):
                        for c in range(4):
                            e.bn_stats(out=bn6t[0:rows, 6 * c:6 * c + 6], in_=xn2[0:rows, 512 * c:512 * (c + 1)])
                        e.drain()
                        e.bn_aggr(out=bn2t[0:rows, :], in_=bn6t[0:rows, 0:24])
                        e.drain()
                        e.tensor_tensor(out=stq[0:rows, 4:5], in0=bn2t[0:rows, 0:1], in1=bn2t[0:rows, 0:1], op=TS.mult)
                        e.drain()
                        e.tensor_tensor(out=stq[0:rows, 4:5], in0=stq[0:rows, 4:5], in1=bn2t[0:rows, 1:2], op=TS.add)
                        e.drain()
                        e.tensor_scalar(out=stq[0:rows, 4:5], in0=stq[0:rows, 4:5], scalar1=1e-5, scalar2=None,
                                        op0=TS.add, op1=TS.bypass)
                        e.drain()
                        e.tensor_reduce(out=stq[0:rows, 5:6], in_=xn2[0:rows, :], axis=AX.X,
                                        op=TS.max, apply_absolute_value=True)
                        return None
                    hS2 = P.add("v", [hXa, hSv], stats2)
                    hSq2 = P.add("s", [hS2, hXa], (lambda e, rows=rows:
                                                   e.sqrt(out=stq[0:rows, 6:7], in_=stq[0:rows, 4:5])))

                    def qscale(e, j=j, rows=rows):
                        e.reciprocal(out=stq[0:rows, 7:8], in_=stq[0:rows, 6:7])
                        e.drain()
                        e.tensor_tensor(out=stq[0:rows, 5:6], in0=stq[0:rows, 5:6], in1=stq[0:rows, 7:8], op=TS.mult)
                        e.drain()
                        e.tensor_scalar(out=stq[0:rows, 5:6], in0=stq[0:rows, 5:6], scalar1=1e-5, scalar2=None,
                                        op0=TS.max, op1=TS.bypass)
                        e.drain()
                        e.reciprocal(out=stq[0:rows, 4:5], in_=stq[0:rows, 5:6])
                        e.drain()
                        e.tensor_scalar(out=stq[0:rows, 4:5], in0=stq[0:rows, 4:5], scalar1=127.0, scalar2=None,
                                        op0=TS.mult, op1=TS.bypass)
                        e.drain()
                        e.tensor_tensor(out=stq[0:rows, 4:5], in0=stq[0:rows, 4:5], in1=stq[0:rows, 7:8], op=TS.mult)
                        e.drain()
                        e.tensor_tensor(out=fc2[0:rows, 2 * j:2 * j + 1], in0=stq[0:rows, 5:6],
                                        in1=wmrow[0:rows, 0:1], op=TS.mult)
                        e.drain()
                        e.tensor_scalar(out=fc2[0:rows, 2 * j:2 * j + 1], in0=fc2[0:rows, 2 * j:2 * j + 1],
                                        scalar1=float(1.0 / 127.0), scalar2=None, op0=TS.mult, op1=TS.bypass)
                        e.drain()
                        e.reciprocal(out=stq[0:rows, 7:8], in_=fc2[0:rows, 2 * j:2 * j + 1])
                        e.drain()
                        e.tensor_scalar(out=xn2[0:rows, :], in0=xn2[0:rows, :],
                                        scalar1=stq[0:rows, 4:5], scalar2=magic[0:rows, 0:1],
                                        op0=TS.mult, op1=TS.add)
                        e.drain()
                        e.tensor_scalar(out=xn2[0:rows, :], in0=xn2[0:rows, :],
                                        scalar1=magic[0:rows, 0:1], scalar2=-128.0,
                                        op0=TS.subtract, op1=TS.max)
                        e.drain()
                        e.tensor_scalar(out=xnb2[0:rows, :], in0=xn2[0:rows, :],
                                        scalar1=127.0, scalar2=None, op0=TS.min, op1=TS.bypass)
                        return None
                    hQ1 = P.add("v", [hSq2, hWm1, hV0], qscale)
                    hQs[j] = hQ1
                    hIr = P.add("t", [hQ1, hIr, hD0], (lambda e, rows=rows:
                                e.matmul(pb[7][0:1, 0:rows], stq[0:rows, 7:8], identf[0:rows, 0:rows],
                                         start=True, stop=True, skip_group_check=True)))
                    hIr = P.add("v", [hIr], (lambda e, off=off, rows=rows:
                                e.tensor_copy(out=frow[0:1, off:off + rows], in_=pb[7][0:1, 0:rows])))
                    hEvq = None
                    for cc in range(0, 16, 2):
                        def qt2(e, cc=cc, rows=rows):
                            last = None
                            for c in (cc, cc + 1):
                                last = e.matmul(pb[5 + c % 2][0:128, 0:rows], xnb2[0:rows, c * 128:(c + 1) * 128],
                                                ident[0:rows, 0:rows], start=True, stop=True, skip_group_check=True)
                            return last
                        hT2 = P.add("t", [hQ1 if cc == 0 else None, hQtr, hEvq], qt2)
                        hQtr = hT2

                        def qev(e, cc=cc, j=j, off=off, rows=rows):
                            for c in (cc, cc + 1):
                                e.tensor_copy(out=qaT[:, c * TW + off:c * TW + off + rows],
                                              in_=pb[5 + c % 2][0:128, 0:rows])
                            return None
                        hEvq = P.add("v", [hT2], qev)
                    hAdone[j] = hEvq
                    hQs[j] = hEvq
                hAq = P.barrier()
                if L0STOP == 'A':
                    return hAq

                hQld = None
                hEvH = {}
                hMM = {}
                hB1d = None
                lastu = {0: None, 1: None}
                for s in range(16):
                    for kh in range(4):
                        hQld = P.dma("g", [hQld, hAq if s == 0 and kh == 0 else None,
                                           lastu[kh % 2]],
                                     (lambda e, s=s, kh=kh, b=qwb[kh % 2]:
                                      [e.dma_start(out=b.ap().rearrange("p (c n) -> p c n", n=512),
                                                   in_=qw1_scr.rearrange("(c p) n -> p c n", p=128)
                                                   [:, kh * 4:(kh + 1) * 4, s * 512:(s + 1) * 512])]))

                        def mm1(e, s=s, kh=kh):
                            last = None
                            for j, (off, rows) in enumerate(subs):
                                for cc in range(4):
                                    c = kh * 4 + cc
                                    last = e.matmul(pb[j][0:rows, 0:512],
                                                    qaT[:, c * TW + off:c * TW + off + rows],
                                                    qwb[kh % 2][:, cc * 512:(cc + 1) * 512],
                                                    start=(c == 0), stop=False)
                            return last
                        hMMg = P.add("t", [hQld] + [hEvH.get((s - 1, j)) for j in range(ns)], mm1)
                        hMM[(s, kh)] = hMMg
                        lastu[kh % 2] = hMMg
                    hB1d = P.dma("g", [hB1d, hMM[(s, 2)]],
                                 (lambda e, s=s: [e.dma_start(out=b1b[:], in_=b1_row[0:1, s * 512:(s + 1) * 512])]))

                    def fold1(e, s=s):
                        last = None
                        for j, (off, rows) in enumerate(subs):
                            last = e.matmul(pb[j][0:rows, 0:512], frow[0:1, off:off + rows],
                                            b1b[0:1, 0:512], start=False, stop=True)
                        return last
                    hF = P.add("t", [hMM[(s, 3)], hB1d], fold1)
                    for j, (off, rows) in enumerate(subs):
                        def evh(e, s=s, j=j, rows=rows):
                            return e.activation(out=h_s[j][0:rows, s * 512:(s + 1) * 512],
                                                in_=pb[j][0:rows, 0:512], func=AF.Silu,
                                                scale=fc2[0:rows, 2 * j:2 * j + 1])
                        hEvH[(s, j)] = P.add("s", [hF, hEvH.get((s - 1, j))], evh)
                hBq = P.barrier()
                if L0STOP == 'B':
                    qaTcm.__exit__(None, None, None)
                    return hBq
                qaTcm.__exit__(None, None, None)
                sgst = SBt("sgst", [128, 512])
                hSgD_last = None

                pairs = [list(range(ns))[i:i + 2] for i in range(0, ns, 2)]
                hgbd = None
                hQ2 = {}
                hapv = None
                hQld2 = None
                hMM2 = {}
                hEvG = {}
                lastu2 = {0: lastu[0], 1: lastu[1]}
                for pi, pair in enumerate(pairs):
                    xqcm = {}
                    xq2 = {}
                    xqpre = {}
                    xqprecm = {}
                    for j in pair:
                        off, rows = subs[j]
                        if sq:
                            xqcm[j] = nc.sbuf_tensor(f"{tag}_xq2_{j}", [128, HD], BF16)
                            xq2[j] = xqcm[j].__enter__()
                            xqpre[j] = xq2[j]
                        else:
                            xqprecm[j] = nc.sbuf_tensor(f"{tag}_xq2p_{j}", [rows, HD], BF16)
                            xqpre[j] = xqprecm[j].__enter__()
                            xqcm[j] = nc.sbuf_tensor(f"{tag}_xq2_{j}", [128, 64 * rows], BF16)
                            xq2[j] = xqcm[j].__enter__()
                        hGel = P.add("s", [hEvH[(15, j)], hQ2.get(j - 1)], (lambda e, j=j, rows=rows:
                                     e.activation(out=h_s[j][0:rows, :], in_=h_s[j][0:rows, :], func=AF.Gelu)))

                        def hstats(e, j=j, rows=rows):
                            for c in range(16):
                                e.bn_stats(out=bn6t[0:rows, 6 * c:6 * c + 6], in_=h_s[j][0:rows, 512 * c:512 * (c + 1)])
                            e.drain()
                            e.bn_aggr(out=bn2t[0:rows, :], in_=bn6t[0:rows, 0:96])
                            e.drain()
                            e.tensor_scalar(out=stq[0:rows, 0:1], in0=bn2t[0:rows, 1:2], scalar1=1e-5,
                                            scalar2=None, op0=TS.add, op1=TS.bypass)
                            return None
                        hHs = P.add("v", [hGel, hapv], hstats, drain_first=True)
                        hSq3 = P.add("s", [hHs, hGel], (lambda e, rows=rows:
                                                        e.sqrt(out=stq[0:rows, 2:3], in_=stq[0:rows, 0:1])))

                        def recip3(e, rows=rows):
                            e.reciprocal(out=stq[0:rows, 3:4], in_=stq[0:rows, 2:3])
                            e.drain()
                            e.tensor_tensor(out=stq[0:rows, 1:2], in0=bn2t[0:rows, 0:1], in1=stq[0:rows, 3:4], op=TS.mult)
                            e.drain()
                            e.tensor_scalar(out=stq[0:rows, 1:2], in0=stq[0:rows, 1:2], scalar1=-1.0, scalar2=None,
                                            op0=TS.mult, op1=TS.bypass)
                            return None
                        hR3 = P.add("v", [hSq3], recip3)
                        hap = None
                        for cq in range(16):
                            hgbd = P.dma("d", [hgbd, hap, hR3 if cq == 0 else None],
                                         (lambda e, cq=cq:
                                          [e.dma_start(out=gbuf[:], in_=g_rep[:, cq * 512:(cq + 1) * 512]),
                                           e.dma_start(out=bbuf[:], in_=b_rep[:, cq * 512:(cq + 1) * 512])]), n=2)
                            sl = slice(cq * 512, (cq + 1) * 512)

                            def lnap_s(e, j=j, rows=rows, sl=sl):
                                return e.activation(out=h_s[j][0:rows, sl], in_=h_s[j][0:rows, sl],
                                                    func=AF.Identity, bias=stq[0:rows, 1:2], scale=stq[0:rows, 3:4])
                            hAs = P.add("s", [hR3, hap], lnap_s)

                            def lnap_v(e, j=j, rows=rows, sl=sl, cq=cq):
                                e.tensor_tensor(out=h_s[j][0:rows, sl], in0=h_s[j][0:rows, sl],
                                                in1=gbuf[0:rows, :], op=TS.mult)
                                e.drain()
                                e.tensor_tensor(out=h_s[j][0:rows, sl], in0=h_s[j][0:rows, sl],
                                                in1=bbuf[0:rows, :], op=TS.add)
                                e.drain()
                                e.tensor_reduce(out=mx4[0:rows, cq:cq + 1], in_=h_s[j][0:rows, sl],
                                                axis=AX.X, op=TS.max, apply_absolute_value=True)
                                return None
                            hap = P.add("v", [hAs, hgbd], lnap_v)
                        hapv = hap
                        if DEBUG_TAPS and tag == "t0" and j == 0:
                            hap = P.dma("d", [hap], (lambda e, rows=rows:
                                        [e.dma_start(out=taps[2][0:rows, :], in_=h_s[0][0:rows, :])]))

                        def q2stats(e, j=j, rows=rows):
                            for cch in range(16):
                                e.bn_stats(out=bn6t[0:rows, 6 * cch:6 * cch + 6],
                                           in_=h_s[j][0:rows, 512 * cch:512 * (cch + 1)])
                            e.drain()
                            e.bn_aggr(out=bn2t[0:rows, :], in_=bn6t[0:rows, 0:96])
                            e.drain()
                            e.tensor_tensor(out=stq[0:rows, 4:5], in0=bn2t[0:rows, 0:1], in1=bn2t[0:rows, 0:1], op=TS.mult)
                            e.drain()
                            e.tensor_tensor(out=stq[0:rows, 4:5], in0=stq[0:rows, 4:5], in1=bn2t[0:rows, 1:2], op=TS.add)
                            e.drain()
                            e.tensor_scalar(out=stq[0:rows, 4:5], in0=stq[0:rows, 4:5], scalar1=1e-5, scalar2=None,
                                            op0=TS.add, op1=TS.bypass)
                            return None
                        hap = P.add("v", [hap], q2stats, drain_first=True)
                        hap = P.add("s", [hap], (lambda e, rows=rows:
                                                 e.sqrt(out=stq[0:rows, 6:7], in_=stq[0:rows, 4:5])))

                        def q2scale(e, j=j, rows=rows, xqpre=xqpre):
                            e.reciprocal(out=stq[0:rows, 2:3], in_=stq[0:rows, 6:7])
                            e.drain()
                            e.tensor_reduce(out=stq[0:rows, 5:6], in_=mx4[0:rows, 0:16], axis=AX.X, op=TS.max)
                            e.drain()
                            e.tensor_tensor(out=stq[0:rows, 5:6], in0=stq[0:rows, 5:6], in1=stq[0:rows, 2:3], op=TS.mult)
                            e.drain()
                            e.tensor_scalar(out=stq[0:rows, 5:6], in0=stq[0:rows, 5:6], scalar1=1e-5, scalar2=None,
                                            op0=TS.max, op1=TS.bypass)
                            e.drain()
                            e.reciprocal(out=stq[0:rows, 4:5], in_=stq[0:rows, 5:6])
                            e.drain()
                            e.tensor_scalar(out=stq[0:rows, 4:5], in0=stq[0:rows, 4:5], scalar1=127.0, scalar2=None,
                                            op0=TS.mult, op1=TS.bypass)
                            e.drain()
                            e.tensor_tensor(out=stq[0:rows, 4:5], in0=stq[0:rows, 4:5], in1=stq[0:rows, 2:3], op=TS.mult)
                            e.drain()
                            e.tensor_tensor(out=fc2[0:rows, 2 * j + 1:2 * j + 2], in0=stq[0:rows, 5:6],
                                            in1=wmrow[0:rows, 2:3], op=TS.mult)
                            e.drain()
                            e.tensor_scalar(out=fc2[0:rows, 2 * j + 1:2 * j + 2], in0=fc2[0:rows, 2 * j + 1:2 * j + 2],
                                            scalar1=float(1.0 / 127.0), scalar2=None, op0=TS.mult, op1=TS.bypass)
                            e.drain()
                            e.reciprocal(out=stq[0:rows, 7:8], in_=fc2[0:rows, 2 * j + 1:2 * j + 2])
                            e.drain()
                            e.tensor_scalar(out=h_s[j][0:rows, :], in0=h_s[j][0:rows, :],
                                            scalar1=stq[0:rows, 4:5], scalar2=magic[0:rows, 0:1],
                                            op0=TS.mult, op1=TS.add)
                            e.drain()
                            e.tensor_scalar(out=h_s[j][0:rows, :], in0=h_s[j][0:rows, :],
                                            scalar1=magic[0:rows, 0:1], scalar2=-128.0,
                                            op0=TS.subtract, op1=TS.max)
                            e.drain()
                            e.tensor_scalar(out=xqpre[j][0:rows, :], in0=h_s[j][0:rows, :],
                                            scalar1=127.0, scalar2=None, op0=TS.min, op1=TS.bypass)
                            return None
                        hQ2v = P.add("v", [hap, hWm2], q2scale)
                        if DEBUG_TAPS and tag == "t0" and j == 0:
                            hQ2v = P.dma("g", [hQ2v], (lambda e, rows=rows, xqpre=xqpre:
                                         [e.dma_start(out=taps[3][0:rows, :], in_=xqpre[0][0:rows, :])]))
                        hIr = P.add("t", [hQ2v, hIr], (lambda e, rows=rows:
                                    e.matmul(pb[7][0:1, 0:rows], stq[0:rows, 7:8], identf[0:rows, 0:rows],
                                             start=True, stop=True, skip_group_check=True)))
                        hIr = P.add("v", [hIr], (lambda e, off=off, rows=rows:
                                    e.tensor_copy(out=frow[0:1, TW + off:TW + off + rows], in_=pb[7][0:1, 0:rows])))
                        hTq2 = None
                        hEvq2 = None
                        for cc in range(0, 64, 2):
                            def xt2(e, cc=cc, j=j, rows=rows, xqpre=xqpre):
                                last = None
                                for c in (cc, cc + 1):
                                    last = e.matmul(pb[5 + c % 2][0:128, 0:rows],
                                                    xqpre[j][0:rows, c * 128:(c + 1) * 128],
                                                    ident[0:rows, 0:rows], start=True, stop=True, skip_group_check=True)
                                return last
                            hT2 = P.add("t", [hQ2v if cc == 0 else None, hTq2, hEvq2, hIr if cc == 0 else None], xt2)
                            hTq2 = hT2

                            def xev(e, cc=cc, j=j, rows=rows, xq2=xq2):
                                for c in (cc, cc + 1):
                                    if sq:
                                        e.tensor_copy(out=xq2[j][0:128, c * 128:(c + 1) * 128],
                                                      in_=pb[5 + c % 2][0:128, 0:128])
                                    else:
                                        e.tensor_copy(out=xq2[j][0:128, c * rows:(c + 1) * rows],
                                                      in_=pb[5 + c % 2][0:128, 0:rows])
                                return None
                            hEvq2 = P.add("v", [hT2], xev)
                        hQ2[j] = hEvq2
                        hcm[j].__exit__(None, None, None)
                    for s4 in range(4):
                        for kh in range(16):
                            hQld2 = P.dma("g", [hQld2, [hQ2[j] for j in pair] if s4 == 0 and kh == 0 else None,
                                                lastu2[kh % 2]],
                                          (lambda e, s4=s4, kh=kh, b=qwb[kh % 2]:
                                           [e.dma_start(out=b.ap().rearrange("p (c n) -> p c n", n=512),
                                                        in_=qw2_scr.rearrange("(c p) n -> p c n", p=128)
                                                        [:, kh * 4:(kh + 1) * 4, s4 * 512:(s4 + 1) * 512])]))

                            def mm2(e, s4=s4, kh=kh, pair=pair, xq2=xq2):
                                last = None
                                for j in pair:
                                    off, rows = subs[j]
                                    for cc in range(4):
                                        c = kh * 4 + cc
                                        last = e.matmul(pb[j % 2][0:rows, 0:512],
                                                        xq2[j][:, c * rows:(c + 1) * rows],
                                                        qwb[kh % 2][:, cc * 512:(cc + 1) * 512],
                                                        start=(c == 0), stop=False)
                                return last
                            hMMg = P.add("t", [hQld2] + [hEvG.get((pi, s4 - 1, j), (None,))[0] for j in pair]
                                         + ([hQ2[j] for j in pair] if s4 == 0 and kh == 0 else []), mm2)
                            hMM2[(pi, s4, kh)] = hMMg
                            lastu2[kh % 2] = hMMg
                        hB1d = P.dma("g", [hB1d, hMM2[(pi, s4, 14)]],
                                     (lambda e, s4=s4: [e.dma_start(out=b1b[:], in_=b2_row[0:1, s4 * 512:(s4 + 1) * 512])]))

                        def fold2(e, s4=s4, pair=pair):
                            last = None
                            for j in pair:
                                off, rows = subs[j]
                                last = e.matmul(pb[j % 2][0:rows, 0:512], frow[0:1, TW + off:TW + off + rows],
                                                b1b[0:1, 0:512], start=False, stop=True)
                            return last
                        hF2 = P.add("t", [hMM2[(pi, s4, 15)], hB1d], fold2)
                        for j in pair:
                            off, rows = subs[j]

                            def evg(e, s4=s4, j=j, rows=rows):
                                return e.activation(out=sgst[0:rows, 0:512],
                                                    in_=pb[j % 2][0:rows, 0:512], func=AF.Silu,
                                                    scale=fc2[0:rows, 2 * j + 1:2 * j + 2])
                            hEvGs = P.add("s", [hF2, hSgD_last], evg)
                            hSgD = P.dma("d", [hEvGs],
                                         (lambda e, s4=s4, j=j, rows=rows:
                                          [e.dma_start(out=sg_fn(j)[0:rows, s4 * 512:(s4 + 1) * 512],
                                                       in_=sgst[0:rows, 0:512])]))
                            hSgD_last = hSgD
                            hEvG[(pi, s4, j)] = (hSgD, hEvGs)
                    hPairDone = P.barrier()
                    for j in reversed(pair):
                        xqcm[j].__exit__(None, None, None)
                        if not sq:
                            xqprecm[j].__exit__(None, None, None)
                hDq = P.barrier()
                if L0STOP == 'D':
                    return hDq

                xnT = SBt("xnT", [128, 16 * TW], BF16)
                qT = SBt("qT", [128, 16 * TW], BF16)
                inner = SBt("inner", [128, D], BF16)
                innerT = SBt("innerT", [128, 16 * TW], BF16)
                normc = SBt("normc", [128, 32])
                recn = SBt("recn", [128, 32])
                s_s = [SBt(f"ss{j}", [r, D]) for j, (_, r) in enumerate(subs)]
                wobcm = []
                wob = []
                for i in range(2):
                    cm, tt = RT(f"wob{i}", [128, 8 * 512], BF16)
                    wobcm.append(cm)
                    wob.append(tt)

                hXn = {}
                hTx = None
                hEvX = None
                for j, (off, rows) in enumerate(subs):
                    hDx = P.dma("d", [hDq, hXn.get(j - 1)],
                                (lambda e, j=j, rows=rows:
                                 [e.dma_start(out=xb[0:rows, :], in_=xsrc_fn(j))]))

                    def xapp2(e, j=j, rows=rows):
                        e.activation(out=xn2[0:rows, :], in_=xb[0:rows, :], func=AF.Identity,
                                     bias=rc2[0:rows, 2 * j + 1:2 * j + 2], scale=rc2[0:rows, 2 * j:2 * j + 1])
                        e.drain()
                        return e.activation(out=xnb2[0:rows, :], in_=xn2[0:rows, :], func=AF.Identity)
                    hXa2 = P.add("s", [hDx, hDq, hXn.get(j - 1)], xapp2)
                    for cc in range(0, 16, 2):
                        def xt3(e, cc=cc, rows=rows):
                            last = None
                            for c in (cc, cc + 1):
                                last = e.matmul(pb[5 + c % 2][0:128, 0:rows], xnb2[0:rows, c * 128:(c + 1) * 128],
                                                ident[0:rows, 0:rows], start=True, stop=True, skip_group_check=True)
                            return last
                        hT3 = P.add("t", [hXa2 if cc == 0 else None, hTx, hEvX], xt3)
                        hTx = hT3

                        def xev3(e, cc=cc, off=off, rows=rows):
                            for c in (cc, cc + 1):
                                e.tensor_copy(out=xnT[:, c * TW + off:c * TW + off + rows],
                                              in_=pb[5 + c % 2][0:128, 0:rows])
                            return None
                        hEvX = P.add("v", [hT3], xev3)
                    hXn[j] = hEvX
                hE1 = P.barrier()
                if L0STOP == 'E1':
                    return hE1

                hWqd = None
                hQe = {}
                lastu3 = {0: None, 1: None}
                for qc4 in range(4):
                    for khq in range(2):
                        hWqd = P.dma("g", [hWqd, hE1 if qc4 == 0 and khq == 0 else None, lastu3[khq]],
                                     (lambda e, qc4=qc4, khq=khq, b=wob[khq]:
                                      [e.dma_start(out=b.ap().rearrange("p (c n) -> p c n", n=512),
                                                   in_=Wq.rearrange("(c p) n -> p c n", p=128)
                                                   [:, khq * 8:(khq + 1) * 8, qc4 * 512:(qc4 + 1) * 512])]))

                        def qproj(e, qc4=qc4, khq=khq):
                            last = None
                            for qq in range(4):
                                for cc in range(8):
                                    c = khq * 8 + cc
                                    last = e.matmul(pb[qq][0:128, 0:TW],
                                                    wob[khq][:, cc * 512 + qq * 128:cc * 512 + (qq + 1) * 128],
                                                    xnT[:, c * TW:(c + 1) * TW],
                                                    start=(c == 0), stop=(c == 15))
                            return last
                        hQp = P.add("t", [hWqd, hE1, hQe.get(qc4 - 1)], qproj)
                        lastu3[khq] = hQp

                    def qev4(e, qc4=qc4):
                        for qq in range(4):
                            qc = qc4 * 4 + qq
                            e.activation(out=qT[:, qc * TW:(qc + 1) * TW], in_=pb[qq][0:128, 0:TW], func=AF.Relu)
                        return None
                    hQe[qc4] = P.add("s", [hQp, hQe.get(qc4 - 1)], qev4)
                hE2 = P.barrier()
                if L0STOP == 'E2':
                    return hE2

                hIt = None
                hIev = None
                hQkv = {}
                for j, (off, rows) in enumerate(subs):
                    def qkv(e, j=j, off=off, rows=rows):
                        last = None
                        order = [H for H in range(32) if H % 2 == 0] + [H for H in range(32) if H % 2 == 1]
                        for i, H in enumerate(order):
                            if i == 16:
                                e.drain()
                            last = e.matmul(pb[H // 4][0:rows, (H % 4) * 128:(H % 4) * 128 + 65],
                                            qT[(H % 2) * 64:(H % 2) * 64 + 64,
                                               (H // 2) * TW + off:(H // 2) * TW + off + rows],
                                            kv_ext[(H % 2) * 64:(H % 2) * 64 + 64,
                                                   (H // 2) * 65:(H // 2) * 65 + 65],
                                            start=True, stop=True, skip_group_check=True)
                        return last
                    hK = P.add("t", [hE2 if j == 0 else None, hQkv.get(j - 1), hIt], qkv)

                    def ngather(e, j=j, rows=rows):
                        for b in range(8):
                            e.tensor_copy(out=normc[0:rows, b * 4:(b + 1) * 4],
                                          in_=bass.AP(pb[b], 64, [[512, rows], [128, 4]]))
                        e.drain()
                        e.tensor_scalar(out=normc[0:rows, :], in0=normc[0:rows, :], scalar1=1e-6,
                                        scalar2=None, op0=TS.max, op1=TS.bypass)
                        e.drain()
                        e.reciprocal(out=recn[0:rows, :], in_=normc[0:rows, :])
                        e.drain()
                        for H in range(32):
                            e.tensor_scalar(out=inner[0:rows, H * 64:(H + 1) * 64],
                                            in0=pb[H // 4][0:rows, (H % 4) * 128:(H % 4) * 128 + 64],
                                            scalar1=recn[0:rows, H:H + 1], scalar2=None,
                                            op0=TS.mult, op1=TS.bypass)
                        return None
                    hN = P.add("v", [hK, hIev], ngather, drain_first=True)
                    for cc in range(0, 16, 2):
                        def it2(e, cc=cc, rows=rows):
                            last = None
                            for c in (cc, cc + 1):
                                last = e.matmul(pb[5 + c % 2][0:128, 0:rows], inner[0:rows, c * 128:(c + 1) * 128],
                                                ident[0:rows, 0:rows], start=True, stop=True, skip_group_check=True)
                            return last
                        hT4 = P.add("t", [hN if cc == 0 else None, hIt, hIev], it2)
                        hIt = hT4

                        def iev(e, cc=cc, off=off, rows=rows):
                            for c in (cc, cc + 1):
                                e.tensor_copy(out=innerT[:, c * TW + off:c * TW + off + rows],
                                              in_=pb[5 + c % 2][0:128, 0:rows])
                            return None
                        hIev = P.add("v", [hT4], iev)
                    hQkv[j] = hIev
                hE3 = P.barrier()
                if L0STOP == 'E3':
                    return hE3

                hWod = None
                hWoe = {}
                lastu4 = {0: lastu3[0], 1: lastu3[1]}
                for s4 in range(4):
                    for kho in range(2):
                        hWod = P.dma("g", [hWod, hE3 if s4 == 0 and kho == 0 else None, lastu4[kho]],
                                     (lambda e, s4=s4, kho=kho, b=wob[kho]:
                                      [e.dma_start(out=b.ap().rearrange("p (c n) -> p c n", n=512),
                                                   in_=Wo.rearrange("(c p) n -> p c n", p=128)
                                                   [:, kho * 8:(kho + 1) * 8, s4 * 512:(s4 + 1) * 512])]))

                        def womm(e, s4=s4, kho=kho):
                            last = None
                            for j, (off, rows) in enumerate(subs):
                                for cc in range(8):
                                    c = kho * 8 + cc
                                    last = e.matmul(pb[j][0:rows, 0:512],
                                                    innerT[:, c * TW + off:c * TW + off + rows],
                                                    wob[kho][:, cc * 512:(cc + 1) * 512],
                                                    start=(c == 0), stop=False)
                            return last
                        hWp2 = P.add("t", [hWod, hE3 if s4 == 0 and kho == 0 else None,
                                           hWoe.get(s4 - 1)], womm)
                        lastu4[kho] = hWp2

                    def bofold(e, s4=s4):
                        last = None
                        for j, (off, rows) in enumerate(subs):
                            last = e.matmul(pb[j][0:rows, 0:512], ones1b[0:1, 0:rows],
                                            bo16[0:1, s4 * 512:(s4 + 1) * 512], start=False, stop=True)
                        return last
                    hWp2 = P.add("t", [hWp2, hG0], bofold)

                    def woev(e, s4=s4):
                        for j, (off, rows) in enumerate(subs):
                            e.tensor_copy(out=s_s[j][0:rows, s4 * 512:(s4 + 1) * 512],
                                          in_=pb[j][0:rows, 0:512])
                        return None
                    hWoe[s4] = P.add("v", [hWp2], woev)
                hE4 = P.barrier()
                if L0STOP == 'E':
                    wobcm[1].__exit__(None, None, None)
                    wobcm[0].__exit__(None, None, None)
                    return hE4
                wobcm[1].__exit__(None, None, None)
                wobcm[0].__exit__(None, None, None)

                hCd = None
                hCb = None
                for j, (off, rows) in enumerate(subs):
                    hDx = P.dma("d", [hE4, hCd, hCb],
                                (lambda e, j=j, rows=rows:
                                 [e.dma_start(out=xb[0:rows, :], in_=xsrc_fn(j)),
                                  e.dma_start(out=xn2[0:rows, :], in_=sg_fn(j))]), n=2)

                    def comb(e, j=j, rows=rows):
                        e.tensor_tensor(out=xn2[0:rows, :], in0=xn2[0:rows, :],
                                        in1=s_s[j][0:rows, :], op=TS.mult)
                        e.drain()
                        e.tensor_tensor(out=xn2[0:rows, :], in0=xn2[0:rows, :],
                                        in1=xb[0:rows, :], op=TS.add)
                        return None
                    hCb = P.add("v", [hDx, hE4], comb)
                    hCd = P.dma("d", [hCb], (lambda e, j=j, rows=rows:
                                             [e.dma_start(out=dst_fn(j), in_=xn2[0:rows, :])]))
            return [hCd]


        if PHASES >= 3:
            hP2 = emit_l0_tile("td", [(0, 64)],
                               lambda j: x_seq[0:64, :],
                               lambda j: x1d_scr[0:64, :],
                               lambda j: sgd_scr[0:64, :], hP1b)
            for t in range(min(NTILE, P2TILES)):
                hP2 = emit_l0_tile(f"t{t}", [(j * 128, 128) for j in range(4)],
                                   lambda j, t=t: x_own[t * 512 + j * 128:t * 512 + (j + 1) * 128, :],
                                   lambda j, t=t: x1_scr[t * 512 + j * 128:t * 512 + (j + 1) * 128, :],
                                   lambda j, t=t: sg_scr[t * 512 + j * 128:t * 512 + (j + 1) * 128, :],
                                   hP2)

        hP3 = hP2 if PHASES >= 3 else hP1b
        if PHASES >= 4:
            with contextlib.ExitStack() as p3s:
                SB3 = lambda n, s, dt=F32: p3s.enter_context(nc.sbuf_tensor(n, s, dt))
                x1d = SB3("p3_x1d", [64, D])
                lq_s = SB3("p3_lq", [128, 16 * 64], BF16)
                bn63 = SB3("p3_bn6", [64, 24])
                bn23 = SB3("p3_bn2", [64, 2])
                st3 = SB3("p3_st", [64, 4])
                xn3 = SB3("p3_xn", [64, D])
                xnb3 = SB3("p3_xnb", [64, D], BF16)
                xnT3 = SB3("p3_xnT", [128, 16 * 64], BF16)
                wb3 = [SB3(f"p3_wb{i}", [128, 16 * 512], BF16) for i in range(2)]
                k64 = SB3("p3_k64", [64, D])
                v64 = SB3("p3_v64", [64, D], BF16)
                kr = SB3("p3_kr", [64, D])
                krb = SB3("p3_krb", [64, D], BF16)
                krT = SB3("p3_krT", [128, 16 * 64], BF16)
                cosb = SB3("p3_cos", [64, 16 * 64])
                sinb = SB3("p3_sin", [64, 16 * 64])
                scf = SB3("p3_scf", [64, 64])
                nmcol = SB3("p3_nm", [64, 2])
                wsm = SB3("p3_wsm", [64, 64])
                wsb = SB3("p3_wsb", [64, 64], BF16)
                wTb = SB3("p3_wTb", [64, 64], BF16)
                ast = SB3("p3_ast", [64, 16 * 128], BF16)

                hXd = P.dma("d", hP3, lambda e: [e.dma_start(out=x1d[:], in_=x1d_scr),
                                                 e.dma_start(out=cosb[:], in_=c_cos),
                                                 e.dma_start(out=sinb[:], in_=c_sin)], n=3)
                hLq = P.dma("g", hP3, lambda e: [e.dma_start(out=lq_s[:], in_=lqT)], n=1)
                hRc3 = emit_ln(x1d, 64, bn63, bn23, st3, [hXd])

                def xap3(e):
                    e.activation(out=xn3[:], in_=x1d[:], func=AF.Identity,
                                 bias=st3[:, 1:2], scale=st3[:, 3:4])
                    e.drain()
                    return e.activation(out=xnb3[:], in_=xn3[:], func=AF.Identity)
                hXa3 = P.add("s", [hRc3, hXd], xap3)
                hT5 = None
                hE5 = None
                for c in range(16):
                    hT5 = P.add("t", [hXa3 if c == 0 else None, hT5, hE5],
                                (lambda e, c=c: e.matmul(pb[c % 2][0:128, 0:64],
                                                         xnb3[0:64, c * 128:(c + 1) * 128],
                                                         ident[0:64, 0:64],
                                                         start=True, stop=True, skip_group_check=True)))
                    hE5 = P.add("v", [hT5], (lambda e, c=c:
                                e.tensor_copy(out=xnT3[:, c * 64:(c + 1) * 64], in_=pb[c % 2][0:128, 0:64])))
                hB3 = P.barrier()

                hWd3 = None
                hPrj = {}
                for wi, (wt, kind) in enumerate([(Wk1, "k"), (Wv1, "v")]):
                    for s4 in range(4):
                        hWd3 = P.dma("g", [hWd3, hB3 if wi == 0 and s4 == 0 else None,
                                           hPrj.get((wi, s4 - 1)) or hPrj.get((wi - 1, 3))],
                                     (lambda e, wt=wt, s4=s4, b=wb3[s4 % 2]:
                                      [e.dma_start(out=b.ap().rearrange("p (c n) -> p c n", n=512),
                                                   in_=wt.rearrange("(c p) n -> p c n", p=128)
                                                   [:, :, s4 * 512:(s4 + 1) * 512])]))

                        def prj(e, s4=s4, b=wb3[s4 % 2]):
                            last = None
                            for c in range(16):
                                last = e.matmul(pb[4][0:64, 0:512], xnT3[:, c * 64:(c + 1) * 64],
                                                b[:, c * 512:(c + 1) * 512], start=(c == 0), stop=(c == 15))
                            return last
                        hPj = P.add("t", [hWd3, hB3, hPrj.get((wi, s4 - 1)) or hPrj.get((wi - 1, 3))], prj)
                        if kind == "k":
                            hPrj[(wi, s4)] = P.add("v", [hPj], (lambda e, s4=s4:
                                            e.tensor_copy(out=k64[:, s4 * 512:(s4 + 1) * 512], in_=pb[4][0:64, 0:512])))
                        else:
                            hPrj[(wi, s4)] = P.add("v", [hPj], (lambda e, s4=s4:
                                            e.tensor_copy(out=v64[:, s4 * 512:(s4 + 1) * 512], in_=pb[4][0:64, 0:512])))
                hB4 = P.barrier()

                def rope(e):
                    kv_ = k64.ap().rearrange("p (h i two) -> p h i two", two=2, i=64)
                    kr_ = kr.ap().rearrange("p (h i two) -> p h i two", two=2, i=64)
                    cos3 = cosb.ap().rearrange("p (h i) -> p h i", i=64).unsqueeze(-1)
                    sin3 = sinb.ap().rearrange("p (h i) -> p h i", i=64).unsqueeze(-1)
                    e.tensor_tensor(out=kr_[:, :, :, 0:1], in0=kv_[:, :, :, 0:1], in1=cos3, op=TS.mult)
                    e.drain()
                    e.tensor_tensor(out=kr_[:, :, :, 1:2], in0=kv_[:, :, :, 0:1], in1=sin3, op=TS.mult)
                    e.drain()
                    t_ = xn3.ap().rearrange("p (h i two) -> p h i two", two=2, i=64)
                    e.tensor_tensor(out=t_[:, :, :, 0:1], in0=kv_[:, :, :, 1:2], in1=sin3, op=TS.mult)
                    e.drain()
                    e.tensor_tensor(out=t_[:, :, :, 1:2], in0=kv_[:, :, :, 1:2], in1=cos3, op=TS.mult)
                    e.drain()
                    e.tensor_tensor(out=kr_[:, :, :, 0:1], in0=kr_[:, :, :, 0:1], in1=t_[:, :, :, 0:1], op=TS.subtract)
                    e.drain()
                    e.tensor_tensor(out=kr_[:, :, :, 1:2], in0=kr_[:, :, :, 1:2], in1=t_[:, :, :, 1:2], op=TS.add)
                    return None
                hRp3 = P.add("v", [hB4], rope, drain_first=True)
                hKb = P.add("s", [hRp3], lambda e: e.activation(out=krb[:], in_=kr[:], func=AF.Identity))
                hT6 = None
                hE6 = None
                for c in range(16):
                    hT6 = P.add("t", [hKb if c == 0 else None, hT6, hE6],
                                (lambda e, c=c: e.matmul(pb[c % 2][0:128, 0:64],
                                                         krb[0:64, c * 128:(c + 1) * 128],
                                                         ident[0:64, 0:64],
                                                         start=True, stop=True, skip_group_check=True)))
                    hE6 = P.add("v", [hT6], (lambda e, c=c:
                                e.tensor_copy(out=krT[:, c * 64:(c + 1) * 64], in_=pb[c % 2][0:128, 0:64])))
                hB5 = P.barrier()

                hSc = {}
                hAt = None
                hAe = None
                for h in range(16):
                    hSc[h] = P.add("t", [hB5 if h == 0 else None, hAt,
                                         hSc.get(h - 4), hLq if h == 0 else None],
                                   (lambda e, h=h: e.matmul(pb[h % 4][0:64, (h // 4) * 128:(h // 4) * 128 + 64],
                                                            lq_s[:, h * 64:(h + 1) * 64],
                                                            krT[:, h * 64:(h + 1) * 64],
                                                            start=True, stop=True, skip_group_check=True)))

                    def smax(e, h=h):
                        e.scalar_tensor_tensor(out=scf[:], in0=pb[h % 4][0:64, (h // 4) * 128:(h // 4) * 128 + 64],
                                               scalar=SC1, in1=mask_s[:], op0=TS.mult, op1=TS.add)
                        e.drain()
                        e.tensor_reduce(out=nmcol[:, 0:1], in_=scf[:], axis=AX.X, op=TS.max, negate=True)
                        return None
                    hSm = P.add("v", [hSc[h], hAe], smax)
                    hEx = P.add("s", [hSm], (lambda e: e.activation(out=wsm[:], in_=scf[:], func=AF.Exp,
                                                                    bias=nmcol[:, 0:1])))

                    def norm(e):
                        e.tensor_reduce(out=nmcol[:, 1:2], in_=wsm[:], axis=AX.X, op=TS.add)
                        e.drain()
                        e.reciprocal(out=nmcol[:, 0:1], in_=nmcol[:, 1:2])
                        e.drain()
                        e.tensor_scalar(out=wsb[:], in0=wsm[:], scalar1=nmcol[:, 0:1], scalar2=None,
                                        op0=TS.mult, op1=TS.bypass)
                        return None
                    hNm = P.add("v", [hEx], norm, drain_first=True)
                    hTw = P.add("t", [hNm, hSc[h]], (lambda e, h=h:
                                e.matmul(pb[h % 4][0:64, (h // 4) * 128:(h // 4) * 128 + 64],
                                         wsb[:], ident[0:64, 0:64], start=True, stop=True,
                                         skip_group_check=True)))
                    hWt = P.add("v", [hTw], (lambda e, h=h:
                                e.tensor_copy(out=wTb[:], in_=pb[h % 4][0:64, (h // 4) * 128:(h // 4) * 128 + 64])))
                    hAt = P.add("t", [hWt], (lambda e, h=h:
                                e.matmul(pb[4 + h % 2][0:64, 0:128], wTb[:],
                                         v64[0:64, h * 128:(h + 1) * 128], start=True, stop=True,
                                         skip_group_check=True)))
                    hAe = P.add("v", [hAt], (lambda e, h=h:
                                e.tensor_copy(out=ast[:, h * 128:(h + 1) * 128], in_=pb[4 + h % 2][0:64, 0:128])))
                hP3 = [P.dma("g", [hAe], lambda e: [
                    e.dma_start(out=attn_s.ap().rearrange("p (H c) -> p H c", c=128)[0:64, :, :],
                                in_=ast.ap().rearrange("p (h c) -> p h c", c=128)[0:64, 0:16:2, :]),
                    e.dma_start(out=attn_s.ap().rearrange("p (H c) -> p H c", c=128)[64:128, :, :],
                                in_=ast.ap().rearrange("p (h c) -> p h c", c=128)[0:64, 1:16:2, :])], n=2)]

        hP4 = hP3

        def emit_l1_tile(tag, t, deps_in):
            subs = [(j * 128, 128) for j in range(4)]
            TW = 512
            tstack = contextlib.ExitStack()
            with tstack:
                SBt = lambda n, s, dt=F32: tstack.enter_context(nc.sbuf_tensor(f"{tag}_{n}", s, dt))

                def RT(n, s, dt=F32):
                    cm = nc.sbuf_tensor(f"{tag}_{n}", s, dt, side="right")
                    return cm, cm.__enter__()

                xb = SBt("xb", [128, D])
                xn2 = SBt("xn2", [128, D])
                xnb2 = SBt("xnb2", [128, D], BF16)
                bn6t = SBt("bn6t", [128, 96])
                bn2t = SBt("bn2t", [128, 2])
                stq = SBt("stq", [128, 8])
                xnT = SBt("xnT", [128, 16 * TW], BF16)
                hf = SBt("hf", [128, 512])
                sg1 = [SBt(f"sg1{j}", [128, D]) for j in range(4)]
                h1cm = [None] * 4
                h1_s = [None] * 4
                for j in reversed(range(4)):
                    h1cm[j], h1_s[j] = RT(f"h1{j}", [128, HD], BF16)
                ibufcm = []
                ibuf = []
                for i in range(2):
                    cm, tt = RT(f"ibuf{i}", [128, 8 * 512], I16)
                    ibufcm.append(cm)
                    ibuf.append(tt)

                hXn = {}
                hTx = None
                hEvX = None
                for j, (off, rows) in enumerate(subs):
                    hDx = P.dma("d", [deps_in, hXn.get(j - 1)],
                                (lambda e, j=j:
                                 [e.dma_start(out=xb[:], in_=x1_scr[t * 512 + j * 128:t * 512 + (j + 1) * 128, :])]))
                    hRc = emit_ln(xb, rows, bn6t, bn2t, stq, [hDx], war_deps=[hXn.get(j - 1)])

                    def xapply(e):
                        e.activation(out=xn2[:], in_=xb[:], func=AF.Identity,
                                     bias=stq[:, 1:2], scale=stq[:, 3:4])
                        e.drain()
                        return e.activation(out=xnb2[:], in_=xn2[:], func=AF.Identity)
                    hXa = P.add("s", [hRc, hDx, hXn.get(j - 1)], xapply)
                    for cc in range(0, 16, 2):
                        def xt3(e, cc=cc):
                            last = None
                            for c in (cc, cc + 1):
                                last = e.matmul(pb[5 + c % 2][:, 0:128], xnb2[:, c * 128:(c + 1) * 128],
                                                ident[:], start=True, stop=True, skip_group_check=True)
                            return last
                        hT3 = P.add("t", [hXa if cc == 0 else None, hTx, hEvX], xt3)
                        hTx = hT3

                        def xev3(e, cc=cc, off=off):
                            for c in (cc, cc + 1):
                                e.tensor_copy(out=xnT[:, c * TW + off:c * TW + off + 128],
                                              in_=pb[5 + c % 2][:, 0:128])
                            return None
                        hEvX = P.add("v", [hT3], xev3)
                    hXn[j] = hEvX
                hL1 = P.barrier()

                hQld = None
                hEvH = {}
                hMM = {}
                lastf = {0: None, 1: None}
                for s in range(16):
                    for kh in range(2):
                        hQld = P.dma("g", [hQld, hL1 if s == 0 and kh == 0 else None, lastf[kh]],
                                     (lambda e, s=s, kh=kh, b=ibuf[kh]:
                                      [e.dma_start(out=b.ap().rearrange("p (c n) -> p c n", n=512),
                                                   in_=qfc_scr.rearrange("(c p) n -> p c n", p=128)
                                                   [:, kh * 8:(kh + 1) * 8, s * 512:(s + 1) * 512])]))

                        def mmfc(e, s=s, kh=kh):
                            last = None
                            for j in range(4):
                                for cc in range(8):
                                    c = kh * 8 + cc
                                    last = e.matmul(pb[j][:, 0:512],
                                                    xnT[:, c * TW + j * 128:c * TW + (j + 1) * 128],
                                                    ibuf[kh].ap().bitcast(BF16)[:, cc * 512:(cc + 1) * 512],
                                                    start=(c == 0), stop=(c == 15))
                            return last
                        hMMg = P.add("t", [hQld] + [hEvH.get((s - 1, j)) for j in range(4)]
                                     + ([hL1] if s == 0 and kh == 0 else []), mmfc)
                        hMM[(s, kh)] = hMMg
                        lastf[kh] = hMMg
                    for j in range(4):
                        def evh(e, s=s, j=j):
                            e.activation(out=hf[:, 0:512], in_=pb[j][:, 0:512], func=AF.Relu)
                            e.drain()
                            return e.activation(out=h1_s[j][:, s * 512:(s + 1) * 512], in_=hf[:, 0:512],
                                                func=AF.Square)
                        hEvH[(s, j)] = P.add("s", [hMM[(s, 1)], hEvH.get((s - 1, j)),
                                                   hEvH.get((s, j - 1)) or hEvH.get((s - 1, 3))], evh)
                hL2 = P.barrier()

                hRms = {}
                hTq2 = None
                for j in range(4):
                    def rstats(e, j=j):
                        for c in range(16):
                            e.bn_stats(out=bn6t[:, 6 * c:6 * c + 6], in_=h1_s[j][:, 512 * c:512 * (c + 1)])
                        e.drain()
                        e.bn_aggr(out=bn2t[:], in_=bn6t[:, 0:96])
                        e.drain()
                        e.tensor_tensor(out=stq[:, 0:1], in0=bn2t[:, 0:1], in1=bn2t[:, 0:1], op=TS.mult)
                        e.drain()
                        e.tensor_tensor(out=stq[:, 0:1], in0=stq[:, 0:1], in1=bn2t[:, 1:2], op=TS.add)
                        e.drain()
                        e.tensor_tensor(out=stq[:, 0:1], in0=stq[:, 0:1], in1=epsd[:, 0:1], op=TS.add)
                        return None
                    hRs = P.add("v", [hL2 if j == 0 else None, hRms.get(j - 1)], rstats, drain_first=True)
                    hSq = P.add("s", [hRs, hEvH.get((15, j))], lambda e: e.sqrt(out=stq[:, 2:3], in_=stq[:, 0:1]))
                    hRc4 = P.add("v", [hSq], lambda e: e.reciprocal(out=stq[:, 3:4], in_=stq[:, 2:3]))
                    hAp = P.add("v", [hRc4], (lambda e, j=j:
                                e.tensor_scalar(out=h1_s[j][:], in0=h1_s[j][:], scalar1=stq[:, 3:4],
                                                scalar2=None, op0=TS.mult, op1=TS.bypass)), drain_first=True)
                    hEvq2 = None
                    for cc in range(0, 64, 2):
                        def xt2(e, cc=cc, j=j):
                            last = None
                            for c in (cc, cc + 1):
                                last = e.matmul(pb[5 + c % 2][:, 0:128], h1_s[j][:, c * 128:(c + 1) * 128],
                                                ident[:], start=True, stop=True, skip_group_check=True)
                            return last
                        hT2 = P.add("t", [hAp if cc == 0 else None, hTq2, hEvq2], xt2)
                        hTq2 = hT2

                        def xev(e, cc=cc, j=j):
                            for c in (cc, cc + 1):
                                e.tensor_copy(out=h1_s[j][:, c * 128:(c + 1) * 128], in_=pb[5 + c % 2][:, 0:128])
                            return None
                        hEvq2 = P.add("v", [hT2], xev)
                    hRms[j] = hEvq2
                hL3 = P.barrier()

                hQld2 = None
                hMM2 = {}
                hEvG = {}
                lastp = {0: lastf[0], 1: lastf[1]}
                for s4 in range(4):
                    for kh in range(8):
                        hQld2 = P.dma("g", [hQld2, hL3 if s4 == 0 and kh == 0 else None, lastp[kh % 2]],
                                      (lambda e, s4=s4, kh=kh, b=ibuf[kh % 2]:
                                       [e.dma_start(out=b.ap().rearrange("p (c n) -> p c n", n=512),
                                                    in_=qproj_scr.rearrange("(c p) n -> p c n", p=128)
                                                    [:, kh * 8:(kh + 1) * 8, s4 * 512:(s4 + 1) * 512])]))

                        def mmpj(e, s4=s4, kh=kh):
                            last = None
                            for j in range(4):
                                for cc in range(8):
                                    c = kh * 8 + cc
                                    last = e.matmul(pb[j][:, 0:512],
                                                    h1_s[j][:, c * 128:(c + 1) * 128],
                                                    ibuf[kh % 2].ap().bitcast(BF16)[:, cc * 512:(cc + 1) * 512],
                                                    start=(c == 0), stop=(c == 63))
                            return last
                        hMMg = P.add("t", [hQld2] + [hEvG.get((s4 - 1, j)) for j in range(4)]
                                     + ([hL3] if s4 == 0 and kh == 0 else []), mmpj)
                        hMM2[(s4, kh)] = hMMg
                        lastp[kh % 2] = hMMg
                    for j in range(4):
                        def evg(e, s4=s4, j=j):
                            return e.activation(out=sg1[j][:, s4 * 512:(s4 + 1) * 512],
                                                in_=pb[j][:, 0:512], func=AF.Silu)
                        hEvG[(s4, j)] = P.add("s", [hMM2[(s4, 7)], hEvG.get((s4 - 1, j))], evg)
                hL4 = P.barrier()
                ibufcm[1].__exit__(None, None, None)
                ibufcm[0].__exit__(None, None, None)
                for j in range(4):
                    h1cm[j].__exit__(None, None, None)

                gt = [SBt(f"gt{j}", [128, 1024]) for j in range(4)]
                gT = SBt("gT", [128, 8 * TW], BF16)
                gmx = SBt("gmx", [128, 16])
                wgbcm = []
                wgb = []
                for i in range(2):
                    cm, tt = RT(f"wgb{i}", [128, 8 * 512], BF16)
                    wgbcm.append(cm)
                    wgb.append(tt)
                hWgd = None
                hGev = {}
                lastg = {0: None, 1: None}
                for sg4 in range(2):
                    for khg in range(2):
                        hWgd = P.dma("g", [hWgd, hL4 if sg4 == 0 and khg == 0 else None, lastg[khg]],
                                     (lambda e, sg4=sg4, khg=khg, b=wgb[khg]:
                                      [e.dma_start(out=b.ap().rearrange("p (c n) -> p c n", n=512),
                                                   in_=Wg1.rearrange("(c p) n -> p c n", p=128)
                                                   [:, khg * 8:(khg + 1) * 8, sg4 * 512:(sg4 + 1) * 512])]))

                        def gmm(e, sg4=sg4, khg=khg):
                            last = None
                            for j in range(4):
                                for cc in range(8):
                                    c = khg * 8 + cc
                                    last = e.matmul(pb[j][:, 0:512],
                                                    xnT[:, c * TW + j * 128:c * TW + (j + 1) * 128],
                                                    wgb[khg][:, cc * 512:(cc + 1) * 512],
                                                    start=(c == 0), stop=(c == 15))
                            return last
                        hGm = P.add("t", [hWgd, hL4 if sg4 == 0 and khg == 0 else None, hGev.get(sg4 - 1)], gmm)
                        lastg[khg] = hGm

                    def gev(e, sg4=sg4):
                        for j in range(4):
                            e.tensor_scalar(out=gt[j][:, sg4 * 512:(sg4 + 1) * 512], in0=pb[j][:, 0:512],
                                            scalar1=SC1, scalar2=None, op0=TS.mult, op1=TS.bypass)
                        return None
                    hGev[sg4] = P.add("v", [hGm], gev)
                hL5 = P.barrier()
                wgbcm[1].__exit__(None, None, None)
                wgbcm[0].__exit__(None, None, None)
                hSm2 = {}
                hTg = None
                hEvTg = None
                for j in range(4):
                    def gsmax(e, j=j):
                        e.tensor_reduce(out=gmx[:, 0:16], in_=gt[j].ap().rearrange("p (g k) -> p g k", k=64),
                                        axis=AX.X, op=TS.max, negate=True)
                        e.drain()
                        e.tensor_tensor(out=gt[j].ap().rearrange("p (g k) -> p g k", k=64),
                                        in0=gt[j].ap().rearrange("p (g k) -> p g k", k=64),
                                        in1=gmx.ap()[:, 0:16].unsqueeze(-1).broadcast_to((128, 16, 64)),
                                        op=TS.add)
                        return None
                    hG1 = P.add("v", [hL5 if j == 0 else None, hSm2.get(j - 1)], gsmax, drain_first=True)
                    hG2 = P.add("s", [hG1], (lambda e, j=j:
                                e.activation(out=gt[j][:], in_=gt[j][:], func=AF.Exp)))

                    def gnorm(e, j=j):
                        e.tensor_reduce(out=gmx[:, 0:16], in_=gt[j].ap().rearrange("p (g k) -> p g k", k=64),
                                        axis=AX.X, op=TS.add)
                        e.drain()
                        e.reciprocal(out=gmx[:, 0:16], in_=gmx[:, 0:16])
                        e.drain()
                        e.tensor_tensor(out=gt[j].ap().rearrange("p (g k) -> p g k", k=64),
                                        in0=gt[j].ap().rearrange("p (g k) -> p g k", k=64),
                                        in1=gmx.ap()[:, 0:16].unsqueeze(-1).broadcast_to((128, 16, 64)),
                                        op=TS.mult)
                        return None
                    hG3 = P.add("v", [hG2], gnorm, drain_first=True)
                    hG4 = P.add("s", [hG3], (lambda e, j=j:
                                e.activation(out=xnb2[:, 0:1024], in_=gt[j][:], func=AF.Identity)))
                    for cc in range(0, 8, 2):
                        def gtr(e, cc=cc):
                            last = None
                            for c in (cc, cc + 1):
                                last = e.matmul(pb[5 + c % 2][:, 0:128], xnb2[:, c * 128:(c + 1) * 128],
                                                ident[:], start=True, stop=True, skip_group_check=True)
                            return last
                        hT7 = P.add("t", [hG4 if cc == 0 else None, hTg, hEvTg], gtr)
                        hTg = hT7

                        def gtev(e, cc=cc, j=j):
                            for c in (cc, cc + 1):
                                e.tensor_copy(out=gT[:, c * TW + j * 128:c * TW + (j + 1) * 128],
                                              in_=pb[5 + c % 2][:, 0:128])
                            return None
                        hEvTg = P.add("v", [hT7], gtev)
                    hSm2[j] = hEvTg
                hL6 = P.barrier()

                wtd = SBt("wtd", [128, 16 * TW], BF16)
                hWt2 = {}
                hWe = {}
                worder = [h for h in range(16) if h % 2 == 0] + [h for h in range(16) if h % 2 == 1]
                last_bank_user = {}
                for i, h in enumerate(worder):
                    bank_dep = hWe.get(last_bank_user.get(h % 8))

                    def wmm(e, h=h, i=i):
                        if i == 8:
                            e.drain()
                        return e.matmul(
                            pb[h % 8][:, 0:TW],
                            attn_s[(h % 2) * 64:(h % 2) * 64 + 64, (h // 2) * 128:(h // 2) * 128 + 128],
                            gT[(h % 2) * 64:(h % 2) * 64 + 64, (h // 2) * TW:(h // 2) * TW + TW],
                            start=True, stop=True, skip_group_check=True)
                    hWt2[h] = P.add("t", [hL6 if i == 0 else None, bank_dep], wmm)
                    last_bank_user[h % 8] = h
                    hWe[h] = P.add("v", [hWt2[h]], (lambda e, h=h:
                                   e.tensor_copy(out=wtd[:, h * TW:(h + 1) * TW], in_=pb[h % 8][:, 0:TW])))
                hL7 = P.barrier()

                s1b = SBt("s1b", [128, D])
                wpbcm = []
                wpb = []
                for i in range(2):
                    cm, tt = RT(f"wpb{i}", [128, 8 * 512], BF16)
                    wpbcm.append(cm)
                    wpb.append(tt)
                s1 = [SBt(f"s1{j}", [128, D]) for j in range(4)]
                hWpd = None
                hWpe = {}
                lastw = {0: None, 1: None}
                for s4 in range(4):
                    for khp in range(2):
                        hWpd = P.dma("g", [hWpd, hL7 if s4 == 0 and khp == 0 else None, lastw[khp]],
                                     (lambda e, s4=s4, khp=khp, b=wpb[khp]:
                                      [e.dma_start(out=b.ap().rearrange("p (c n) -> p c n", n=512),
                                                   in_=Wp1.rearrange("(c p) n -> p c n", p=128)
                                                   [:, khp * 8:(khp + 1) * 8, s4 * 512:(s4 + 1) * 512])]))

                        def wpmm(e, s4=s4, khp=khp):
                            last = None
                            for j in range(4):
                                for cc in range(8):
                                    c = khp * 8 + cc
                                    last = e.matmul(pb[j][:, 0:512],
                                                    wtd[:, c * TW + j * 128:c * TW + (j + 1) * 128],
                                                    wpb[khp][:, cc * 512:(cc + 1) * 512],
                                                    start=(c == 0), stop=(c == 15))
                            return last
                        hWp3 = P.add("t", [hWpd, hL7 if s4 == 0 and khp == 0 else None,
                                           hWpe.get(s4 - 1)], wpmm)
                        lastw[khp] = hWp3

                    def wpev(e, s4=s4):
                        for j in range(4):
                            e.tensor_copy(out=s1[j][:, s4 * 512:(s4 + 1) * 512], in_=pb[j][:, 0:512])
                        return None
                    hWpe[s4] = P.add("v", [hWp3], wpev)
                hL8 = P.barrier()
                wpbcm[1].__exit__(None, None, None)
                wpbcm[0].__exit__(None, None, None)

                hCd = None
                hCb = None
                for j in range(4):
                    hDx = P.dma("d", [hL8, hCd, hCb],
                                (lambda e, j=j:
                                 [e.dma_start(out=xb[:], in_=x1_scr[t * 512 + j * 128:t * 512 + (j + 1) * 128, :])]))

                    def comb(e, j=j):
                        e.tensor_tensor(out=sg1[j][:], in0=sg1[j][:], in1=s1[j][:], op=TS.mult)
                        e.drain()
                        e.tensor_tensor(out=sg1[j][:], in0=sg1[j][:], in1=xb[:], op=TS.add)
                        return None
                    hCb = P.add("v", [hDx, hL8], comb)
                    hCd = P.dma("d", [hCb], (lambda e, j=j:
                                [e.dma_start(out=out[t * 512 + j * 128:t * 512 + (j + 1) * 128, :],
                                             in_=sg1[j][:])]))
            return [hCd]


        if PHASES >= 5:
            for t in range(NTILE):
                hP4 = emit_l1_tile(f"u{t}", t, hP4)

        P.add("g", hP4 if isinstance(hP4, list) else [hP4], lambda e: e.nop(), inc=0)

        block = st.enter_context(nc.Block())
        P.play(block)
    return nc


def _f32c(a):
    return np.ascontiguousarray(a, dtype=np.float32)


def make_constants():
    half = 64
    inv_freq = 1.0 / (10000.0 ** (np.arange(0, half, dtype=np.float64) * 2.0 / 128.0))
    ang = np.arange(64, dtype=np.float64)[:, None] * inv_freq[None, :]
    cos = np.cos(ang).astype(np.float32)
    sin = np.sin(ang).astype(np.float32)
    cos16 = np.tile(cos, (1, 16))
    sin16 = np.tile(sin, (1, 16))
    maskadd = np.where(np.arange(64)[None, :] <= np.arange(64)[:, None], 0.0, NEG).astype(np.float32)
    ident = np.eye(128, dtype=np.float32)
    gsum = np.zeros((128, 128), np.float32)
    gsum[0:64, 0] = 1.0
    gsum[64:128, 1] = 1.0
    grep = np.zeros((128, 128), np.float32)
    grep[0, 0:64] = 1.0
    grep[1, 64:128] = 1.0
    ones1 = np.ones((1, 128), np.float32)
    onesc = np.ones((128, 1), np.float32)
    return dict(c_cos=cos16, c_sin=sin16, c_mask=maskadd, c_ident=ident,
                c_gsum=gsum, c_grep=grep, c_ones1=ones1, c_onesc=onesc)


def make_in_map(inputs, core):
    b, half = core // 2, core % 2
    x = inputs["L0_attn_hidden_states"]
    lq = inputs["L1_attn_latent_queries"]
    lqT = np.ascontiguousarray(np.transpose(lq[0], (1, 2, 0)).reshape(16, 128, 64)
                               .transpose(1, 0, 2).reshape(128, 16 * 64))
    m = dict(
        x_seq=_f32c(x[b]),
        x_own=_f32c(x[b, half * OWN:(half + 1) * OWN]),
        Wq=_f32c(inputs["L0_attn_Wq"]), Wk=_f32c(inputs["L0_attn_Wk"]),
        Wv=_f32c(inputs["L0_attn_Wv"]), Wo=_f32c(inputs["L0_attn_Wo"]),
        bo_row=_f32c(inputs["L0_attn_bo"][None, :]),
        w1T=_f32c(np.transpose(inputs["L0_ffn_w1"])),
        w2T=_f32c(np.transpose(inputs["L0_ffn_w2"])),
        b1_row=_f32c(inputs["L0_ffn_b1"][None, :]),
        b2_row=_f32c(inputs["L0_ffn_b2"][None, :]),
        g_rep=_f32c(np.broadcast_to(inputs["L0_ffn_ln_g"][None, :], (128, HD))),
        b_rep=_f32c(np.broadcast_to(inputs["L0_ffn_ln_b"][None, :], (128, HD))),
        lqT=_f32c(lqT),
        Wk1=_f32c(inputs["L1_attn_Wk"]), Wv1=_f32c(inputs["L1_attn_Wv"]),
        Wg1=_f32c(inputs["L1_attn_Wg"]), Wp1=_f32c(inputs["L1_attn_Wp"]),
        WfcT=_f32c(np.transpose(inputs["L1_ffn_W_fc"])),
        WprojT=_f32c(np.transpose(inputs["L1_ffn_W_proj"])),
    )
    m.update(make_constants())
    return m


def kernel(**inputs):
    nc = build_program()
    in_maps = [make_in_map(inputs, c) for c in range(8)]
    res = run_bass_kernel_spmd(nc, in_maps, core_ids=list(range(8)))
    halves = [r["out"] for r in res.results]
    seqs = [np.concatenate([halves[2 * b], halves[2 * b + 1]], axis=0) for b in range(4)]
    return np.stack(seqs, axis=0)


# version 10
# speedup vs baseline: 1.0238x; 1.0065x over previous
import contextlib
import numpy as np
import concourse.bass as bass
import concourse.mybir as mybir
from concourse.alu_op_type import AluOpType
from concourse.bass_utils import run_bass_kernel_spmd

F32 = mybir.dt.float32
BF16 = mybir.dt.bfloat16
I16 = mybir.dt.int16
AF = mybir.ActivationFunctionType
AX = mybir.AxisListType
TS = AluOpType

D = 2048
T = 4096
OWN = 2048
HD = 8192
NTILE = 4
NSUB = 32
NEG = -1.0e9
SC1 = float(1.0 / np.sqrt(128.0))
M32 = 1.5 * 2.0 ** 23
M64 = 1.5 * 2.0 ** 52
E32 = float(np.finfo(np.float32).eps)
E64 = float(np.finfo(np.float64).eps)

ENG = {"t": "tensor", "v": "vector", "s": "scalar", "d": "sync", "g": "gpsimd",
       "e": "scalar"}

PHASES = 99
DEBUG_TAPS = False
P2TILES = 99
DRAINREP = False
PUMPL1 = 7
PUMPSET = 255
L0STOP = 'Z'


class Prog:

    def __init__(self, nc, stack):
        self.nc = nc
        self.sem = {e: stack.enter_context(nc.semaphore("sem_" + e)) for e in ENG}
        self.cnt = {e: 0 for e in ENG}
        self.ops = {e: [] for e in ENG}
        self.seq = 0

    def add(self, eng, deps, fn, inc=1, self_inc=False, drain_first=False):
        flat = []

        def _fl(d):
            if d is None:
                return
            if isinstance(d, list):
                for x in d:
                    _fl(x)
            else:
                flat.append(d)
        _fl(deps)
        deps = flat
        self.ops[eng].append((self.seq, eng, deps, fn, inc, self_inc, drain_first))
        self.seq += 1
        self.cnt[eng] += inc
        return (eng, self.cnt[eng])

    def dma(self, eng, deps, fn, n=1):
        start_cnt = self.cnt[eng]

        def body(e):
            if start_cnt > 0:
                e.wait_ge(self.sem[eng], start_cnt)
            insts = fn(e)
            assert len(insts) == n, f"dma group: expected {n}, got {len(insts)}"
            for i in insts:
                i.then_inc(self.sem[eng], 16)
            return "self"
        return self.add(eng, deps, body, inc=16 * n, self_inc=True)

    def barrier(self):
        return [(e, self.cnt[e]) for e in ENG if self.cnt[e] > 0]

    def play(self, block):
        sem = self.sem
        phys_keys = {}
        for e in ENG:
            phys_keys.setdefault(ENG[e], []).append(e)
        for phys, keys in phys_keys.items():
            ops = sorted((op for k in keys for op in self.ops[k]), key=lambda t: t[0])
            if not ops:
                continue

            def make_body(phys, keys, ops):
                def body(eng):
                    water = {}
                    for _seq, e, deps, fn, inc, self_inc, drain_first in ops:
                        drain = drain_first
                        for de, dv in deps:
                            if de == e:
                                drain = True
                            elif water.get(de, 0) < dv:
                                eng.wait_ge(sem[de], dv)
                                water[de] = dv
                        if drain:
                            eng.drain()
                        last = fn(eng)
                        if not self_inc and inc > 0:
                            if last is None:
                                eng.drain()
                                last = eng.nop()
                            last.then_inc(sem[e], inc)
                return body
            getattr(block, phys)(make_body(phys, keys, ops))


def build_program():
    nc = bass.Bass("TRN2", target_bir_lowering=False)
    dr = lambda n, s, dt=F32, kind="ExternalInput": nc.dram_tensor(n, s, dt, kind=kind).ap()

    x_seq = dr("x_seq", [T, D])
    x_own = dr("x_own", [OWN, D])
    Wq = dr("Wq", [D, D]); Wk = dr("Wk", [D, D]); Wv = dr("Wv", [D, D]); Wo = dr("Wo", [D, D])
    bo_row = dr("bo_row", [1, D])
    w1T = dr("w1T", [D, HD]); w2T = dr("w2T", [HD, D])
    b1_row = dr("b1_row", [1, HD]); b2_row = dr("b2_row", [1, D])
    g_rep = dr("g_rep", [128, HD]); b_rep = dr("b_rep", [128, HD])
    lqT = dr("lqT", [128, 16 * 64])
    Wk1 = dr("Wk1", [D, D]); Wv1 = dr("Wv1", [D, D])
    Wg1 = dr("Wg1", [D, 1024]); Wp1 = dr("Wp1", [D, D])
    WfcT = dr("WfcT", [D, HD]); WprojT = dr("WprojT", [HD, D])
    c_ident = dr("c_ident", [128, 128])
    c_gsum = dr("c_gsum", [128, 128])
    c_grep = dr("c_grep", [128, 128])
    c_ones1 = dr("c_ones1", [1, 128])
    c_onesc = dr("c_onesc", [128, 1])
    c_cos = dr("c_cos", [64, 16 * 64]); c_sin = dr("c_sin", [64, 16 * 64])
    c_mask = dr("c_mask", [64, 64])
    out = dr("out", [OWN, D], kind="ExternalOutput")

    x1_scr = dr("x1_scr", [OWN, D], kind="Internal")
    x1d_scr = dr("x1d_scr", [64, D], kind="Internal")
    sg_scr = dr("sg_scr", [OWN, D], kind="Internal")
    taps = [nc.dram_tensor(f"tap{i}", [128, HD], F32, kind="Internal").ap() for i in range(4)] if DEBUG_TAPS else None
    sgd_scr = dr("sgd_scr", [64, D], kind="Internal")
    qw1_scr = dr("qw1_scr", [D, HD], BF16, kind="Internal")
    qw2_scr = dr("qw2_scr", [HD, D], BF16, kind="Internal")
    qfc_scr = dr("qfc_scr", [D, HD], I16, kind="Internal")
    qproj_scr = dr("qproj_scr", [HD, D], I16, kind="Internal")
    sc_scr = dr("sc_scr", [4, 2048], BF16, kind="Internal")
    wcol_scr = dr("wcol_scr", [2, 128], kind="Internal")
    wtmp_scr = dr("wtmp_scr", [2, 2], kind="Internal")

    st = contextlib.ExitStack()
    with st:
        SB = lambda n, s, dt=F32: st.enter_context(nc.sbuf_tensor(n, s, dt))
        ident = SB("ident", [128, 128], BF16)
        identf = SB("identf", [128, 128])
        gsum = SB("gsum", [128, 128], BF16)
        grepl = SB("grepl", [128, 128], BF16)
        ones1f = SB("ones1f", [1, 128])
        ones1b = SB("ones1b", [1, 128], BF16)
        onescf = SB("onescf", [128, 1])
        bo16 = SB("bo16", [1, D], BF16)
        mask_s = SB("mask_s", [64, 64])
        ind = SB("ind", [128, 1]); magic = SB("magic", [128, 1]); epsd = SB("epsd", [128, 1])
        kv_ext = SB("kv_ext", [128, 16 * 65], BF16)
        attn_s = SB("attn_s", [128, 8 * 128], BF16)
        wmrow = SB("wmrow", [128, 4])
        pb = [st.enter_context(nc.psum_tensor(f"pb{i}", [128, 512], F32)) for i in range(8)]

        P = Prog(nc, st)

        hD0 = P.dma("d", [], lambda e: [
            e.dma_start(out=mask_s[:], in_=c_mask),
            e.dma_start(out=ones1f[:], in_=c_ones1),
            e.dma_start(out=onescf[:], in_=c_onesc),
            e.dma_start(out=identf[:], in_=c_ident)], n=4)
        hG0 = P.dma("g", [], lambda e: [
            e.dma_start(out=ident[:], in_=c_ident),
            e.dma_start(out=gsum[:], in_=c_gsum),
            e.dma_start(out=grepl[:], in_=c_grep),
            e.dma_start(out=ones1b[:], in_=c_ones1),
            e.dma_start(out=bo16[:], in_=bo_row)], n=5)

        def p0v(e):
            e.memset(ind[:], 1.0)
            e.drain()
            e.tensor_scalar(out=ind[:], in0=ind[:], scalar1=float(2.0 ** -25), scalar2=-1.0,
                            op0=TS.add, op1=TS.add)
            e.drain()
            e.tensor_scalar(out=ind[:], in0=ind[:], scalar1=float(2.0 ** 25), scalar2=None,
                            op0=TS.mult, op1=TS.bypass)
            e.drain()
            e.tensor_scalar(out=magic[:], in0=ind[:], scalar1=float(M64 - M32),
                            scalar2=float(M32), op0=TS.mult, op1=TS.add)
            e.tensor_scalar(out=epsd[:], in0=ind[:], scalar1=float(E64 - E32),
                            scalar2=float(E32), op0=TS.mult, op1=TS.add)
            return None
        hV0 = P.add("v", [], p0v)

        def emit_ln(src, rows, bn6, bn2, statc, deps, war_deps=()):
            def stats(e):
                for c in range(4):
                    e.bn_stats(out=bn6[0:rows, 6 * c:6 * c + 6], in_=src[0:rows, 512 * c:512 * (c + 1)])
                e.drain()
                e.bn_aggr(out=bn2[0:rows, :], in_=bn6[0:rows, 0:24])
                e.drain()
                e.tensor_scalar(out=statc[0:rows, 0:1], in0=bn2[0:rows, 1:2], scalar1=1e-5,
                                scalar2=None, op0=TS.add, op1=TS.bypass)
                return None
            hSt = P.add("v", list(deps) + list(war_deps), stats, drain_first=True)
            hSq = P.add("s", [hSt], lambda e: e.sqrt(out=statc[0:rows, 2:3], in_=statc[0:rows, 0:1]))

            def recip(e):
                e.reciprocal(out=statc[0:rows, 3:4], in_=statc[0:rows, 2:3])
                e.drain()
                e.tensor_tensor(out=statc[0:rows, 1:2], in0=bn2[0:rows, 0:1], in1=statc[0:rows, 3:4], op=TS.mult)
                e.drain()
                e.tensor_scalar(out=statc[0:rows, 1:2], in0=statc[0:rows, 1:2], scalar1=-1.0, scalar2=None,
                                op0=TS.mult, op1=TS.bypass)
                return None
            return P.add("v", [hSq], recip)

        hP1 = P.barrier()
        hWm1 = hWm2 = None
        if PHASES >= 1:
            qs_ = contextlib.ExitStack()
            SBq = lambda n, s, dt=F32: qs_.enter_context(nc.sbuf_tensor(n, s, dt))
            wbuf = [SBq(f"q_wbuf{i}", [128, 2048]) for i in range(2)]
            wqb2 = [SBq(f"q_wqb{i}", [128, 2048], BF16) for i in range(2)]
            wcol = SBq("q_wcol", [128, 128])
            wtmp = SBq("q_wtmp", [1, 4])
            wrow = SBq("q_wrow", [1, 128])
            negm = SBq("q_negm", [128, 1])
            l1b = [SBq(f"q_l1b{i}", [128, 2048], BF16) for i in range(2)]
            wab = [SBq(f"q_wab{i}", [128, 2048], BF16) for i in range(2)]
            sc16q = [SBq(f"q_sc{i}", [128, 2048], BF16) for i in range(2)]
            screp = [SBq(f"q_srp{i}", [128, 2048], BF16) for i in range(2)]
            cmpa = SBq("q_cmpa", [128, 2048])
            cmpb = SBq("q_cmpb", [128, 2048])
            wsgn = SBq("q_wsgn", [128, 2048], BF16)
            qoutb = SBq("q_qout", [128, 2048], BF16)

            def qzero(e):
                e.memset(wcol[:], 0.0)
                e.drain()
                e.memset(sc16q[0][:], 0.0)
                e.drain()
                e.memset(sc16q[1][:], 0.0)
                return None
            hQz = P.add("v", [hV0], qzero)
            hNm = P.add("v", [hQz], lambda e: e.tensor_scalar(
                out=negm[:], in0=magic[:], scalar1=-1.0, scalar2=None,
                op0=TS.mult, op1=TS.bypass))

            QS = {'wl': [None, None], 'wq': [None, None], 'acc': [hQz, hQz],
                  'b7g': None, 'b7r': None, 'l1l': [None, None], 'wabl': [None, None],
                  'scl': [hQz, hQz], 'srpl': [None, None],
                  'cmpa': None, 'cmpb': None, 'wsg': None, 'qo': None,
                  'si': 0, 'l1i': 0}

            def l0p1_item(wi, wt, rb, cs, q):
                slot = QS['si'] % 2; QS['si'] += 1
                b = wbuf[slot]
                hLd = P.dma(q, [QS['wl'][slot]],
                            (lambda e, b=b, wt=wt, rb=rb, cs=cs:
                             [e.dma_start(out=b[:], in_=wt[rb * 128:(rb + 1) * 128, cs:cs + 2048])]))

                def absred(e, b=b, wi=wi):
                    e.tensor_reduce(out=wcol[:, 2 * wi + 1:2 * wi + 2], in_=b[:],
                                    axis=AX.X, op=TS.add, apply_absolute_value=True)
                    e.drain()
                    e.tensor_tensor(out=wcol[:, 2 * wi:2 * wi + 1], in0=wcol[:, 2 * wi:2 * wi + 1],
                                    in1=wcol[:, 2 * wi + 1:2 * wi + 2], op=TS.add)
                    return None
                hAb = P.add("v", [hLd, QS['acc'][wi]], absred, drain_first=True)
                QS['acc'][wi] = hAb
                QS['wl'][slot] = hAb

            def wmean_item(wi):
                hS1 = P.dma("e", [QS['acc'][wi]],
                            (lambda e, wi=wi:
                             [e.dma_start(out=wcol_scr[wi:wi + 1, 0:128].rearrange("one p -> p one"),
                                          in_=wcol[:, 2 * wi:2 * wi + 1])]))
                hL1 = P.dma("e", [hS1, QS.get('wmrd')],
                            (lambda e, wi=wi:
                             [e.dma_start(out=wrow[:], in_=wcol_scr[wi:wi + 1, 0:128])]))

                def wmcalc(e, wi=wi):
                    e.tensor_reduce(out=wtmp[0:1, 2 * wi:2 * wi + 1], in_=wrow[0:1, :],
                                    axis=AX.X, op=TS.add)
                    e.drain()
                    e.tensor_scalar(out=wtmp[0:1, 2 * wi:2 * wi + 1], in0=wtmp[0:1, 2 * wi:2 * wi + 1],
                                    scalar1=float(1.0 / (D * HD)), scalar2=1e-5,
                                    op0=TS.mult, op1=TS.max)
                    e.drain()
                    e.reciprocal(out=wtmp[0:1, 2 * wi + 1:2 * wi + 2], in_=wtmp[0:1, 2 * wi:2 * wi + 1])
                    return None
                hWmv = P.add("v", [hL1], wmcalc, drain_first=True)
                QS['wmrd'] = hWmv
                hS2 = P.dma("e", [hWmv], (lambda e, wi=wi:
                            [e.dma_start(out=wtmp_scr[wi:wi + 1, :], in_=wtmp[0:1, 2 * wi:2 * wi + 2])]))
                hWmr = P.dma("e", [hS2], (lambda e, wi=wi:
                             [e.dma_start(out=wmrow[:, 2 * wi:2 * wi + 2],
                                          in_=wtmp_scr[wi:wi + 1, :].broadcast_to([128, 2]))]))
                return hWmr

            def l0p2_item(wi, wt, scr, rb, cs, q):
                slot = QS['si'] % 2; QS['si'] += 1
                b = wbuf[slot]
                hLd = P.dma(q, [QS['wl'][slot]],
                            (lambda e, b=b, wt=wt, rb=rb, cs=cs:
                             [e.dma_start(out=b[:], in_=wt[rb * 128:(rb + 1) * 128, cs:cs + 2048])]))

                def mround(e, b=b, wi=wi):
                    e.activation(out=b[:], in_=b[:], func=AF.Identity,
                                 bias=magic[:, 0:1], scale=wmrow[:, 2 * wi + 1:2 * wi + 2])
                    e.drain()
                    e.activation(out=b[:], in_=b[:], func=AF.Identity, bias=negm[:, 0:1])
                    return None
                hMr = P.add("s", [hLd, QS['wm1'] if wi == 0 else QS['wm2'], hNm], mround)

                def clip1(e, b=b, slot=slot):
                    e.tensor_scalar(out=wqb2[slot][:], in0=b[:], scalar1=-1.0, scalar2=1.0,
                                    op0=TS.max, op1=TS.min)
                    return None
                hCl = P.add("v", [hMr, QS['wq'][slot]], clip1)
                QS['wl'][slot] = hCl
                hSt = P.dma(q, [hCl], (lambda e, slot=slot, scr=scr, rb=rb, cs=cs:
                                       [e.dma_start(out=scr[rb * 128:(rb + 1) * 128, cs:cs + 2048],
                                                    in_=wqb2[slot][:])]))
                QS['wq'][slot] = hSt

            def l1_item(kind, u):
                slot = QS['l1i'] % 2; QS['l1i'] += 1
                if kind == 0:
                    wt, scr, rb, cs = WfcT, qfc_scr, u // 4, (u % 4) * 2048
                else:
                    wt, scr, rb, cs = WprojT, qproj_scr, u, 0
                b = l1b[slot]
                hLd = P.dma("g", [QS['l1l'][slot]],
                            (lambda e, b=b, wt=wt, rb=rb, cs=cs:
                             [e.dma_start(out=b[:], in_=wt[rb * 128:(rb + 1) * 128, cs:cs + 2048])]))
                if PUMPL1 < 2:
                    QS['l1l'][slot] = hLd
                    return
                hAbs = P.add("s", [hLd, QS['wabl'][slot], hG0],
                             (lambda e, b=b, slot=slot: e.activation(out=wab[slot][:], in_=b[:],
                                                                     func=AF.Abs)))
                if PUMPL1 < 3:
                    QS['l1l'][slot] = hAbs
                    QS['wabl'][slot] = hAbs
                    return
                hQm = None
                hQms = []
                for sl in range(8):
                    hMm = P.add("t", [hAbs if sl == 0 else None, QS['b7g'] if sl == 0 else hQm],
                                (lambda e, slot=slot, sl=sl:
                                 e.matmul(pb[7][0:128, 0:256], gsum[:],
                                          wab[slot][:, sl * 256:(sl + 1) * 256],
                                          start=True, stop=True, skip_group_check=True)))
                    hQm = P.add("v", [hMm, QS['scl'][slot] if sl == 0 else None],
                                (lambda e, slot=slot, sl=sl:
                                 e.tensor_scalar(out=sc16q[slot][0:2, sl * 256:(sl + 1) * 256],
                                                 in0=pb[7][0:2, 0:256], scalar1=float(1.0 / 64.0),
                                                 scalar2=1e-8, op0=TS.mult, op1=TS.max)))
                    hQms.append(hQm)
                if PUMPL1 < 4:
                    QS['b7g'] = hQm
                    QS['scl'][slot] = hQm
                    QS['l1l'][slot] = hAbs
                    QS['wabl'][slot] = hQm
                    return
                nrep = 1 if PUMPL1 == 41 else 8
                rcol = 0 if PUMPL1 == 43 else 256
                hEv = None
                for sl in range(nrep):
                    hRp = P.add("t", [hQms[sl], hQms[-1] if PUMPL1 in (41, 43) else None, hEv,
                                      [QS['b7r'], QS['srpl'][slot]] if sl == 0 else None],
                                (lambda e, slot=slot, sl=sl, rcol=rcol:
                                 e.matmul(pb[7][0:128, rcol:rcol + 256], grepl[:],
                                          sc16q[slot][0:128, sl * 256:(sl + 1) * 256],
                                          start=True, stop=True, skip_group_check=True)),
                                drain_first=(DRAINREP and sl == 0))

                    if PUMPL1 == 35:
                        hEv = P.add("v", [hRp], (lambda e, slot=slot, sl=sl:
                                     e.tensor_copy(out=screp[slot][0:2, sl * 256:sl * 256 + 2],
                                                   in_=pb[7][0:2, 256:258])))
                    else:
                        hEv = P.add("v", [hRp], (lambda e, slot=slot, sl=sl:
                                     e.tensor_copy(out=screp[slot][:, sl * 256:(sl + 1) * 256],
                                                   in_=pb[7][0:128, 256:512])))
                QS['b7g'] = hQm
                QS['b7r'] = hEv
                QS['scl'][slot] = hEv
                if PUMPL1 < 5 or PUMPL1 == 35:
                    QS['l1l'][slot] = hAbs
                    QS['wabl'][slot] = hQm
                    QS['srpl'][slot] = hEv
                    return

                def sacts(e, slot=slot):
                    e.tensor_scalar(out=cmpa[:], in0=screp[slot][:], scalar1=0.5,
                                    scalar2=None, op0=TS.mult, op1=TS.bypass)
                    e.drain()
                    e.tensor_scalar(out=cmpb[:], in0=screp[slot][:],
                                    scalar1=float(2.0 ** -9 + 2.0 ** -25),
                                    scalar2=None, op0=TS.mult, op1=TS.bypass)
                    return None
                hSa = P.add("v", [hEv, QS['cmpa'], QS['cmpb']], sacts, drain_first=True)

                def dsub(e, slot=slot):
                    e.tensor_tensor(out=cmpa[:], in0=wab[slot][:], in1=cmpa[:], op=TS.subtract)
                    e.drain()
                    e.tensor_tensor(out=cmpa[:], in0=cmpa[:], in1=cmpb[:], op=TS.subtract)
                    e.drain()
                    e.tensor_scalar(out=cmpa[:], in0=cmpa[:], scalar1=0.0,
                                    scalar2=None, op0=TS.max, op1=TS.bypass)
                    return None
                hDs = P.add("v", [hSa, hAbs], dsub, drain_first=True)
                QS['cmpb'] = hDs
                if PUMPL1 < 6:
                    QS['cmpa'] = hDs
                    QS['l1l'][slot] = hAbs
                    QS['wabl'][slot] = hDs
                    QS['srpl'][slot] = hSa
                    return

                def signs(e, b=b):
                    e.activation(out=cmpa[:], in_=cmpa[:], func=AF.Sign)
                    e.drain()
                    return e.activation(out=wsgn[:], in_=b[:], func=AF.Sign)
                hSg = P.add("s", [hDs, QS['wsg'], hLd], signs)
                QS['l1l'][slot] = hSg

                def qmul(e, slot=slot):
                    e.tensor_tensor(out=wab[slot][:], in0=cmpa[:], in1=wsgn[:], op=TS.mult)
                    e.drain()
                    e.tensor_tensor(out=qoutb[:], in0=wab[slot][:], in1=screp[slot][:], op=TS.mult)
                    return None
                hQl = P.add("v", [hSg, QS['qo'], hEv], qmul, drain_first=True)
                QS['cmpa'] = hQl
                QS['wsg'] = hQl
                QS['wabl'][slot] = hQl
                QS['srpl'][slot] = hQl
                if PUMPL1 < 7:
                    return
                hSo = P.dma("e", [hQl], (lambda e, scr=scr, rb=rb, cs=cs:
                                         [e.dma_start(out=scr[rb * 128:(rb + 1) * 128, cs:cs + 2048],
                                                      in_=qoutb.ap().bitcast(I16)[:])]))
                QS['qo'] = hSo

            pump_sched = []
            if PHASES >= 2:
                if PUMPSET & 1:
                    for u in range(64):
                        pump_sched.append((u // 4, lambda u=u: l0p1_item(0, w1T, u // 4, (u % 4) * 2048, "e")))
                if PUMPSET & 2:
                    for u in range(64):
                        pump_sched.append((u // 2, lambda u=u: l1_item(0, u)))
                if PUMPSET & 4:
                    for u in range(64):
                        pump_sched.append((2 + u // 2, lambda u=u: l0p1_item(1, w2T, u, 0, "d")))

                def do_wm1():
                    QS['wm1'] = wmean_item(0)
                if PUMPSET & 8:
                    pump_sched.append((16, do_wm1))
                if PUMPSET & 16:
                    for u in range(64):
                        pump_sched.append((17 + u // 2, lambda u=u:
                                           l0p2_item(0, w1T, qw1_scr, u // 4, (u % 4) * 2048, "e")))
                if PUMPSET & 32:
                    for u in range(64):
                        pump_sched.append((32 + u // 2, lambda u=u: l1_item(1, u)))

                def do_wm2():
                    QS['wm2'] = wmean_item(1)
                if PUMPSET & 64:
                    pump_sched.append((34, do_wm2))
                if PUMPSET & 128:
                    for u in range(64):
                        pump_sched.append((35 + u // 2, lambda u=u:
                                           l0p2_item(1, w2T, qw2_scr, u, 0, "d")))
            pump_sched.sort(key=lambda x: x[0])

            def pump(it_now):
                while pump_sched and pump_sched[0][0] <= it_now:
                    pump_sched.pop(0)[1]()

            with contextlib.ExitStack() as p1s:
                SB1 = lambda n, s, dt=F32: p1s.enter_context(nc.sbuf_tensor(n, s, dt))
                wk_s = SB1("wk_s", [128, 16 * 1024], BF16)
                wv_s = SB1("wv_s", [128, 16 * 1024], BF16)
                xt = [SB1(f"p1_xt{i}", [128, D]) for i in range(3)]
                xn1p = SB1("p1_xn", [128, D])
                xnb1p = [SB1(f"p1_xnb{i}", [128, D], BF16) for i in range(2)]
                xnT1p = SB1("p1_xnT", [128, 16 * 128], BF16)
                kt1p = SB1("p1_kt", [128, 16 * 64], BF16)
                vt1p = SB1("p1_vt", [128, 16 * 65], BF16)
                kvst = SB1("p1_kvst", [64, 16 * 65], BF16)
                bn6 = SB1("p1_bn6", [128, 24])
                bn2 = SB1("p1_bn2", [128, 2])
                statc = SB1("p1_stat", [128, 4])

                hVones = P.add("v", [hV0], lambda e: e.memset(
                    vt1p.ap().rearrange("p (h c) -> p h c", c=65)[:, :, 64:65], 1.0))

                hWld = None
                hXa = None
                hVxT = None
                hEvK = None
                hEvV = None
                hKVa = None
                hXa_hist = {}
                hTt_hist = {}
                for pss in range(2):
                    def wload(e, pss=pss):
                        return [e.dma_start(out=wk_s.ap().rearrange("p (a b) -> p a b", b=1024),
                                            in_=Wk[:, pss * 1024:(pss + 1) * 1024]
                                            .rearrange("(a p) b -> p a b", p=128)),
                                e.dma_start(out=wv_s.ap().rearrange("p (a b) -> p a b", b=1024),
                                            in_=Wv[:, pss * 1024:(pss + 1) * 1024]
                                            .rearrange("(a p) b -> p a b", p=128))]
                    hWld = P.dma("g", [hG0 if pss == 0 else hKVa], wload, n=2)
                    for i in range(NSUB):
                        it = pss * NSUB + i
                        buf = xt[it % 3]
                        hDx = P.dma("d", [hD0, hXa_hist.get(it - 3)],
                                    (lambda e, i=i, buf=buf: [e.dma_start(out=buf[:], in_=x_seq[i * 128:(i + 1) * 128, :])]))
                        hRc = emit_ln(buf, 128, bn6, bn2, statc, [hDx], war_deps=[hXa])

                        def xapply(e, buf=buf, it=it):
                            e.activation(out=xn1p[:], in_=buf[:], func=AF.Identity,
                                         bias=statc[:, 1:2], scale=statc[:, 3:4])
                            e.drain()
                            return e.activation(out=xnb1p[it % 2][:], in_=xn1p[:], func=AF.Identity)
                        hXa = P.add("s", [hRc, hDx, hTt_hist.get(it - 2)], xapply)
                        hXa_hist[it] = hXa

                        def trans(e, it=it):
                            last = None
                            for c in range(16):
                                last = e.matmul(pb[c // 4][:, (c % 4) * 128:(c % 4) * 128 + 128],
                                                xnb1p[it % 2][:, c * 128:(c + 1) * 128], ident[:],
                                                start=True, stop=True, skip_group_check=True)
                            return last
                        hTt = P.add("t", [hXa, hEvK, hEvV, hG0], trans)
                        hTt_hist[it] = hTt

                        def evT(e):
                            for q in range(4):
                                e.tensor_copy(out=xnT1p[:, 512 * q:512 * (q + 1)], in_=pb[q][:, 0:512])
                            return None
                        hVxT = P.add("v", [hTt], evT)

                        def proj(e):
                            last = None
                            for c in range(16):
                                s0, s1 = (c == 0), (c == 15)
                                lhsT = xnT1p[:, c * 128:c * 128 + 128]
                                last = e.matmul(pb[0][:], lhsT, wk_s[:, c * 1024:c * 1024 + 512], start=s0, stop=s1)
                                last = e.matmul(pb[1][:], lhsT, wk_s[:, c * 1024 + 512:c * 1024 + 1024], start=s0, stop=s1)
                                last = e.matmul(pb[2][:], lhsT, wv_s[:, c * 1024:c * 1024 + 512], start=s0, stop=s1)
                                last = e.matmul(pb[3][:], lhsT, wv_s[:, c * 1024 + 512:c * 1024 + 1024], start=s0, stop=s1)
                            return last
                        hPr = P.add("t", [hVxT, hWld], proj)

                        def evK(e):
                            e.tensor_scalar(out=kt1p[:, 0:512], in0=pb[0][:], scalar1=0.0,
                                            scalar2=None, op0=TS.max, op1=TS.bypass)
                            e.drain()
                            e.tensor_scalar(out=kt1p[:, 512:1024], in0=pb[1][:], scalar1=0.0,
                                            scalar2=None, op0=TS.max, op1=TS.bypass)
                            return None
                        hEvK = P.add("v", [hPr, hKVa], evK, drain_first=True)

                        def evV(e):
                            for q in range(2):
                                e.tensor_copy(
                                    out=vt1p.ap().rearrange("p (h c) -> p h c", c=65)[:, 8 * q:8 * (q + 1), 0:64],
                                    in_=pb[2 + q].ap().rearrange("p (h c) -> p h c", c=64))
                            return None
                        hEvV = P.add("v", [hPr, hVones, hKVa], evV)

                        def kvacc(e, i=i):
                            last = None
                            for h in range(16):
                                bank = pb[4 + h // 6]
                                o = (h % 6) * 85
                                last = e.matmul(bank[0:64, o:o + 65],
                                                kt1p[:, h * 64:(h + 1) * 64],
                                                vt1p[:, h * 65:(h + 1) * 65],
                                                start=(i == 0 and h % 6 == 0),
                                                stop=(i == NSUB - 1 and h in (5, 11, 15)))
                            return last
                        hKVa = P.add("t", [hEvK, hEvV], kvacc)
                        pump(pss * NSUB + i)

                    def evKV(e, pss=pss):
                        for h in range(16):
                            bank = pb[4 + h // 6]
                            o = (h % 6) * 85
                            e.tensor_copy(out=kvst[0:64, h * 65:(h + 1) * 65],
                                          in_=bank[0:64, o:o + 65])
                        return None
                    hKVa = P.add("v", [hKVa], evKV)

                    def kvmove(e, pss=pss):
                        return [e.dma_start(
                                    out=kv_ext.ap().rearrange("p (H c) -> p H c", c=65)[0:64, pss * 8:pss * 8 + 8, :],
                                    in_=kvst.ap().rearrange("p (h c) -> p h c", c=65)[0:64, 0:16:2, :]),
                                e.dma_start(
                                    out=kv_ext.ap().rearrange("p (H c) -> p H c", c=65)[64:128, pss * 8:pss * 8 + 8, :],
                                    in_=kvst.ap().rearrange("p (h c) -> p h c", c=65)[0:64, 1:16:2, :])]
                    hKVa = P.dma("g", [hKVa], kvmove, n=2)
                hP1 = P.barrier()

        if PHASES >= 1:
            pump(10 ** 9)
            qs_.close()
            hWm1 = QS.get('wm1')
            hWm2 = QS.get('wm2')
        hP1b = P.barrier()

        hP2 = hP1b

        def emit_l0_tile(tag, subs, xsrc_fn, dst_fn, sg_fn, deps_in):
            TW = sum(r for _, r in subs)
            ns = len(subs)
            sq = (subs[0][1] == 128)
            tstack = contextlib.ExitStack()
            with tstack:
                SBt = lambda n, s, dt=F32: tstack.enter_context(nc.sbuf_tensor(f"{tag}_{n}", s, dt))

                def RT(n, s, dt=F32):
                    cm = nc.sbuf_tensor(f"{tag}_{n}", s, dt, side="right")
                    return cm, cm.__enter__()

                xb = SBt("xb", [128, D])
                xn2 = SBt("xn2", [128, D])
                xnb2 = SBt("xnb2", [128, D], BF16)
                bn6t = SBt("bn6t", [128, 96])
                bn2t = SBt("bn2t", [128, 2])
                stq = SBt("stq", [128, 8])
                rc2 = SBt("rc2", [128, 2 * ns])
                fc2 = SBt("fc2", [128, 2 * ns])
                frow = SBt("frow", [1, 2 * TW], BF16)
                gbuf = SBt("gbuf", [128, 512])
                bbuf = SBt("bbuf", [128, 512])
                b1b = SBt("b1b", [1, 512], BF16)
                mx4 = SBt("mx4", [128, 16])
                hcm = [None] * ns
                h_s = [None] * ns
                for j in reversed(range(ns)):
                    hcm[j], h_s[j] = RT(f"h{j}", [subs[j][1], HD], F32)
                qaTcm, qaT = RT("qaT", [128, 16 * TW], BF16)
                qwb = [SBt(f"qwb{i}", [128, 4 * 512], BF16) for i in range(2)]

                hLN = {}
                hQs = {}
                hIr = None
                hQtr = None
                hAdone = {}
                for j, (off, rows) in enumerate(subs):
                    hDx = P.dma("d", [deps_in, hAdone.get(j - 1)],
                                (lambda e, j=j, rows=rows:
                                 [e.dma_start(out=xb[0:rows, :], in_=xsrc_fn(j))]))
                    hRc = emit_ln(xb, rows, bn6t, bn2t, stq, [hDx], war_deps=[hLN.get(j - 1)])

                    def savestats(e, j=j, rows=rows):
                        e.tensor_copy(out=rc2[0:rows, 2 * j:2 * j + 1], in_=stq[0:rows, 3:4])
                        e.tensor_copy(out=rc2[0:rows, 2 * j + 1:2 * j + 2], in_=stq[0:rows, 1:2])
                        return None
                    hSv = P.add("v", [hRc], savestats, drain_first=True)

                    def xapply(e, j=j, rows=rows):
                        return e.activation(out=xn2[0:rows, :], in_=xb[0:rows, :], func=AF.Identity,
                                            bias=stq[0:rows, 1:2], scale=stq[0:rows, 3:4])
                    hXa = P.add("s", [hRc, hDx, hQs.get(j - 1)], xapply)
                    hLN[j] = hXa

                    def stats2(e, rows=rows):
                        for c in range(4):
                            e.bn_stats(out=bn6t[0:rows, 6 * c:6 * c + 6], in_=xn2[0:rows, 512 * c:512 * (c + 1)])
                        e.drain()
                        e.bn_aggr(out=bn2t[0:rows, :], in_=bn6t[0:rows, 0:24])
                        e.drain()
                        e.tensor_tensor(out=stq[0:rows, 4:5], in0=bn2t[0:rows, 0:1], in1=bn2t[0:rows, 0:1], op=TS.mult)
                        e.drain()
                        e.tensor_tensor(out=stq[0:rows, 4:5], in0=stq[0:rows, 4:5], in1=bn2t[0:rows, 1:2], op=TS.add)
                        e.drain()
                        e.tensor_scalar(out=stq[0:rows, 4:5], in0=stq[0:rows, 4:5], scalar1=1e-5, scalar2=None,
                                        op0=TS.add, op1=TS.bypass)
                        e.drain()
                        e.tensor_reduce(out=stq[0:rows, 5:6], in_=xn2[0:rows, :], axis=AX.X,
                                        op=TS.max, apply_absolute_value=True)
                        return None
                    hS2 = P.add("v", [hXa, hSv], stats2)
                    hSq2 = P.add("s", [hS2, hXa], (lambda e, rows=rows:
                                                   e.sqrt(out=stq[0:rows, 6:7], in_=stq[0:rows, 4:5])))

                    def qscale(e, j=j, rows=rows):
                        e.reciprocal(out=stq[0:rows, 7:8], in_=stq[0:rows, 6:7])
                        e.drain()
                        e.tensor_tensor(out=stq[0:rows, 5:6], in0=stq[0:rows, 5:6], in1=stq[0:rows, 7:8], op=TS.mult)
                        e.drain()
                        e.tensor_scalar(out=stq[0:rows, 5:6], in0=stq[0:rows, 5:6], scalar1=1e-5, scalar2=None,
                                        op0=TS.max, op1=TS.bypass)
                        e.drain()
                        e.reciprocal(out=stq[0:rows, 4:5], in_=stq[0:rows, 5:6])
                        e.drain()
                        e.tensor_scalar(out=stq[0:rows, 4:5], in0=stq[0:rows, 4:5], scalar1=127.0, scalar2=None,
                                        op0=TS.mult, op1=TS.bypass)
                        e.drain()
                        e.tensor_tensor(out=stq[0:rows, 4:5], in0=stq[0:rows, 4:5], in1=stq[0:rows, 7:8], op=TS.mult)
                        e.drain()
                        e.tensor_tensor(out=fc2[0:rows, 2 * j:2 * j + 1], in0=stq[0:rows, 5:6],
                                        in1=wmrow[0:rows, 0:1], op=TS.mult)
                        e.drain()
                        e.tensor_scalar(out=fc2[0:rows, 2 * j:2 * j + 1], in0=fc2[0:rows, 2 * j:2 * j + 1],
                                        scalar1=float(1.0 / 127.0), scalar2=None, op0=TS.mult, op1=TS.bypass)
                        e.drain()
                        e.reciprocal(out=stq[0:rows, 7:8], in_=fc2[0:rows, 2 * j:2 * j + 1])
                        e.drain()
                        e.tensor_scalar(out=xn2[0:rows, :], in0=xn2[0:rows, :],
                                        scalar1=stq[0:rows, 4:5], scalar2=magic[0:rows, 0:1],
                                        op0=TS.mult, op1=TS.add)
                        e.drain()
                        e.tensor_scalar(out=xn2[0:rows, :], in0=xn2[0:rows, :],
                                        scalar1=magic[0:rows, 0:1], scalar2=-128.0,
                                        op0=TS.subtract, op1=TS.max)
                        e.drain()
                        e.tensor_scalar(out=xnb2[0:rows, :], in0=xn2[0:rows, :],
                                        scalar1=127.0, scalar2=None, op0=TS.min, op1=TS.bypass)
                        return None
                    hQ1 = P.add("v", [hSq2, hWm1, hV0], qscale)
                    hQs[j] = hQ1
                    hIr = P.add("t", [hQ1, hIr, hD0], (lambda e, rows=rows:
                                e.matmul(pb[7][0:1, 0:rows], stq[0:rows, 7:8], identf[0:rows, 0:rows],
                                         start=True, stop=True, skip_group_check=True)))
                    hIr = P.add("v", [hIr], (lambda e, off=off, rows=rows:
                                e.tensor_copy(out=frow[0:1, off:off + rows], in_=pb[7][0:1, 0:rows])))
                    hEvq = None
                    for cc in range(0, 16, 2):
                        def qt2(e, cc=cc, rows=rows):
                            last = None
                            for c in (cc, cc + 1):
                                last = e.matmul(pb[5 + c % 2][0:128, 0:rows], xnb2[0:rows, c * 128:(c + 1) * 128],
                                                ident[0:rows, 0:rows], start=True, stop=True, skip_group_check=True)
                            return last
                        hT2 = P.add("t", [hQ1 if cc == 0 else None, hQtr, hEvq], qt2)
                        hQtr = hT2

                        def qev(e, cc=cc, j=j, off=off, rows=rows):
                            for c in (cc, cc + 1):
                                e.tensor_copy(out=qaT[:, c * TW + off:c * TW + off + rows],
                                              in_=pb[5 + c % 2][0:128, 0:rows])
                            return None
                        hEvq = P.add("v", [hT2], qev)
                    hAdone[j] = hEvq
                    hQs[j] = hEvq
                hAq = P.barrier()
                if L0STOP == 'A':
                    return hAq

                hQld = None
                hEvH = {}
                hMM = {}
                hB1d = None
                lastu = {0: None, 1: None}
                for s in range(16):
                    for kh in range(4):
                        hQld = P.dma("g", [hQld, hAq if s == 0 and kh == 0 else None,
                                           lastu[kh % 2]],
                                     (lambda e, s=s, kh=kh, b=qwb[kh % 2]:
                                      [e.dma_start(out=b.ap().rearrange("p (c n) -> p c n", n=512),
                                                   in_=qw1_scr.rearrange("(c p) n -> p c n", p=128)
                                                   [:, kh * 4:(kh + 1) * 4, s * 512:(s + 1) * 512])]))

                        def mm1(e, s=s, kh=kh):
                            last = None
                            for j, (off, rows) in enumerate(subs):
                                for cc in range(4):
                                    c = kh * 4 + cc
                                    last = e.matmul(pb[j][0:rows, 0:512],
                                                    qaT[:, c * TW + off:c * TW + off + rows],
                                                    qwb[kh % 2][:, cc * 512:(cc + 1) * 512],
                                                    start=(c == 0), stop=False)
                            return last
                        hMMg = P.add("t", [hQld] + [hEvH.get((s - 1, j)) for j in range(ns)], mm1)
                        hMM[(s, kh)] = hMMg
                        lastu[kh % 2] = hMMg
                    hB1d = P.dma("g", [hB1d, hMM[(s, 2)]],
                                 (lambda e, s=s: [e.dma_start(out=b1b[:], in_=b1_row[0:1, s * 512:(s + 1) * 512])]))

                    def fold1(e, s=s):
                        last = None
                        for j, (off, rows) in enumerate(subs):
                            last = e.matmul(pb[j][0:rows, 0:512], frow[0:1, off:off + rows],
                                            b1b[0:1, 0:512], start=False, stop=True)
                        return last
                    hF = P.add("t", [hMM[(s, 3)], hB1d], fold1)
                    for j, (off, rows) in enumerate(subs):
                        def evh(e, s=s, j=j, rows=rows):
                            return e.activation(out=h_s[j][0:rows, s * 512:(s + 1) * 512],
                                                in_=pb[j][0:rows, 0:512], func=AF.Silu,
                                                scale=fc2[0:rows, 2 * j:2 * j + 1])
                        hEvH[(s, j)] = P.add("s", [hF, hEvH.get((s - 1, j))], evh)
                hBq = P.barrier()
                if L0STOP == 'B':
                    qaTcm.__exit__(None, None, None)
                    return hBq
                qaTcm.__exit__(None, None, None)
                sgst = SBt("sgst", [128, 512])
                hSgD_last = None

                pairs = [list(range(ns))[i:i + 2] for i in range(0, ns, 2)]
                hgbd = None
                hQ2 = {}
                hapv = None
                hQld2 = None
                hMM2 = {}
                hEvG = {}
                lastu2 = {0: lastu[0], 1: lastu[1]}
                for pi, pair in enumerate(pairs):
                    xqcm = {}
                    xq2 = {}
                    xqpre = {}
                    xqprecm = {}
                    for j in pair:
                        off, rows = subs[j]
                        if sq:
                            xqcm[j] = nc.sbuf_tensor(f"{tag}_xq2_{j}", [128, HD], BF16)
                            xq2[j] = xqcm[j].__enter__()
                            xqpre[j] = xq2[j]
                        else:
                            xqprecm[j] = nc.sbuf_tensor(f"{tag}_xq2p_{j}", [rows, HD], BF16)
                            xqpre[j] = xqprecm[j].__enter__()
                            xqcm[j] = nc.sbuf_tensor(f"{tag}_xq2_{j}", [128, 64 * rows], BF16)
                            xq2[j] = xqcm[j].__enter__()
                        hGel = P.add("s", [hEvH[(15, j)], hQ2.get(j - 1)], (lambda e, j=j, rows=rows:
                                     e.activation(out=h_s[j][0:rows, :], in_=h_s[j][0:rows, :], func=AF.Gelu)))

                        def hstats(e, j=j, rows=rows):
                            for c in range(16):
                                e.bn_stats(out=bn6t[0:rows, 6 * c:6 * c + 6], in_=h_s[j][0:rows, 512 * c:512 * (c + 1)])
                            e.drain()
                            e.bn_aggr(out=bn2t[0:rows, :], in_=bn6t[0:rows, 0:96])
                            e.drain()
                            e.tensor_scalar(out=stq[0:rows, 0:1], in0=bn2t[0:rows, 1:2], scalar1=1e-5,
                                            scalar2=None, op0=TS.add, op1=TS.bypass)
                            return None
                        hHs = P.add("v", [hGel, hapv], hstats, drain_first=True)
                        hSq3 = P.add("s", [hHs, hGel], (lambda e, rows=rows:
                                                        e.sqrt(out=stq[0:rows, 2:3], in_=stq[0:rows, 0:1])))

                        def recip3(e, rows=rows):
                            e.reciprocal(out=stq[0:rows, 3:4], in_=stq[0:rows, 2:3])
                            e.drain()
                            e.tensor_tensor(out=stq[0:rows, 1:2], in0=bn2t[0:rows, 0:1], in1=stq[0:rows, 3:4], op=TS.mult)
                            e.drain()
                            e.tensor_scalar(out=stq[0:rows, 1:2], in0=stq[0:rows, 1:2], scalar1=-1.0, scalar2=None,
                                            op0=TS.mult, op1=TS.bypass)
                            return None
                        hR3 = P.add("v", [hSq3], recip3)
                        hap = None
                        for cq in range(16):
                            hgbd = P.dma("d", [hgbd, hap, hR3 if cq == 0 else None],
                                         (lambda e, cq=cq:
                                          [e.dma_start(out=gbuf[:], in_=g_rep[:, cq * 512:(cq + 1) * 512]),
                                           e.dma_start(out=bbuf[:], in_=b_rep[:, cq * 512:(cq + 1) * 512])]), n=2)
                            sl = slice(cq * 512, (cq + 1) * 512)

                            def lnap_s(e, j=j, rows=rows, sl=sl):
                                return e.activation(out=h_s[j][0:rows, sl], in_=h_s[j][0:rows, sl],
                                                    func=AF.Identity, bias=stq[0:rows, 1:2], scale=stq[0:rows, 3:4])
                            hAs = P.add("s", [hR3, hap], lnap_s)

                            def lnap_v(e, j=j, rows=rows, sl=sl, cq=cq):
                                e.tensor_tensor(out=h_s[j][0:rows, sl], in0=h_s[j][0:rows, sl],
                                                in1=gbuf[0:rows, :], op=TS.mult)
                                e.drain()
                                e.tensor_tensor(out=h_s[j][0:rows, sl], in0=h_s[j][0:rows, sl],
                                                in1=bbuf[0:rows, :], op=TS.add)
                                e.drain()
                                e.tensor_reduce(out=mx4[0:rows, cq:cq + 1], in_=h_s[j][0:rows, sl],
                                                axis=AX.X, op=TS.max, apply_absolute_value=True)
                                return None
                            hap = P.add("v", [hAs, hgbd], lnap_v)
                        hapv = hap
                        if DEBUG_TAPS and tag == "t0" and j == 0:
                            hap = P.dma("d", [hap], (lambda e, rows=rows:
                                        [e.dma_start(out=taps[2][0:rows, :], in_=h_s[0][0:rows, :])]))

                        def q2stats(e, j=j, rows=rows):
                            for cch in range(16):
                                e.bn_stats(out=bn6t[0:rows, 6 * cch:6 * cch + 6],
                                           in_=h_s[j][0:rows, 512 * cch:512 * (cch + 1)])
                            e.drain()
                            e.bn_aggr(out=bn2t[0:rows, :], in_=bn6t[0:rows, 0:96])
                            e.drain()
                            e.tensor_tensor(out=stq[0:rows, 4:5], in0=bn2t[0:rows, 0:1], in1=bn2t[0:rows, 0:1], op=TS.mult)
                            e.drain()
                            e.tensor_tensor(out=stq[0:rows, 4:5], in0=stq[0:rows, 4:5], in1=bn2t[0:rows, 1:2], op=TS.add)
                            e.drain()
                            e.tensor_scalar(out=stq[0:rows, 4:5], in0=stq[0:rows, 4:5], scalar1=1e-5, scalar2=None,
                                            op0=TS.add, op1=TS.bypass)
                            return None
                        hap = P.add("v", [hap], q2stats, drain_first=True)
                        hap = P.add("s", [hap], (lambda e, rows=rows:
                                                 e.sqrt(out=stq[0:rows, 6:7], in_=stq[0:rows, 4:5])))

                        def q2scale(e, j=j, rows=rows, xqpre=xqpre):
                            e.reciprocal(out=stq[0:rows, 2:3], in_=stq[0:rows, 6:7])
                            e.drain()
                            e.tensor_reduce(out=stq[0:rows, 5:6], in_=mx4[0:rows, 0:16], axis=AX.X, op=TS.max)
                            e.drain()
                            e.tensor_tensor(out=stq[0:rows, 5:6], in0=stq[0:rows, 5:6], in1=stq[0:rows, 2:3], op=TS.mult)
                            e.drain()
                            e.tensor_scalar(out=stq[0:rows, 5:6], in0=stq[0:rows, 5:6], scalar1=1e-5, scalar2=None,
                                            op0=TS.max, op1=TS.bypass)
                            e.drain()
                            e.reciprocal(out=stq[0:rows, 4:5], in_=stq[0:rows, 5:6])
                            e.drain()
                            e.tensor_scalar(out=stq[0:rows, 4:5], in0=stq[0:rows, 4:5], scalar1=127.0, scalar2=None,
                                            op0=TS.mult, op1=TS.bypass)
                            e.drain()
                            e.tensor_tensor(out=stq[0:rows, 4:5], in0=stq[0:rows, 4:5], in1=stq[0:rows, 2:3], op=TS.mult)
                            e.drain()
                            e.tensor_tensor(out=fc2[0:rows, 2 * j + 1:2 * j + 2], in0=stq[0:rows, 5:6],
                                            in1=wmrow[0:rows, 2:3], op=TS.mult)
                            e.drain()
                            e.tensor_scalar(out=fc2[0:rows, 2 * j + 1:2 * j + 2], in0=fc2[0:rows, 2 * j + 1:2 * j + 2],
                                            scalar1=float(1.0 / 127.0), scalar2=None, op0=TS.mult, op1=TS.bypass)
                            e.drain()
                            e.reciprocal(out=stq[0:rows, 7:8], in_=fc2[0:rows, 2 * j + 1:2 * j + 2])
                            e.drain()
                            e.tensor_scalar(out=h_s[j][0:rows, :], in0=h_s[j][0:rows, :],
                                            scalar1=stq[0:rows, 4:5], scalar2=magic[0:rows, 0:1],
                                            op0=TS.mult, op1=TS.add)
                            e.drain()
                            e.tensor_scalar(out=h_s[j][0:rows, :], in0=h_s[j][0:rows, :],
                                            scalar1=magic[0:rows, 0:1], scalar2=-128.0,
                                            op0=TS.subtract, op1=TS.max)
                            e.drain()
                            e.tensor_scalar(out=xqpre[j][0:rows, :], in0=h_s[j][0:rows, :],
                                            scalar1=127.0, scalar2=None, op0=TS.min, op1=TS.bypass)
                            return None
                        hQ2v = P.add("v", [hap, hWm2], q2scale)
                        if DEBUG_TAPS and tag == "t0" and j == 0:
                            hQ2v = P.dma("g", [hQ2v], (lambda e, rows=rows, xqpre=xqpre:
                                         [e.dma_start(out=taps[3][0:rows, :], in_=xqpre[0][0:rows, :])]))
                        hIr = P.add("t", [hQ2v, hIr], (lambda e, rows=rows:
                                    e.matmul(pb[7][0:1, 0:rows], stq[0:rows, 7:8], identf[0:rows, 0:rows],
                                             start=True, stop=True, skip_group_check=True)))
                        hIr = P.add("v", [hIr], (lambda e, off=off, rows=rows:
                                    e.tensor_copy(out=frow[0:1, TW + off:TW + off + rows], in_=pb[7][0:1, 0:rows])))
                        hTq2 = None
                        hEvq2 = None
                        for cc in range(0, 64, 2):
                            def xt2(e, cc=cc, j=j, rows=rows, xqpre=xqpre):
                                last = None
                                for c in (cc, cc + 1):
                                    last = e.matmul(pb[5 + c % 2][0:128, 0:rows],
                                                    xqpre[j][0:rows, c * 128:(c + 1) * 128],
                                                    ident[0:rows, 0:rows], start=True, stop=True, skip_group_check=True)
                                return last
                            hT2 = P.add("t", [hQ2v if cc == 0 else None, hTq2, hEvq2, hIr if cc == 0 else None], xt2)
                            hTq2 = hT2

                            def xev(e, cc=cc, j=j, rows=rows, xq2=xq2):
                                for c in (cc, cc + 1):
                                    if sq:
                                        e.tensor_copy(out=xq2[j][0:128, c * 128:(c + 1) * 128],
                                                      in_=pb[5 + c % 2][0:128, 0:128])
                                    else:
                                        e.tensor_copy(out=xq2[j][0:128, c * rows:(c + 1) * rows],
                                                      in_=pb[5 + c % 2][0:128, 0:rows])
                                return None
                            hEvq2 = P.add("v", [hT2], xev)
                        hQ2[j] = hEvq2
                        hcm[j].__exit__(None, None, None)
                    for s4 in range(4):
                        for kh in range(16):
                            hQld2 = P.dma("g", [hQld2, [hQ2[j] for j in pair] if s4 == 0 and kh == 0 else None,
                                                lastu2[kh % 2]],
                                          (lambda e, s4=s4, kh=kh, b=qwb[kh % 2]:
                                           [e.dma_start(out=b.ap().rearrange("p (c n) -> p c n", n=512),
                                                        in_=qw2_scr.rearrange("(c p) n -> p c n", p=128)
                                                        [:, kh * 4:(kh + 1) * 4, s4 * 512:(s4 + 1) * 512])]))

                            def mm2(e, s4=s4, kh=kh, pair=pair, xq2=xq2):
                                last = None
                                for j in pair:
                                    off, rows = subs[j]
                                    for cc in range(4):
                                        c = kh * 4 + cc
                                        last = e.matmul(pb[j % 2][0:rows, 0:512],
                                                        xq2[j][:, c * rows:(c + 1) * rows],
                                                        qwb[kh % 2][:, cc * 512:(cc + 1) * 512],
                                                        start=(c == 0), stop=False)
                                return last
                            hMMg = P.add("t", [hQld2] + [hEvG.get((pi, s4 - 1, j), (None,))[0] for j in pair]
                                         + ([hQ2[j] for j in pair] if s4 == 0 and kh == 0 else []), mm2)
                            hMM2[(pi, s4, kh)] = hMMg
                            lastu2[kh % 2] = hMMg
                        hB1d = P.dma("g", [hB1d, hMM2[(pi, s4, 14)]],
                                     (lambda e, s4=s4: [e.dma_start(out=b1b[:], in_=b2_row[0:1, s4 * 512:(s4 + 1) * 512])]))

                        def fold2(e, s4=s4, pair=pair):
                            last = None
                            for j in pair:
                                off, rows = subs[j]
                                last = e.matmul(pb[j % 2][0:rows, 0:512], frow[0:1, TW + off:TW + off + rows],
                                                b1b[0:1, 0:512], start=False, stop=True)
                            return last
                        hF2 = P.add("t", [hMM2[(pi, s4, 15)], hB1d], fold2)
                        for j in pair:
                            off, rows = subs[j]

                            def evg(e, s4=s4, j=j, rows=rows):
                                return e.activation(out=sgst[0:rows, 0:512],
                                                    in_=pb[j % 2][0:rows, 0:512], func=AF.Silu,
                                                    scale=fc2[0:rows, 2 * j + 1:2 * j + 2])
                            hEvGs = P.add("s", [hF2, hSgD_last], evg)
                            hSgD = P.dma("d", [hEvGs],
                                         (lambda e, s4=s4, j=j, rows=rows:
                                          [e.dma_start(out=sg_fn(j)[0:rows, s4 * 512:(s4 + 1) * 512],
                                                       in_=sgst[0:rows, 0:512])]))
                            hSgD_last = hSgD
                            hEvG[(pi, s4, j)] = (hSgD, hEvGs)
                    hPairDone = P.barrier()
                    for j in reversed(pair):
                        xqcm[j].__exit__(None, None, None)
                        if not sq:
                            xqprecm[j].__exit__(None, None, None)
                hDq = P.barrier()
                if L0STOP == 'D':
                    return hDq

                xnT = SBt("xnT", [128, 16 * TW], BF16)
                qT = SBt("qT", [128, 16 * TW], BF16)
                inner = SBt("inner", [128, D], BF16)
                innerT = SBt("innerT", [128, 16 * TW], BF16)
                normc = SBt("normc", [128, 32])
                recn = SBt("recn", [128, 32])
                s_s = [SBt(f"ss{j}", [r, D]) for j, (_, r) in enumerate(subs)]
                wobcm = []
                wob = []
                for i in range(2):
                    cm, tt = RT(f"wob{i}", [128, 8 * 512], BF16)
                    wobcm.append(cm)
                    wob.append(tt)

                hXn = {}
                hTx = None
                hEvX = None
                for j, (off, rows) in enumerate(subs):
                    hDx = P.dma("d", [hDq, hXn.get(j - 1)],
                                (lambda e, j=j, rows=rows:
                                 [e.dma_start(out=xb[0:rows, :], in_=xsrc_fn(j))]))

                    def xapp2(e, j=j, rows=rows):
                        e.activation(out=xn2[0:rows, :], in_=xb[0:rows, :], func=AF.Identity,
                                     bias=rc2[0:rows, 2 * j + 1:2 * j + 2], scale=rc2[0:rows, 2 * j:2 * j + 1])
                        e.drain()
                        return e.activation(out=xnb2[0:rows, :], in_=xn2[0:rows, :], func=AF.Identity)
                    hXa2 = P.add("s", [hDx, hDq, hXn.get(j - 1)], xapp2)
                    for cc in range(0, 16, 2):
                        def xt3(e, cc=cc, rows=rows):
                            last = None
                            for c in (cc, cc + 1):
                                last = e.matmul(pb[5 + c % 2][0:128, 0:rows], xnb2[0:rows, c * 128:(c + 1) * 128],
                                                ident[0:rows, 0:rows], start=True, stop=True, skip_group_check=True)
                            return last
                        hT3 = P.add("t", [hXa2 if cc == 0 else None, hTx, hEvX], xt3)
                        hTx = hT3

                        def xev3(e, cc=cc, off=off, rows=rows):
                            for c in (cc, cc + 1):
                                e.tensor_copy(out=xnT[:, c * TW + off:c * TW + off + rows],
                                              in_=pb[5 + c % 2][0:128, 0:rows])
                            return None
                        hEvX = P.add("v", [hT3], xev3)
                    hXn[j] = hEvX
                hE1 = P.barrier()
                if L0STOP == 'E1':
                    return hE1

                hWqd = None
                hQe = {}
                lastu3 = {0: None, 1: None}
                for qc4 in range(4):
                    for khq in range(2):
                        hWqd = P.dma("g", [hWqd, hE1 if qc4 == 0 and khq == 0 else None, lastu3[khq]],
                                     (lambda e, qc4=qc4, khq=khq, b=wob[khq]:
                                      [e.dma_start(out=b.ap().rearrange("p (c n) -> p c n", n=512),
                                                   in_=Wq.rearrange("(c p) n -> p c n", p=128)
                                                   [:, khq * 8:(khq + 1) * 8, qc4 * 512:(qc4 + 1) * 512])]))

                        def qproj(e, qc4=qc4, khq=khq):
                            last = None
                            for qq in range(4):
                                for cc in range(8):
                                    c = khq * 8 + cc
                                    last = e.matmul(pb[qq][0:128, 0:TW],
                                                    wob[khq][:, cc * 512 + qq * 128:cc * 512 + (qq + 1) * 128],
                                                    xnT[:, c * TW:(c + 1) * TW],
                                                    start=(c == 0), stop=(c == 15))
                            return last
                        hQp = P.add("t", [hWqd, hE1, hQe.get(qc4 - 1)], qproj)
                        lastu3[khq] = hQp

                    def qev4(e, qc4=qc4):
                        for qq in range(4):
                            qc = qc4 * 4 + qq
                            e.activation(out=qT[:, qc * TW:(qc + 1) * TW], in_=pb[qq][0:128, 0:TW], func=AF.Relu)
                        return None
                    hQe[qc4] = P.add("s", [hQp, hQe.get(qc4 - 1)], qev4)
                hE2 = P.barrier()
                if L0STOP == 'E2':
                    return hE2

                hIt = None
                hIev = None
                hQkv = {}
                for j, (off, rows) in enumerate(subs):
                    def qkv(e, j=j, off=off, rows=rows):
                        last = None
                        order = [H for H in range(32) if H % 2 == 0] + [H for H in range(32) if H % 2 == 1]
                        for i, H in enumerate(order):
                            if i == 16:
                                e.drain()
                            last = e.matmul(pb[H // 4][0:rows, (H % 4) * 128:(H % 4) * 128 + 65],
                                            qT[(H % 2) * 64:(H % 2) * 64 + 64,
                                               (H // 2) * TW + off:(H // 2) * TW + off + rows],
                                            kv_ext[(H % 2) * 64:(H % 2) * 64 + 64,
                                                   (H // 2) * 65:(H // 2) * 65 + 65],
                                            start=True, stop=True, skip_group_check=True)
                        return last
                    hK = P.add("t", [hE2 if j == 0 else None, hQkv.get(j - 1), hIt], qkv)

                    def ngather(e, j=j, rows=rows):
                        for b in range(8):
                            e.tensor_copy(out=normc[0:rows, b * 4:(b + 1) * 4],
                                          in_=bass.AP(pb[b], 64, [[512, rows], [128, 4]]))
                        e.drain()
                        e.tensor_scalar(out=normc[0:rows, :], in0=normc[0:rows, :], scalar1=1e-6,
                                        scalar2=None, op0=TS.max, op1=TS.bypass)
                        e.drain()
                        e.reciprocal(out=recn[0:rows, :], in_=normc[0:rows, :])
                        e.drain()
                        for H in range(32):
                            e.tensor_scalar(out=inner[0:rows, H * 64:(H + 1) * 64],
                                            in0=pb[H // 4][0:rows, (H % 4) * 128:(H % 4) * 128 + 64],
                                            scalar1=recn[0:rows, H:H + 1], scalar2=None,
                                            op0=TS.mult, op1=TS.bypass)
                        return None
                    hN = P.add("v", [hK, hIev], ngather, drain_first=True)
                    for cc in range(0, 16, 2):
                        def it2(e, cc=cc, rows=rows):
                            last = None
                            for c in (cc, cc + 1):
                                last = e.matmul(pb[5 + c % 2][0:128, 0:rows], inner[0:rows, c * 128:(c + 1) * 128],
                                                ident[0:rows, 0:rows], start=True, stop=True, skip_group_check=True)
                            return last
                        hT4 = P.add("t", [hN if cc == 0 else None, hIt, hIev], it2)
                        hIt = hT4

                        def iev(e, cc=cc, off=off, rows=rows):
                            for c in (cc, cc + 1):
                                e.tensor_copy(out=innerT[:, c * TW + off:c * TW + off + rows],
                                              in_=pb[5 + c % 2][0:128, 0:rows])
                            return None
                        hIev = P.add("v", [hT4], iev)
                    hQkv[j] = hIev
                hE3 = P.barrier()
                if L0STOP == 'E3':
                    return hE3

                hWod = None
                hWoe = {}
                lastu4 = {0: lastu3[0], 1: lastu3[1]}
                for s4 in range(4):
                    for kho in range(2):
                        hWod = P.dma("g", [hWod, hE3 if s4 == 0 and kho == 0 else None, lastu4[kho]],
                                     (lambda e, s4=s4, kho=kho, b=wob[kho]:
                                      [e.dma_start(out=b.ap().rearrange("p (c n) -> p c n", n=512),
                                                   in_=Wo.rearrange("(c p) n -> p c n", p=128)
                                                   [:, kho * 8:(kho + 1) * 8, s4 * 512:(s4 + 1) * 512])]))

                        def womm(e, s4=s4, kho=kho):
                            last = None
                            for j, (off, rows) in enumerate(subs):
                                for cc in range(8):
                                    c = kho * 8 + cc
                                    last = e.matmul(pb[j][0:rows, 0:512],
                                                    innerT[:, c * TW + off:c * TW + off + rows],
                                                    wob[kho][:, cc * 512:(cc + 1) * 512],
                                                    start=(c == 0), stop=False)
                            return last
                        hWp2 = P.add("t", [hWod, hE3 if s4 == 0 and kho == 0 else None,
                                           hWoe.get(s4 - 1)], womm)
                        lastu4[kho] = hWp2

                    def bofold(e, s4=s4):
                        last = None
                        for j, (off, rows) in enumerate(subs):
                            last = e.matmul(pb[j][0:rows, 0:512], ones1b[0:1, 0:rows],
                                            bo16[0:1, s4 * 512:(s4 + 1) * 512], start=False, stop=True)
                        return last
                    hWp2 = P.add("t", [hWp2, hG0], bofold)

                    def woev(e, s4=s4):
                        for j, (off, rows) in enumerate(subs):
                            e.tensor_copy(out=s_s[j][0:rows, s4 * 512:(s4 + 1) * 512],
                                          in_=pb[j][0:rows, 0:512])
                        return None
                    hWoe[s4] = P.add("v", [hWp2], woev)
                hE4 = P.barrier()
                if L0STOP == 'E':
                    wobcm[1].__exit__(None, None, None)
                    wobcm[0].__exit__(None, None, None)
                    return hE4
                wobcm[1].__exit__(None, None, None)
                wobcm[0].__exit__(None, None, None)

                hCd = None
                hCb = None
                for j, (off, rows) in enumerate(subs):
                    hDx = P.dma("d", [hE4, hCd, hCb],
                                (lambda e, j=j, rows=rows:
                                 [e.dma_start(out=xb[0:rows, :], in_=xsrc_fn(j)),
                                  e.dma_start(out=xn2[0:rows, :], in_=sg_fn(j))]), n=2)

                    def comb(e, j=j, rows=rows):
                        e.tensor_tensor(out=xn2[0:rows, :], in0=xn2[0:rows, :],
                                        in1=s_s[j][0:rows, :], op=TS.mult)
                        e.drain()
                        e.tensor_tensor(out=xn2[0:rows, :], in0=xn2[0:rows, :],
                                        in1=xb[0:rows, :], op=TS.add)
                        return None
                    hCb = P.add("v", [hDx, hE4], comb)
                    hCd = P.dma("d", [hCb], (lambda e, j=j, rows=rows:
                                             [e.dma_start(out=dst_fn(j), in_=xn2[0:rows, :])]))
            return [hCd]


        if PHASES >= 3:
            hP2 = emit_l0_tile("td", [(0, 64)],
                               lambda j: x_seq[0:64, :],
                               lambda j: x1d_scr[0:64, :],
                               lambda j: sgd_scr[0:64, :], hP1b)
            for t in range(min(NTILE, P2TILES)):
                hP2 = emit_l0_tile(f"t{t}", [(j * 128, 128) for j in range(4)],
                                   lambda j, t=t: x_own[t * 512 + j * 128:t * 512 + (j + 1) * 128, :],
                                   lambda j, t=t: x1_scr[t * 512 + j * 128:t * 512 + (j + 1) * 128, :],
                                   lambda j, t=t: sg_scr[t * 512 + j * 128:t * 512 + (j + 1) * 128, :],
                                   hP2)

        hP3 = hP2 if PHASES >= 3 else hP1b
        if PHASES >= 4:
            with contextlib.ExitStack() as p3s:
                SB3 = lambda n, s, dt=F32: p3s.enter_context(nc.sbuf_tensor(n, s, dt))
                x1d = SB3("p3_x1d", [64, D])
                lq_s = SB3("p3_lq", [128, 16 * 64], BF16)
                bn63 = SB3("p3_bn6", [64, 24])
                bn23 = SB3("p3_bn2", [64, 2])
                st3 = SB3("p3_st", [64, 4])
                xn3 = SB3("p3_xn", [64, D])
                xnb3 = SB3("p3_xnb", [64, D], BF16)
                xnT3 = SB3("p3_xnT", [128, 16 * 64], BF16)
                wb3 = [SB3(f"p3_wb{i}", [128, 16 * 512], BF16) for i in range(2)]
                k64 = SB3("p3_k64", [64, D])
                v64 = SB3("p3_v64", [64, D], BF16)
                kr = SB3("p3_kr", [64, D])
                krb = SB3("p3_krb", [64, D], BF16)
                krT = SB3("p3_krT", [128, 16 * 64], BF16)
                cosb = SB3("p3_cos", [64, 16 * 64])
                sinb = SB3("p3_sin", [64, 16 * 64])
                scf = SB3("p3_scf", [64, 64])
                nmcol = SB3("p3_nm", [64, 2])
                wsm = SB3("p3_wsm", [64, 64])
                wsb = SB3("p3_wsb", [64, 64], BF16)
                wTb = SB3("p3_wTb", [64, 64], BF16)
                ast = SB3("p3_ast", [64, 16 * 128], BF16)

                hXd = P.dma("d", hP3, lambda e: [e.dma_start(out=x1d[:], in_=x1d_scr),
                                                 e.dma_start(out=cosb[:], in_=c_cos),
                                                 e.dma_start(out=sinb[:], in_=c_sin)], n=3)
                hLq = P.dma("g", hP3, lambda e: [e.dma_start(out=lq_s[:], in_=lqT)], n=1)
                hRc3 = emit_ln(x1d, 64, bn63, bn23, st3, [hXd])

                def xap3(e):
                    e.activation(out=xn3[:], in_=x1d[:], func=AF.Identity,
                                 bias=st3[:, 1:2], scale=st3[:, 3:4])
                    e.drain()
                    return e.activation(out=xnb3[:], in_=xn3[:], func=AF.Identity)
                hXa3 = P.add("s", [hRc3, hXd], xap3)
                hT5 = None
                hE5 = None
                for c in range(16):
                    hT5 = P.add("t", [hXa3 if c == 0 else None, hT5, hE5],
                                (lambda e, c=c: e.matmul(pb[c % 2][0:128, 0:64],
                                                         xnb3[0:64, c * 128:(c + 1) * 128],
                                                         ident[0:64, 0:64],
                                                         start=True, stop=True, skip_group_check=True)))
                    hE5 = P.add("v", [hT5], (lambda e, c=c:
                                e.tensor_copy(out=xnT3[:, c * 64:(c + 1) * 64], in_=pb[c % 2][0:128, 0:64])))
                hB3 = P.barrier()

                hWd3 = None
                hPrj = {}
                for wi, (wt, kind) in enumerate([(Wk1, "k"), (Wv1, "v")]):
                    for s4 in range(4):
                        hWd3 = P.dma("g", [hWd3, hB3 if wi == 0 and s4 == 0 else None,
                                           hPrj.get((wi, s4 - 1)) or hPrj.get((wi - 1, 3))],
                                     (lambda e, wt=wt, s4=s4, b=wb3[s4 % 2]:
                                      [e.dma_start(out=b.ap().rearrange("p (c n) -> p c n", n=512),
                                                   in_=wt.rearrange("(c p) n -> p c n", p=128)
                                                   [:, :, s4 * 512:(s4 + 1) * 512])]))

                        def prj(e, s4=s4, b=wb3[s4 % 2]):
                            last = None
                            for c in range(16):
                                last = e.matmul(pb[4][0:64, 0:512], xnT3[:, c * 64:(c + 1) * 64],
                                                b[:, c * 512:(c + 1) * 512], start=(c == 0), stop=(c == 15))
                            return last
                        hPj = P.add("t", [hWd3, hB3, hPrj.get((wi, s4 - 1)) or hPrj.get((wi - 1, 3))], prj)
                        if kind == "k":
                            hPrj[(wi, s4)] = P.add("v", [hPj], (lambda e, s4=s4:
                                            e.tensor_copy(out=k64[:, s4 * 512:(s4 + 1) * 512], in_=pb[4][0:64, 0:512])))
                        else:
                            hPrj[(wi, s4)] = P.add("v", [hPj], (lambda e, s4=s4:
                                            e.tensor_copy(out=v64[:, s4 * 512:(s4 + 1) * 512], in_=pb[4][0:64, 0:512])))
                hB4 = P.barrier()

                def rope(e):
                    kv_ = k64.ap().rearrange("p (h i two) -> p h i two", two=2, i=64)
                    kr_ = kr.ap().rearrange("p (h i two) -> p h i two", two=2, i=64)
                    cos3 = cosb.ap().rearrange("p (h i) -> p h i", i=64).unsqueeze(-1)
                    sin3 = sinb.ap().rearrange("p (h i) -> p h i", i=64).unsqueeze(-1)
                    e.tensor_tensor(out=kr_[:, :, :, 0:1], in0=kv_[:, :, :, 0:1], in1=cos3, op=TS.mult)
                    e.drain()
                    e.tensor_tensor(out=kr_[:, :, :, 1:2], in0=kv_[:, :, :, 0:1], in1=sin3, op=TS.mult)
                    e.drain()
                    t_ = xn3.ap().rearrange("p (h i two) -> p h i two", two=2, i=64)
                    e.tensor_tensor(out=t_[:, :, :, 0:1], in0=kv_[:, :, :, 1:2], in1=sin3, op=TS.mult)
                    e.drain()
                    e.tensor_tensor(out=t_[:, :, :, 1:2], in0=kv_[:, :, :, 1:2], in1=cos3, op=TS.mult)
                    e.drain()
                    e.tensor_tensor(out=kr_[:, :, :, 0:1], in0=kr_[:, :, :, 0:1], in1=t_[:, :, :, 0:1], op=TS.subtract)
                    e.drain()
                    e.tensor_tensor(out=kr_[:, :, :, 1:2], in0=kr_[:, :, :, 1:2], in1=t_[:, :, :, 1:2], op=TS.add)
                    return None
                hRp3 = P.add("v", [hB4], rope, drain_first=True)
                hKb = P.add("s", [hRp3], lambda e: e.activation(out=krb[:], in_=kr[:], func=AF.Identity))
                hT6 = None
                hE6 = None
                for c in range(16):
                    hT6 = P.add("t", [hKb if c == 0 else None, hT6, hE6],
                                (lambda e, c=c: e.matmul(pb[c % 2][0:128, 0:64],
                                                         krb[0:64, c * 128:(c + 1) * 128],
                                                         ident[0:64, 0:64],
                                                         start=True, stop=True, skip_group_check=True)))
                    hE6 = P.add("v", [hT6], (lambda e, c=c:
                                e.tensor_copy(out=krT[:, c * 64:(c + 1) * 64], in_=pb[c % 2][0:128, 0:64])))
                hB5 = P.barrier()

                hSc = {}
                hAt = None
                hAe = None
                for h in range(16):
                    hSc[h] = P.add("t", [hB5 if h == 0 else None, hAt,
                                         hSc.get(h - 4), hLq if h == 0 else None],
                                   (lambda e, h=h: e.matmul(pb[h % 4][0:64, (h // 4) * 128:(h // 4) * 128 + 64],
                                                            lq_s[:, h * 64:(h + 1) * 64],
                                                            krT[:, h * 64:(h + 1) * 64],
                                                            start=True, stop=True, skip_group_check=True)))

                    def smax(e, h=h):
                        e.scalar_tensor_tensor(out=scf[:], in0=pb[h % 4][0:64, (h // 4) * 128:(h // 4) * 128 + 64],
                                               scalar=SC1, in1=mask_s[:], op0=TS.mult, op1=TS.add)
                        e.drain()
                        e.tensor_reduce(out=nmcol[:, 0:1], in_=scf[:], axis=AX.X, op=TS.max, negate=True)
                        return None
                    hSm = P.add("v", [hSc[h], hAe], smax)
                    hEx = P.add("s", [hSm], (lambda e: e.activation(out=wsm[:], in_=scf[:], func=AF.Exp,
                                                                    bias=nmcol[:, 0:1])))

                    def norm(e):
                        e.tensor_reduce(out=nmcol[:, 1:2], in_=wsm[:], axis=AX.X, op=TS.add)
                        e.drain()
                        e.reciprocal(out=nmcol[:, 0:1], in_=nmcol[:, 1:2])
                        e.drain()
                        e.tensor_scalar(out=wsb[:], in0=wsm[:], scalar1=nmcol[:, 0:1], scalar2=None,
                                        op0=TS.mult, op1=TS.bypass)
                        return None
                    hNm = P.add("v", [hEx], norm, drain_first=True)
                    hTw = P.add("t", [hNm, hSc[h]], (lambda e, h=h:
                                e.matmul(pb[h % 4][0:64, (h // 4) * 128:(h // 4) * 128 + 64],
                                         wsb[:], ident[0:64, 0:64], start=True, stop=True,
                                         skip_group_check=True)))
                    hWt = P.add("v", [hTw], (lambda e, h=h:
                                e.tensor_copy(out=wTb[:], in_=pb[h % 4][0:64, (h // 4) * 128:(h // 4) * 128 + 64])))
                    hAt = P.add("t", [hWt], (lambda e, h=h:
                                e.matmul(pb[4 + h % 2][0:64, 0:128], wTb[:],
                                         v64[0:64, h * 128:(h + 1) * 128], start=True, stop=True,
                                         skip_group_check=True)))
                    hAe = P.add("v", [hAt], (lambda e, h=h:
                                e.tensor_copy(out=ast[:, h * 128:(h + 1) * 128], in_=pb[4 + h % 2][0:64, 0:128])))
                hP3 = [P.dma("g", [hAe], lambda e: [
                    e.dma_start(out=attn_s.ap().rearrange("p (H c) -> p H c", c=128)[0:64, :, :],
                                in_=ast.ap().rearrange("p (h c) -> p h c", c=128)[0:64, 0:16:2, :]),
                    e.dma_start(out=attn_s.ap().rearrange("p (H c) -> p H c", c=128)[64:128, :, :],
                                in_=ast.ap().rearrange("p (h c) -> p h c", c=128)[0:64, 1:16:2, :])], n=2)]

        hP4 = hP3

        def emit_l1_tile(tag, t, deps_in):
            subs = [(j * 128, 128) for j in range(4)]
            TW = 512
            tstack = contextlib.ExitStack()
            with tstack:
                SBt = lambda n, s, dt=F32: tstack.enter_context(nc.sbuf_tensor(f"{tag}_{n}", s, dt))

                def RT(n, s, dt=F32):
                    cm = nc.sbuf_tensor(f"{tag}_{n}", s, dt, side="right")
                    return cm, cm.__enter__()

                xb = SBt("xb", [128, D])
                xn2 = SBt("xn2", [128, D])
                xnb2 = SBt("xnb2", [128, D], BF16)
                bn6t = SBt("bn6t", [128, 96])
                bn2t = SBt("bn2t", [128, 2])
                stq = SBt("stq", [128, 8])
                xnT = SBt("xnT", [128, 16 * TW], BF16)
                hf = SBt("hf", [128, 512])
                sg1 = [SBt(f"sg1{j}", [128, D]) for j in range(4)]
                h1cm = [None] * 4
                h1_s = [None] * 4
                for j in reversed(range(4)):
                    h1cm[j], h1_s[j] = RT(f"h1{j}", [128, HD], BF16)
                ibufcm = []
                ibuf = []
                for i in range(2):
                    cm, tt = RT(f"ibuf{i}", [128, 8 * 512], I16)
                    ibufcm.append(cm)
                    ibuf.append(tt)

                hXn = {}
                hTx = None
                hEvX = None
                for j, (off, rows) in enumerate(subs):
                    hDx = P.dma("d", [deps_in, hXn.get(j - 1)],
                                (lambda e, j=j:
                                 [e.dma_start(out=xb[:], in_=x1_scr[t * 512 + j * 128:t * 512 + (j + 1) * 128, :])]))
                    hRc = emit_ln(xb, rows, bn6t, bn2t, stq, [hDx], war_deps=[hXn.get(j - 1)])

                    def xapply(e):
                        e.activation(out=xn2[:], in_=xb[:], func=AF.Identity,
                                     bias=stq[:, 1:2], scale=stq[:, 3:4])
                        e.drain()
                        return e.activation(out=xnb2[:], in_=xn2[:], func=AF.Identity)
                    hXa = P.add("s", [hRc, hDx, hXn.get(j - 1)], xapply)
                    for cc in range(0, 16, 2):
                        def xt3(e, cc=cc):
                            last = None
                            for c in (cc, cc + 1):
                                last = e.matmul(pb[5 + c % 2][:, 0:128], xnb2[:, c * 128:(c + 1) * 128],
                                                ident[:], start=True, stop=True, skip_group_check=True)
                            return last
                        hT3 = P.add("t", [hXa if cc == 0 else None, hTx, hEvX], xt3)
                        hTx = hT3

                        def xev3(e, cc=cc, off=off):
                            for c in (cc, cc + 1):
                                e.tensor_copy(out=xnT[:, c * TW + off:c * TW + off + 128],
                                              in_=pb[5 + c % 2][:, 0:128])
                            return None
                        hEvX = P.add("v", [hT3], xev3)
                    hXn[j] = hEvX
                hL1 = P.barrier()

                hQld = None
                hEvH = {}
                hMM = {}
                lastf = {0: None, 1: None}
                for s in range(16):
                    for kh in range(2):
                        hQld = P.dma("g", [hQld, hL1 if s == 0 and kh == 0 else None, lastf[kh]],
                                     (lambda e, s=s, kh=kh, b=ibuf[kh]:
                                      [e.dma_start(out=b.ap().rearrange("p (c n) -> p c n", n=512),
                                                   in_=qfc_scr.rearrange("(c p) n -> p c n", p=128)
                                                   [:, kh * 8:(kh + 1) * 8, s * 512:(s + 1) * 512])]))

                        def mmfc(e, s=s, kh=kh):
                            last = None
                            for j in range(4):
                                for cc in range(8):
                                    c = kh * 8 + cc
                                    last = e.matmul(pb[j][:, 0:512],
                                                    xnT[:, c * TW + j * 128:c * TW + (j + 1) * 128],
                                                    ibuf[kh].ap().bitcast(BF16)[:, cc * 512:(cc + 1) * 512],
                                                    start=(c == 0), stop=(c == 15))
                            return last
                        hMMg = P.add("t", [hQld] + [hEvH.get((s - 1, j)) for j in range(4)]
                                     + ([hL1] if s == 0 and kh == 0 else []), mmfc)
                        hMM[(s, kh)] = hMMg
                        lastf[kh] = hMMg
                    for j in range(4):
                        def evh(e, s=s, j=j):
                            e.activation(out=hf[:, 0:512], in_=pb[j][:, 0:512], func=AF.Relu)
                            e.drain()
                            return e.activation(out=h1_s[j][:, s * 512:(s + 1) * 512], in_=hf[:, 0:512],
                                                func=AF.Square)
                        hEvH[(s, j)] = P.add("s", [hMM[(s, 1)], hEvH.get((s - 1, j)),
                                                   hEvH.get((s, j - 1)) or hEvH.get((s - 1, 3))], evh)
                hL2 = P.barrier()

                hRms = {}
                hTq2 = None
                for j in range(4):
                    def rstats(e, j=j):
                        for c in range(16):
                            e.bn_stats(out=bn6t[:, 6 * c:6 * c + 6], in_=h1_s[j][:, 512 * c:512 * (c + 1)])
                        e.drain()
                        e.bn_aggr(out=bn2t[:], in_=bn6t[:, 0:96])
                        e.drain()
                        e.tensor_tensor(out=stq[:, 0:1], in0=bn2t[:, 0:1], in1=bn2t[:, 0:1], op=TS.mult)
                        e.drain()
                        e.tensor_tensor(out=stq[:, 0:1], in0=stq[:, 0:1], in1=bn2t[:, 1:2], op=TS.add)
                        e.drain()
                        e.tensor_tensor(out=stq[:, 0:1], in0=stq[:, 0:1], in1=epsd[:, 0:1], op=TS.add)
                        return None
                    hRs = P.add("v", [hL2 if j == 0 else None, hRms.get(j - 1)], rstats, drain_first=True)
                    hSq = P.add("s", [hRs, hEvH.get((15, j))], lambda e: e.sqrt(out=stq[:, 2:3], in_=stq[:, 0:1]))
                    hRc4 = P.add("v", [hSq], lambda e: e.reciprocal(out=stq[:, 3:4], in_=stq[:, 2:3]))
                    hAp = P.add("v", [hRc4], (lambda e, j=j:
                                e.tensor_scalar(out=h1_s[j][:], in0=h1_s[j][:], scalar1=stq[:, 3:4],
                                                scalar2=None, op0=TS.mult, op1=TS.bypass)), drain_first=True)
                    hEvq2 = None
                    for cc in range(0, 64, 2):
                        def xt2(e, cc=cc, j=j):
                            last = None
                            for c in (cc, cc + 1):
                                last = e.matmul(pb[5 + c % 2][:, 0:128], h1_s[j][:, c * 128:(c + 1) * 128],
                                                ident[:], start=True, stop=True, skip_group_check=True)
                            return last
                        hT2 = P.add("t", [hAp if cc == 0 else None, hTq2, hEvq2], xt2)
                        hTq2 = hT2

                        def xev(e, cc=cc, j=j):
                            for c in (cc, cc + 1):
                                e.tensor_copy(out=h1_s[j][:, c * 128:(c + 1) * 128], in_=pb[5 + c % 2][:, 0:128])
                            return None
                        hEvq2 = P.add("v", [hT2], xev)
                    hRms[j] = hEvq2
                hL3 = P.barrier()

                hQld2 = None
                hMM2 = {}
                hEvG = {}
                lastp = {0: lastf[0], 1: lastf[1]}
                for s4 in range(4):
                    for kh in range(8):
                        hQld2 = P.dma("g", [hQld2, hL3 if s4 == 0 and kh == 0 else None, lastp[kh % 2]],
                                      (lambda e, s4=s4, kh=kh, b=ibuf[kh % 2]:
                                       [e.dma_start(out=b.ap().rearrange("p (c n) -> p c n", n=512),
                                                    in_=qproj_scr.rearrange("(c p) n -> p c n", p=128)
                                                    [:, kh * 8:(kh + 1) * 8, s4 * 512:(s4 + 1) * 512])]))

                        def mmpj(e, s4=s4, kh=kh):
                            last = None
                            for j in range(4):
                                for cc in range(8):
                                    c = kh * 8 + cc
                                    last = e.matmul(pb[j][:, 0:512],
                                                    h1_s[j][:, c * 128:(c + 1) * 128],
                                                    ibuf[kh % 2].ap().bitcast(BF16)[:, cc * 512:(cc + 1) * 512],
                                                    start=(c == 0), stop=(c == 63))
                            return last
                        hMMg = P.add("t", [hQld2] + [hEvG.get((s4 - 1, j)) for j in range(4)]
                                     + ([hL3] if s4 == 0 and kh == 0 else []), mmpj)
                        hMM2[(s4, kh)] = hMMg
                        lastp[kh % 2] = hMMg
                    for j in range(4):
                        def evg(e, s4=s4, j=j):
                            return e.activation(out=sg1[j][:, s4 * 512:(s4 + 1) * 512],
                                                in_=pb[j][:, 0:512], func=AF.Silu)
                        hEvG[(s4, j)] = P.add("s", [hMM2[(s4, 7)], hEvG.get((s4 - 1, j))], evg)
                hL4 = P.barrier()
                ibufcm[1].__exit__(None, None, None)
                ibufcm[0].__exit__(None, None, None)
                for j in range(4):
                    h1cm[j].__exit__(None, None, None)

                gt = [SBt(f"gt{j}", [128, 1024]) for j in range(4)]
                gT = SBt("gT", [128, 8 * TW], BF16)
                gmx = SBt("gmx", [128, 16])
                wgbcm = []
                wgb = []
                for i in range(2):
                    cm, tt = RT(f"wgb{i}", [128, 8 * 512], BF16)
                    wgbcm.append(cm)
                    wgb.append(tt)
                hWgd = None
                hGev = {}
                lastg = {0: None, 1: None}
                for sg4 in range(2):
                    for khg in range(2):
                        hWgd = P.dma("g", [hWgd, hL4 if sg4 == 0 and khg == 0 else None, lastg[khg]],
                                     (lambda e, sg4=sg4, khg=khg, b=wgb[khg]:
                                      [e.dma_start(out=b.ap().rearrange("p (c n) -> p c n", n=512),
                                                   in_=Wg1.rearrange("(c p) n -> p c n", p=128)
                                                   [:, khg * 8:(khg + 1) * 8, sg4 * 512:(sg4 + 1) * 512])]))

                        def gmm(e, sg4=sg4, khg=khg):
                            last = None
                            for j in range(4):
                                for cc in range(8):
                                    c = khg * 8 + cc
                                    last = e.matmul(pb[j][:, 0:512],
                                                    xnT[:, c * TW + j * 128:c * TW + (j + 1) * 128],
                                                    wgb[khg][:, cc * 512:(cc + 1) * 512],
                                                    start=(c == 0), stop=(c == 15))
                            return last
                        hGm = P.add("t", [hWgd, hL4 if sg4 == 0 and khg == 0 else None, hGev.get(sg4 - 1)], gmm)
                        lastg[khg] = hGm

                    def gev(e, sg4=sg4):
                        for j in range(4):
                            e.tensor_scalar(out=gt[j][:, sg4 * 512:(sg4 + 1) * 512], in0=pb[j][:, 0:512],
                                            scalar1=SC1, scalar2=None, op0=TS.mult, op1=TS.bypass)
                        return None
                    hGev[sg4] = P.add("v", [hGm], gev)
                hL5 = P.barrier()
                wgbcm[1].__exit__(None, None, None)
                wgbcm[0].__exit__(None, None, None)
                hSm2 = {}
                hTg = None
                hEvTg = None
                for j in range(4):
                    def gsmax(e, j=j):
                        e.tensor_reduce(out=gmx[:, 0:16], in_=gt[j].ap().rearrange("p (g k) -> p g k", k=64),
                                        axis=AX.X, op=TS.max, negate=True)
                        e.drain()
                        e.tensor_tensor(out=gt[j].ap().rearrange("p (g k) -> p g k", k=64),
                                        in0=gt[j].ap().rearrange("p (g k) -> p g k", k=64),
                                        in1=gmx.ap()[:, 0:16].unsqueeze(-1).broadcast_to((128, 16, 64)),
                                        op=TS.add)
                        return None
                    hG1 = P.add("v", [hL5 if j == 0 else None, hSm2.get(j - 1)], gsmax, drain_first=True)
                    hG2 = P.add("s", [hG1], (lambda e, j=j:
                                e.activation(out=gt[j][:], in_=gt[j][:], func=AF.Exp)))

                    def gnorm(e, j=j):
                        e.tensor_reduce(out=gmx[:, 0:16], in_=gt[j].ap().rearrange("p (g k) -> p g k", k=64),
                                        axis=AX.X, op=TS.add)
                        e.drain()
                        e.reciprocal(out=gmx[:, 0:16], in_=gmx[:, 0:16])
                        e.drain()
                        e.tensor_tensor(out=gt[j].ap().rearrange("p (g k) -> p g k", k=64),
                                        in0=gt[j].ap().rearrange("p (g k) -> p g k", k=64),
                                        in1=gmx.ap()[:, 0:16].unsqueeze(-1).broadcast_to((128, 16, 64)),
                                        op=TS.mult)
                        return None
                    hG3 = P.add("v", [hG2], gnorm, drain_first=True)
                    hG4 = P.add("s", [hG3], (lambda e, j=j:
                                e.activation(out=xnb2[:, 0:1024], in_=gt[j][:], func=AF.Identity)))
                    for cc in range(0, 8, 2):
                        def gtr(e, cc=cc):
                            last = None
                            for c in (cc, cc + 1):
                                last = e.matmul(pb[5 + c % 2][:, 0:128], xnb2[:, c * 128:(c + 1) * 128],
                                                ident[:], start=True, stop=True, skip_group_check=True)
                            return last
                        hT7 = P.add("t", [hG4 if cc == 0 else None, hTg, hEvTg], gtr)
                        hTg = hT7

                        def gtev(e, cc=cc, j=j):
                            for c in (cc, cc + 1):
                                e.tensor_copy(out=gT[:, c * TW + j * 128:c * TW + (j + 1) * 128],
                                              in_=pb[5 + c % 2][:, 0:128])
                            return None
                        hEvTg = P.add("v", [hT7], gtev)
                    hSm2[j] = hEvTg
                hL6 = P.barrier()

                wtd = SBt("wtd", [128, 16 * TW], BF16)
                hWt2 = {}
                hWe = {}
                worder = [h for h in range(16) if h % 2 == 0] + [h for h in range(16) if h % 2 == 1]
                last_bank_user = {}
                for i, h in enumerate(worder):
                    bank_dep = hWe.get(last_bank_user.get(h % 8))

                    def wmm(e, h=h, i=i):
                        if i == 8:
                            e.drain()
                        return e.matmul(
                            pb[h % 8][:, 0:TW],
                            attn_s[(h % 2) * 64:(h % 2) * 64 + 64, (h // 2) * 128:(h // 2) * 128 + 128],
                            gT[(h % 2) * 64:(h % 2) * 64 + 64, (h // 2) * TW:(h // 2) * TW + TW],
                            start=True, stop=True, skip_group_check=True)
                    hWt2[h] = P.add("t", [hL6 if i == 0 else None, bank_dep], wmm)
                    last_bank_user[h % 8] = h
                    hWe[h] = P.add("v", [hWt2[h]], (lambda e, h=h:
                                   e.tensor_copy(out=wtd[:, h * TW:(h + 1) * TW], in_=pb[h % 8][:, 0:TW])))
                hL7 = P.barrier()

                s1b = SBt("s1b", [128, D])
                wpbcm = []
                wpb = []
                for i in range(2):
                    cm, tt = RT(f"wpb{i}", [128, 8 * 512], BF16)
                    wpbcm.append(cm)
                    wpb.append(tt)
                s1 = [SBt(f"s1{j}", [128, D]) for j in range(4)]
                hWpd = None
                hWpe = {}
                lastw = {0: None, 1: None}
                for s4 in range(4):
                    for khp in range(2):
                        hWpd = P.dma("g", [hWpd, hL7 if s4 == 0 and khp == 0 else None, lastw[khp]],
                                     (lambda e, s4=s4, khp=khp, b=wpb[khp]:
                                      [e.dma_start(out=b.ap().rearrange("p (c n) -> p c n", n=512),
                                                   in_=Wp1.rearrange("(c p) n -> p c n", p=128)
                                                   [:, khp * 8:(khp + 1) * 8, s4 * 512:(s4 + 1) * 512])]))

                        def wpmm(e, s4=s4, khp=khp):
                            last = None
                            for j in range(4):
                                for cc in range(8):
                                    c = khp * 8 + cc
                                    last = e.matmul(pb[j][:, 0:512],
                                                    wtd[:, c * TW + j * 128:c * TW + (j + 1) * 128],
                                                    wpb[khp][:, cc * 512:(cc + 1) * 512],
                                                    start=(c == 0), stop=(c == 15))
                            return last
                        hWp3 = P.add("t", [hWpd, hL7 if s4 == 0 and khp == 0 else None,
                                           hWpe.get(s4 - 1)], wpmm)
                        lastw[khp] = hWp3

                    def wpev(e, s4=s4):
                        for j in range(4):
                            e.tensor_copy(out=s1[j][:, s4 * 512:(s4 + 1) * 512], in_=pb[j][:, 0:512])
                        return None
                    hWpe[s4] = P.add("v", [hWp3], wpev)
                hL8 = P.barrier()
                wpbcm[1].__exit__(None, None, None)
                wpbcm[0].__exit__(None, None, None)

                hCd = None
                hCb = None
                for j in range(4):
                    hDx = P.dma("d", [hL8, hCd, hCb],
                                (lambda e, j=j:
                                 [e.dma_start(out=xb[:], in_=x1_scr[t * 512 + j * 128:t * 512 + (j + 1) * 128, :])]))

                    def comb(e, j=j):
                        e.tensor_tensor(out=sg1[j][:], in0=sg1[j][:], in1=s1[j][:], op=TS.mult)
                        e.drain()
                        e.tensor_tensor(out=sg1[j][:], in0=sg1[j][:], in1=xb[:], op=TS.add)
                        return None
                    hCb = P.add("v", [hDx, hL8], comb)
                    hCd = P.dma("d", [hCb], (lambda e, j=j:
                                [e.dma_start(out=out[t * 512 + j * 128:t * 512 + (j + 1) * 128, :],
                                             in_=sg1[j][:])]))
            return [hCd]


        if PHASES >= 5:
            for t in range(NTILE):
                hP4 = emit_l1_tile(f"u{t}", t, hP4)

        P.add("g", hP4 if isinstance(hP4, list) else [hP4], lambda e: e.nop(), inc=0)

        block = st.enter_context(nc.Block())
        P.play(block)
    return nc


def _f32c(a):
    return np.ascontiguousarray(a, dtype=np.float32)


def make_constants():
    half = 64
    inv_freq = 1.0 / (10000.0 ** (np.arange(0, half, dtype=np.float64) * 2.0 / 128.0))
    ang = np.arange(64, dtype=np.float64)[:, None] * inv_freq[None, :]
    cos = np.cos(ang).astype(np.float32)
    sin = np.sin(ang).astype(np.float32)
    cos16 = np.tile(cos, (1, 16))
    sin16 = np.tile(sin, (1, 16))
    maskadd = np.where(np.arange(64)[None, :] <= np.arange(64)[:, None], 0.0, NEG).astype(np.float32)
    ident = np.eye(128, dtype=np.float32)
    gsum = np.zeros((128, 128), np.float32)
    gsum[0:64, 0] = 1.0
    gsum[64:128, 1] = 1.0
    grep = np.zeros((128, 128), np.float32)
    grep[0, 0:64] = 1.0
    grep[1, 64:128] = 1.0
    ones1 = np.ones((1, 128), np.float32)
    onesc = np.ones((128, 1), np.float32)
    return dict(c_cos=cos16, c_sin=sin16, c_mask=maskadd, c_ident=ident,
                c_gsum=gsum, c_grep=grep, c_ones1=ones1, c_onesc=onesc)


def make_in_map(inputs, core):
    b, half = core // 2, core % 2
    x = inputs["L0_attn_hidden_states"]
    lq = inputs["L1_attn_latent_queries"]
    lqT = np.ascontiguousarray(np.transpose(lq[0], (1, 2, 0)).reshape(16, 128, 64)
                               .transpose(1, 0, 2).reshape(128, 16 * 64))
    m = dict(
        x_seq=_f32c(x[b]),
        x_own=_f32c(x[b, half * OWN:(half + 1) * OWN]),
        Wq=_f32c(inputs["L0_attn_Wq"]), Wk=_f32c(inputs["L0_attn_Wk"]),
        Wv=_f32c(inputs["L0_attn_Wv"]), Wo=_f32c(inputs["L0_attn_Wo"]),
        bo_row=_f32c(inputs["L0_attn_bo"][None, :]),
        w1T=_f32c(np.transpose(inputs["L0_ffn_w1"])),
        w2T=_f32c(np.transpose(inputs["L0_ffn_w2"])),
        b1_row=_f32c(inputs["L0_ffn_b1"][None, :]),
        b2_row=_f32c(inputs["L0_ffn_b2"][None, :]),
        g_rep=_f32c(np.broadcast_to(inputs["L0_ffn_ln_g"][None, :], (128, HD))),
        b_rep=_f32c(np.broadcast_to(inputs["L0_ffn_ln_b"][None, :], (128, HD))),
        lqT=_f32c(lqT),
        Wk1=_f32c(inputs["L1_attn_Wk"]), Wv1=_f32c(inputs["L1_attn_Wv"]),
        Wg1=_f32c(inputs["L1_attn_Wg"]), Wp1=_f32c(inputs["L1_attn_Wp"]),
        WfcT=_f32c(np.transpose(inputs["L1_ffn_W_fc"])),
        WprojT=_f32c(np.transpose(inputs["L1_ffn_W_proj"])),
    )
    m.update(make_constants())
    return m


def kernel(**inputs):
    nc = build_program()
    in_maps = [make_in_map(inputs, c) for c in range(8)]
    res = run_bass_kernel_spmd(nc, in_maps, core_ids=list(range(8)))
    halves = [r["out"] for r in res.results]
    seqs = [np.concatenate([halves[2 * b], halves[2 * b + 1]], axis=0) for b in range(4)]
    return np.stack(seqs, axis=0)
